# version 06
# baseline (speedup 1.0000x reference)
import math
import numpy as np
import ml_dtypes
import concourse.bass as bass
import concourse.mybir as mybir
from concourse.expressions_rust import RuntimeValue
from concourse.ap import AP
from concourse.bass_utils import run_bass_kernel_spmd
from contextlib import ExitStack

F32 = mybir.dt.float32
BF16 = mybir.dt.bfloat16
I32 = mybir.dt.int32
AF = mybir.ActivationFunctionType
ALU = mybir.AluOpType
ds = bass.ds

H = 32
D = 4096
S = 8192
NCORE = 8
TLOC = 1024
CUT = 96.0
EPS = 1e-5
QKV_SCALE = 1.0 / math.sqrt(128.0)


def _slopes():
    m = np.arange(1, H + 1, dtype=np.float64) * (8.0 / H)
    return 0.5 ** m


def _plan():
    sl = _slopes()
    w = np.floor((511.0 + CUT / sl) / 128.0).astype(np.int64) + 1
    w = np.maximum(w, 5)
    t_lo = np.minimum(32, w)
    t_hi = np.minimum(64, w)
    cols = {}
    nt = 0
    for h in range(H):
        for role in range(2):
            T = int(t_lo[h] if role == 0 else t_hi[h])
            Tp = ((T + 3) // 4) * 4
            cols[(h, role)] = (nt, Tp)
            nt += Tp
    return sl, t_lo, t_hi, cols, nt


SL, T_LO, T_HI, COLS, NT = _plan()
OSW = [max(128 * (int(T_HI[h]) - 5) + 512, 512) for h in range(H)]
OS_OFF = np.cumsum([0] + OSW).tolist()
BIAS_OK = [bool(SL[h] * 511.0 <= 80.0) for h in range(H)]


def host_tables():
    sl = SL
    p = np.arange(128.0)[:, None]
    c = np.arange(896.0)[None, :]
    t = p - c + 384.0
    dstrip = np.zeros((H * 128, 2048), dtype=ml_dtypes.bfloat16)
    for h in range(H):
        if BIAS_OK[h]:
            c5 = np.arange(512.0)[None, :]
            for i in range(4):
                koff = p + 384.0 - 128.0 * i
                v = np.where(koff <= c5, np.exp(sl[h] * (koff - 511.0)), 0.0)
                dstrip[128 * h:128 * (h + 1), 512 * i:512 * (i + 1)] = \
                    v.astype(ml_dtypes.bfloat16)
        else:
            v = np.where(t <= 0, np.exp(sl[h] * t), 0.0)
            dstrip[128 * h:128 * (h + 1), 0:896] = v.astype(ml_dtypes.bfloat16)
    ostrip = np.zeros((128, OS_OFF[-1] + 896), dtype=ml_dtypes.bfloat16)
    for h in range(H):
        if not BIAS_OK[h]:
            cc = np.arange(float(OSW[h]))[None, :]
            v = np.exp(sl[h] * (p - cc - 128.0))
            ostrip[:, OS_OFF[h]:OS_OFF[h + 1]] = v.astype(ml_dtypes.bfloat16)
    ident = np.eye(128, dtype=ml_dtypes.bfloat16)
    return dstrip, ostrip, ident


def kill_table(core):
    kt = np.zeros((128, NT), dtype=np.float32)
    p = np.arange(128.0)
    for h in range(H):
        for role in range(2):
            base, Tp = COLS[(h, role)]
            B = core if role == 0 else core + 8
            T = int(T_LO[h] if role == 0 else T_HI[h])
            live = min(4 * (B + 1), T)
            if BIAS_OK[h]:
                kt[:, base:base + Tp] = -300.0
                for i in range(4, live):
                    kt[:, base + i] = SL[h] * (p + 384.0 - 128.0 * i - 511.0)
            else:
                kt[:, base:base + live] = 1.0
    return kt


def build(debug=(), stop_after_q=False, skip_dq=False, skip_ag=False, stop_after_attn=False):
    nc = bass.Bass("TRN2", target_bir_lowering=False)

    x_in = nc.dram_tensor("x_c", [TLOC, D], F32, kind="ExternalInput").ap()
    wqkvT = nc.dram_tensor("wqkvT", [D, 3 * D], F32, kind="ExternalInput").ap()
    bqkv_in = nc.dram_tensor("bqkv", [3 * D], F32, kind="ExternalInput").ap()
    woutT = nc.dram_tensor("woutT", [D, D], F32, kind="ExternalInput").ap()
    bout_in = nc.dram_tensor("bout", [D], F32, kind="ExternalInput").ap()
    qw_in = nc.dram_tensor("qw_r", [D, 3072], I32, kind="ExternalInput").ap()
    wsc_in = nc.dram_tensor("wsc", [D], F32, kind="ExternalInput").ap()
    wzp_in = nc.dram_tensor("wzp", [D], F32, kind="ExternalInput").ap()
    fb_in = nc.dram_tensor("fb", [D], F32, kind="ExternalInput").ap()
    dstrip_in = nc.dram_tensor("dstrip", [H * 128, 2048], BF16, kind="ExternalInput").ap()
    ostrip_in = nc.dram_tensor("ostrip", [128, OS_OFF[-1] + 896], BF16, kind="ExternalInput").ap()
    ident_in = nc.dram_tensor("ident", [128, 128], BF16, kind="ExternalInput").ap()
    ktab_in = nc.dram_tensor("ktab", [128, NT], F32, kind="ExternalInput").ap()

    y_out = nc.dram_tensor("y", [TLOC, D], F32, kind="ExternalOutput").ap()
    dbg = {}
    for name, shape, dt_ in debug:
        dbg[name] = nc.dram_tensor(name, shape, dt_, kind="ExternalOutput").ap()

    qd = nc.dram_tensor("qd", [D, TLOC], BF16)
    contrib = nc.dram_tensor("contrib", [4 * D, 512], BF16)
    gathered = nc.dram_tensor("gathered", [NCORE * 4 * D, 512], BF16,
                              addr_space="Shared")
    rbuf = nc.dram_tensor("rbuf", [NCORE * 4 * D, 512], BF16)
    wdq = nc.dram_tensor("wdq", [D, D], BF16)
    xp_d = nc.dram_tensor("xp_d", [TLOC, D], F32)

    gv = gathered.ap()
    gw = gathered.ap().rearrange("(n w) c -> n w c", w=8)
    cw = contrib.ap().rearrange("(n w) c -> n w c", w=8)

    with ExitStack() as ctx:
        e = ctx.enter_context
        arena = e(nc.sbuf_tensor("arena", [128, 32 * TLOC], BF16))
        wbuf = [e(nc.sbuf_tensor(f"wbuf{i}", [128, 4096], BF16)) for i in range(2)]
        xt = e(nc.sbuf_tensor("xt", [128, D], F32))
        lnt = e(nc.sbuf_tensor("lnt", [128, D], BF16))
        evb = [e(nc.sbuf_tensor(f"evb{i}", [128, TLOC], BF16)) for i in range(2)]
        vst = e(nc.sbuf_tensor("vst", [128, TLOC], BF16))
        stats = e(nc.sbuf_tensor("stats", [128, 8], F32))
        bq_sb = e(nc.sbuf_tensor("bq_sb", [128, 96], F32))
        bqq_sb = e(nc.sbuf_tensor("bqq_sb", [128, 32], F32))
        bo_sb = e(nc.sbuf_tensor("bo_sb", [128, 32], F32))
        fb_sb = e(nc.sbuf_tensor("fb_sb", [128, 32], F32))
        wsc_sb = e(nc.sbuf_tensor("wsc_sb", [128, 32], F32))
        wzp_sb = e(nc.sbuf_tensor("wzp_sb", [128, 32], F32))
        nzs_sb = e(nc.sbuf_tensor("nzs_sb", [128, 32], F32))
        idn = e(nc.sbuf_tensor("idn", [128, 128], BF16))
        onesb = e(nc.sbuf_tensor("onesb", [128, 1], BF16))
        onesf = e(nc.sbuf_tensor("onesf", [1, 128], F32))
        eps_sb = e(nc.sbuf_tensor("eps_sb", [128, 1], F32))
        ktab = e(nc.sbuf_tensor("ktab_sb", [128, NT], F32))
        osb = [e(nc.sbuf_tensor(f"osb{i}", [128, 896], BF16)) for i in range(2)]
        dsb = [e(nc.sbuf_tensor(f"dsb{i}", [128, 2048], BF16)) for i in range(2)]
        qb = [e(nc.sbuf_tensor(f"qb{i}", [128, TLOC], BF16)) for i in range(2)]
        kbb = [e(nc.sbuf_tensor(f"kb{i}", [128, 512], BF16)) for i in range(2)]
        vbb = [e(nc.sbuf_tensor(f"vb{i}", [128, 512], BF16)) for i in range(2)]
        ptb = e(nc.sbuf_tensor("ptb", [128, 4096], BF16))
        recip = e(nc.sbuf_tensor("recip", [1, 512], F32))
        recb = e(nc.sbuf_tensor("recb", [128, 512], F32))
        qwb = e(nc.sbuf_tensor("qwb", [128, 3072], I32))
        uqb = e(nc.sbuf_tensor("uqb", [128, D], I32))
        wdq_sb = e(nc.sbuf_tensor("wdq_sb", [128, D], BF16))
        wf_sb = [e(nc.sbuf_tensor(f"wf_sb{i}", [128, 256], BF16)) for i in range(2)]
        xo_sb = [e(nc.sbuf_tensor(f"xo_sb{i}", [128, TLOC], F32)) for i in range(2)]
        xps = [e(nc.sbuf_tensor(f"xps{i}", [128, TLOC], F32)) for i in range(2)]

        PS1 = e(nc.psum_tensor("PS1", [128, 1024], F32))
        PS2 = e(nc.psum_tensor("PS2", [128, 1024], F32))
        PS3 = e(nc.psum_tensor("PS3", [128, 1024], F32))
        PSB = e(nc.psum_tensor("PSB", [128, 2048], BF16))

        s_x = e(nc.semaphore("s_x"))
        s_w = e(nc.semaphore("s_w"))
        s_kv = e(nc.semaphore("s_kv"))
        s_sp = e(nc.semaphore("s_sp"))
        s_cc = e(nc.semaphore("s_cc"))
        s_pe = e(nc.semaphore("s_pe"))
        s_ac = e(nc.semaphore("s_ac"))
        s_dv = e(nc.semaphore("s_dv"))
        s_kv2 = [e(nc.semaphore(f"s_kva{i}")) for i in range(2)]
        s_qh2 = [e(nc.semaphore(f"s_qh{i}")) for i in range(2)]
        s_os3 = [e(nc.semaphore(f"s_os{i}")) for i in range(2)]
        s_cp = [e(nc.semaphore(f"s_cp{i}")) for i in range(16)]
        block = e(nc.Block())

        C = {"x": 0, "w": 0, "kv": 0, "sp": 0, "cc": 0, "pe": 0, "ac": 0, "dv": 0}
        SEM = {"x": s_x, "w": s_w, "kv": s_kv, "sp": s_sp, "cc": s_cc,
               "pe": s_pe, "ac": s_ac, "dv": s_dv}
        LM = {}

        def inc(instr, key, n=1):
            instr.then_inc(SEM[key], n)
            C[key] += n
            return C[key]

        def ln_arena(tt, k0, nk=1):
            return arena[:, 4096 * tt + 128 * k0: 4096 * tt + 128 * (k0 + nk)]

        arena_v = arena.ap().rearrange("p (t x) -> p t x", x=4096)

        def ln_rhs(kc, half):
            return arena_v[:, 4 * half:4 * half + 4, 128 * kc:128 * (kc + 1)]

        def ao_arena(h, c0, n):
            return arena[:, 1024 * h + c0: 1024 * h + c0 + n]

        @block.gpsimd
        def _(g):
            g.dma_start(out=idn[:, :], in_=ident_in).then_inc(s_w, 16)
            g.dma_start(out=bq_sb[:, :], in_=bqkv_in.rearrange("(o p) -> p o", p=128),
                        allow_slow_non_contiguous=True).then_inc(s_w, 16)
            g.dma_start(out=bo_sb[:, :], in_=bout_in.rearrange("(o p) -> p o", p=128),
                        allow_slow_non_contiguous=True).then_inc(s_w, 16)
            g.dma_start(out=fb_sb[:, :], in_=fb_in.rearrange("(o p) -> p o", p=128),
                        allow_slow_non_contiguous=True).then_inc(s_w, 16)
            g.dma_start(out=wsc_sb[:, :], in_=wsc_in.rearrange("(o p) -> p o", p=128),
                        allow_slow_non_contiguous=True).then_inc(s_w, 16)
            g.dma_start(out=wzp_sb[:, :], in_=wzp_in.rearrange("(o p) -> p o", p=128),
                        allow_slow_non_contiguous=True).then_inc(s_w, 16)
            g.dma_start(out=ktab[:, :], in_=ktab_in).then_inc(s_w, 16)
            C["w"] += 7 * 16
        LM["w_const"] = C["w"]

        @block.vector
        def _(v):
            v.wait_ge(s_w, LM["w_const"])
            v.memset(onesb[:, :], 1.0)
            v.memset(onesf[:, :], 1.0)
            v.memset(eps_sb[:, :], EPS)
            v.tensor_scalar(bqq_sb[:, :], bq_sb[:, 0:32], QKV_SCALE, 0.0, ALU.mult, ALU.add)
            inc(v.scalar_tensor_tensor(nzs_sb[:, :], wzp_sb[:, :], -1.0, wsc_sb[:, :],
                                       ALU.mult, ALU.mult), "dv")
        LM["dv_const"] = C["dv"]

        def ln_stats_dve(v, xbuf, tag):
            ac0 = LM[f"ac0_{tag}"]
            dv0 = LM[f"dv0_{tag}"]
            inc(v.tensor_scalar(ptb[:, :], xbuf[:, :], 0.0, 0.0, ALU.add, ALU.add,
                                accum_out=stats[:, 0:1]), "dv")
            v.wait_ge(s_ac, ac0 + 1)
            v.wait_ge(s_dv, dv0 + 1)
            inc(v.tensor_scalar(stats[:, 2:3], stats[:, 0:1], 1.0 / D, 0.0,
                                ALU.mult, ALU.add), "dv")
            v.wait_ge(s_dv, dv0 + 2)
            inc(v.scalar_tensor_tensor(stats[:, 3:4], stats[:, 2:3], -1.0, stats[:, 2:3],
                                       ALU.mult, ALU.mult), "dv")
            v.wait_ge(s_dv, dv0 + 3)
            inc(v.scalar_tensor_tensor(stats[:, 4:5], stats[:, 1:2], 1.0 / D, stats[:, 3:4],
                                       ALU.mult, ALU.add), "dv")
            v.wait_ge(s_ac, ac0 + 3)
            inc(v.scalar_tensor_tensor(stats[:, 7:8], stats[:, 2:3], -1.0, stats[:, 6:7],
                                       ALU.mult, ALU.mult), "dv")
            assert C["dv"] == dv0 + 5

        def ln_acts(s, xbuf, tag, prev_lnT_lm):
            dv0 = LM[f"dv0_{tag}"]
            ac0 = LM[f"ac0_{tag}"]
            if prev_lnT_lm:
                s.wait_ge(s_pe, prev_lnT_lm)
            inc(s.activation(lnt[:, :], xbuf[:, :], AF.Square, accum_out=stats[:, 1:2]), "ac")
            s.wait_ge(s_dv, dv0 + 4)
            inc(s.activation(stats[:, 5:6], stats[:, 4:5], AF.Ln, bias=eps_sb[:, 0:1]), "ac")
            s.wait_ge(s_ac, ac0 + 2)
            inc(s.activation(stats[:, 6:7], stats[:, 5:6], AF.Exp, scale=-0.5), "ac")
            s.wait_ge(s_dv, dv0 + 5)
            inc(s.activation(lnt[:, :], xbuf[:, :], AF.Identity,
                             scale=stats[:, 6:7], bias=stats[:, 7:8]), "ac")
            assert C["ac"] == ac0 + 4
            LM[f"ac_ln_{tag}"] = C["ac"]

        def emit_ln_phase(phase, src_ap, extra_first_wait=None):
            for tt in range(8):
                tag = f"{phase}t{tt}"
                LM[f"ac0_{tag}"] = C["ac"]
                LM[f"dv0_{tag}"] = C["dv"]
                for q in range(4):
                    LM[f"dv_lnTev_{tag}_{q}"] = LM[f"dv0_{tag}"] + 6 + q

                @block.sync
                def _(sy, tt=tt, tag=tag):
                    if phase == "l2" and tt == 0:
                        sy.wait_ge(s_sp, LM["sp_xp_done"])
                    if tt >= 1:
                        sy.wait_ge(s_ac, LM[f"ac_ln_{phase}t{tt - 1}"])
                    sy.dma_start(out=xt[:, :],
                                 in_=src_ap[128 * tt:128 * (tt + 1), :]).then_inc(s_x, 16)
                    C["x"] += 16
                LM[f"x_{tag}"] = C["x"]

                @block.vector
                def _(v, tt=tt, tag=tag):
                    v.wait_ge(s_x, LM[f"x_{tag}"])
                    if extra_first_wait and tt == 0:
                        v.wait_ge(*extra_first_wait)
                    ln_stats_dve(v, xt, tag)

                @block.scalar
                def _(s, tt=tt, tag=tag):
                    s.wait_ge(s_x, LM[f"x_{tag}"])
                    prev = LM.get(f"pe_lnT_{phase}t{tt - 1}") if tt >= 1 else None
                    ln_acts(s, xt, tag, prev)

                @block.tensor
                def _(t, tt=tt, tag=tag):
                    t.wait_ge(s_ac, LM[f"ac_ln_{tag}"])
                    for q in range(4):
                        if q >= 2:
                            prev_ev = LM.get(f"dv_lnTev_{tag}_{q - 2}", 0)
                        else:
                            prev_ev = LM.get(f"dv_lnTev_{phase}t{tt - 1}_{q + 2}",
                                             LM.get("dv_xp_31", 0) if phase == "l2" else 0)
                        t.wait_ge(s_dv, prev_ev)
                        last = None
                        for k in range(8):
                            kp = 8 * q + k
                            if phase == "l1":
                                src = lnt[:, 128 * kp:128 * (kp + 1)]
                            else:
                                jj, i8 = kp // 8, kp % 8
                                src = lnt[:, 512 * i8 + jj:512 * (i8 + 1):4]
                            last = t.transpose(PSB[:, 1024 * (q % 2) + 128 * k:1024 * (q % 2) + 128 * (k + 1)],
                                               src, idn[:, :])
                        inc(last, "pe")
                        LM[f"pe_lnT_{tag}_{q}"] = C["pe"]
                    LM[f"pe_lnT_{tag}"] = C["pe"]

                @block.vector
                def _(v, tt=tt, tag=tag):
                    if phase == "l2" and tt == 0:
                        v.wait_ge(s_pe, LM["pe_op_done"])
                    for q in range(4):
                        v.wait_ge(s_pe, LM[f"pe_lnT_{tag}_{q}"])
                        inc(v.tensor_copy(ln_arena(tt, 8 * q, 8),
                                          PSB[:, 1024 * (q % 2):1024 * (q % 2 + 1)]), "dv")
                        assert C["dv"] == LM[f"dv_lnTev_{tag}_{q}"]
            LM[f"dv_{phase}T_done"] = C["dv"]

        emit_ln_phase("l1", x_in)

        def emit_wload(key, src, prev_mm_lm, cc_wait=False):
            @block.gpsimd
            def _(g):
                if cc_wait:
                    g.wait_ge(s_cc, LM["cc_ag"])
                if prev_mm_lm:
                    g.wait_ge(s_pe, prev_mm_lm)
                g.dma_start(out=wbuf[key % 2].ap().rearrange("p (kc o) -> p kc o", o=128),
                            in_=src.rearrange("(kc p) o -> p kc o", p=128)).then_inc(s_w, 16)
                C["w"] += 16
            LM[f"w_chunk_{key}"] = C["w"]

        projm = {}

        def emit_proj_mm(key, psum, prev_evac, rhs_fn):
            @block.tensor
            def _(t):
                t.wait_ge(s_w, LM[f"w_chunk_{key}"])
                t.wait_ge(s_dv, LM["dv_l1T_done"])
                if prev_evac:
                    t.wait_ge(s_ac, prev_evac)
                last = None
                for kc in range(32):
                    for half in range(2):
                        last = t.matmul(psum[:, 512 * half:512 * (half + 1)],
                                        wbuf[key % 2][:, 128 * kc:128 * (kc + 1)],
                                        rhs_fn(kc, half),
                                        start=(kc == 0), stop=(kc == 31))
                inc(last, "pe")
            projm[key % 2] = C["pe"]
            LM[f"pe_proj_{key}"] = C["pe"]

        for oc in range(32, 64):
            slot = oc % 2
            emit_wload(oc, wqkvT[:, 128 * oc:128 * (oc + 1)], projm.get(slot))
            emit_proj_mm(oc, PS1 if slot == 0 else PS2,
                         LM.get(f"ac_evac_{oc - 2}"), ln_rhs)

            @block.scalar
            def _(s, oc=oc, slot=slot):
                s.wait_ge(s_pe, LM[f"pe_proj_{oc}"])
                if oc - 2 >= 32:
                    s.wait_ge(s_sp, LM[f"sp_K_{oc - 2}"])
                inc(s.activation(evb[slot][:, :], (PS1 if slot == 0 else PS2)[:, :],
                                 AF.Identity, bias=bq_sb[:, oc:oc + 1]), "ac")
            LM[f"ac_evac_{oc}"] = C["ac"]

            @block.sync
            def _(sy, oc=oc, slot=slot):
                sy.wait_ge(s_ac, LM[f"ac_evac_{oc}"])
                op = oc - 32
                sy.dma_start(out=contrib.ap()[128 * op:128 * (op + 1), :],
                             in_=evb[slot][:, 0:512]).then_inc(s_sp, 16)
                sy.dma_start(out=contrib.ap()[2 * D + 128 * op:2 * D + 128 * (op + 1), :],
                             in_=evb[slot][:, 512:1024]).then_inc(s_sp, 16)
                C["sp"] += 32
            LM[f"sp_K_{oc}"] = C["sp"]

        for oc in range(64, 96):
            slot = oc % 2
            emit_wload(oc, wqkvT[:, 128 * oc:128 * (oc + 1)], projm.get(slot))
            emit_proj_mm(oc, PS1 if slot == 0 else PS2,
                         LM.get(f"ac_evac_{oc - 2}"), ln_rhs)

            @block.scalar
            def _(s, oc=oc, slot=slot):
                s.wait_ge(s_pe, LM[f"pe_proj_{oc}"])
                if oc - 2 >= 64:
                    s.wait_ge(s_pe, LM[f"pe_vT_{oc - 2}"])
                elif oc - 2 >= 32:
                    s.wait_ge(s_sp, LM[f"sp_K_{oc - 2}"])
                inc(s.activation(evb[slot][:, :], (PS1 if slot == 0 else PS2)[:, :],
                                 AF.Identity, bias=bq_sb[:, oc:oc + 1]), "ac")
            LM[f"ac_evac_{oc}"] = C["ac"]

            @block.tensor
            def _(t, oc=oc, slot=slot):
                t.wait_ge(s_ac, LM[f"ac_evac_{oc}"])
                for half in range(2):
                    t.wait_ge(s_dv, LM.get(f"dv_vT_{oc - 1}_{half}", 0))
                    last = None
                    for k in range(4):
                        tt = 4 * half + k
                        last = t.transpose(PSB[:, 1024 * half + 128 * k:1024 * half + 128 * (k + 1)],
                                           evb[slot][:, 128 * tt:128 * (tt + 1)], idn[:, :])
                    inc(last, "pe")
                    LM[f"pe_vT_{oc}_{half}"] = C["pe"]
                LM[f"pe_vT_{oc}"] = C["pe"]

            @block.vector
            def _(v, oc=oc):
                for half in range(2):
                    v.wait_ge(s_pe, LM[f"pe_vT_{oc}_{half}"])
                    v.wait_ge(s_sp, LM.get(f"sp_V_{oc - 1}_{half}", 0))
                    inc(v.tensor_copy(vst[:, 512 * half:512 * (half + 1)],
                                      PSB[:, 1024 * half:1024 * half + 512]), "dv")
                    LM[f"dv_vT_{oc}_{half}"] = C["dv"]

            @block.sync
            def _(sy, oc=oc):
                op = oc - 64
                for half in range(2):
                    sy.wait_ge(s_dv, LM[f"dv_vT_{oc}_{half}"])
                    for k in range(4):
                        tt = 4 * half + k
                        blk = 1 if tt < 4 else 3
                        trow = 128 * (tt % 4)
                        sy.dma_start(
                            out=cw[512 * blk + trow:512 * blk + trow + 128,
                                   op // 4, 128 * (op % 4):128 * (op % 4) + 128],
                            in_=vst[:, 512 * half + 128 * k:512 * half + 128 * (k + 1)],
                        ).then_inc(s_sp, 16)
                        C["sp"] += 16
                    LM[f"sp_V_{oc}_{half}"] = C["sp"]
        LM["sp_contrib_done"] = C["sp"]



        for oc in range(0, 32):
            slot = oc % 2
            emit_wload(200 + oc, wqkvT[:, 128 * oc:128 * (oc + 1)], projm.get(slot))
            prev_evac = LM.get(f"ac_evac_q{oc - 2}") if oc >= 2 else LM[f"ac_evac_{94 + oc}"]
            emit_proj_mm(200 + oc, PS1 if slot == 0 else PS2, prev_evac, ln_rhs)
            LM[f"pe_proj_q{oc}"] = LM[f"pe_proj_{200 + oc}"]

            @block.scalar
            def _(s, oc=oc, slot=slot):
                s.wait_ge(s_pe, LM[f"pe_proj_q{oc}"])
                if oc >= 2:
                    s.wait_ge(s_sp, LM[f"sp_Q_{oc - 2}"])
                else:
                    s.wait_ge(s_pe, LM[f"pe_vT_{94 + oc}"])
                inc(s.activation(evb[slot][:, :], (PS1 if slot == 0 else PS2)[:, :],
                                 AF.Identity, scale=QKV_SCALE, bias=bqq_sb[:, oc:oc + 1]), "ac")
            LM[f"ac_evac_q{oc}"] = C["ac"]

            @block.sync
            def _(sy, oc=oc, slot=slot):
                sy.wait_ge(s_ac, LM[f"ac_evac_q{oc}"])
                sy.dma_start(out=qd.ap()[128 * oc:128 * (oc + 1), :],
                             in_=evb[slot][:, :]).then_inc(s_sp, 16)
                C["sp"] += 16
            LM[f"sp_Q_{oc}"] = C["sp"]
        LM["sp_Q_done"] = C["sp"]
        LM["pe_q_done"] = C["pe"]

        if not skip_ag:
            @block.gpsimd
            def _(g):
                g.wait_ge(s_sp, LM["sp_contrib_done"])
                inc(g.collective_compute(
                    "AllGather", ALU.bypass,
                    replica_groups=[list(range(NCORE))],
                    ins=[contrib.ap()[0:8192, :]],
                    outs=[gathered.ap()[0:65536, :]],
                ), "cc")
                inc(g.collective_compute(
                    "AllGather", ALU.bypass,
                    replica_groups=[list(range(NCORE))],
                    ins=[contrib.ap()[8192:16384, :]],
                    outs=[gathered.ap()[65536:131072, :]],
                ), "cc")
        else:
            @block.gpsimd
            def _(g):
                g.wait_ge(s_sp, LM["sp_contrib_done"])
                inc(g.nop(), "cc")
        LM["cc_ag"] = C["cc"]

        for n in ([] if skip_dq else range(32)):
            @block.sync
            def _(sy, n=n):
                if n >= 1:
                    sy.wait_ge(s_dv, LM[f"dv_uq_{n - 1}"])
                sy.dma_start(out=qwb[:, :],
                             in_=qw_in[128 * n:128 * (n + 1), :]).then_inc(s_x, 16)
                C["x"] += 16
            LM[f"x_qw_{n}"] = C["x"]

            @block.vector
            def _(v, n=n):
                v.wait_ge(s_x, LM[f"x_qw_{n}"])
                if n >= 1:
                    v.wait_ge(s_ac, LM[f"ac_dq_{n - 1}"])
                v.tensor_single_scalar(uqb[:, 0:1024], qwb[:, 0:1024], 63, ALU.bitwise_and)
                v.tensor_single_scalar(uqb[:, 1024:2048], qwb[:, 0:1024], 6, ALU.logical_shift_right)
                v.tensor_single_scalar(uqb[:, 2048:3072], qwb[:, 1024:2048], 63, ALU.bitwise_and)
                inc(v.tensor_single_scalar(uqb[:, 3072:4096], qwb[:, 2048:3072], 63,
                                           ALU.bitwise_and), "dv")
            LM[f"dv_uq_{n}"] = C["dv"]

            @block.scalar
            def _(s, n=n):
                s.wait_ge(s_dv, LM[f"dv_uq_{n}"])
                if n >= 1:
                    s.wait_ge(s_sp, LM[f"sp_dq_{n - 1}"])
                inc(s.activation(wdq_sb[:, :], uqb[:, :], AF.Identity,
                                 scale=wsc_sb[:, n:n + 1], bias=nzs_sb[:, n:n + 1]), "ac")
            LM[f"ac_dq_{n}"] = C["ac"]

            @block.sync
            def _(sy, n=n):
                sy.wait_ge(s_ac, LM[f"ac_dq_{n}"])
                sy.dma_start(out=wdq.ap()[128 * n:128 * (n + 1), :],
                             in_=wdq_sb[:, :]).then_inc(s_sp, 16)
                C["sp"] += 16
            LM[f"sp_dq_{n}"] = C["sp"]
        LM["sp_dq_done"] = C["sp"]
        if skip_dq:
            LM[f"ac_dq_done"] = C["ac"]

        if stop_after_q:
            if dbg:
                @block.sync
                def _(sy):
                    sy.wait_ge(s_sp, LM["sp_Q_done"])
                    if "d_qd" in dbg:
                        sy.dma_start(out=dbg["d_qd"], in_=qd.ap()).then_inc(s_sp, 16)
                        C["sp"] += 16
                    if "d_gath" in dbg:
                        sy.wait_ge(s_cc, LM["cc_ag"])
                        sy.dma_start(out=dbg["d_gath"], in_=gathered.ap()).then_inc(s_sp, 16)
                        C["sp"] += 16
                    if "d_wdq" in dbg:
                        sy.wait_ge(s_sp, LM["sp_dq_done"])
                        sy.dma_start(out=dbg["d_wdq"], in_=wdq.ap()).then_inc(s_sp, 16)
                        C["sp"] += 16

            @block.sync
            def _(sy):
                sy.wait_ge(s_sp, C["sp"])
                sy.wait_ge(s_cc, LM["cc_ag"])
            return nc

        CPORD = [0, 8, 1, 9, 2, 10, 3, 11, 4, 12, 5, 13, 6, 14, 7, 15]
        CPK = {t: k for k, t in enumerate(CPORD)}

        SCH = {}
        pes, acs, dvs = C["pe"], C["ac"], C["dv"]
        pe_attn_h = {}
        dv_attn_h = {}
        ring_lm = [0, 0]
        KV_TOT = [0, 0]
        QH_TOT = [0, 0]
        OS_TOT = [0, 0]
        OS_HIST = []
        gctr = 0
        prev_pe_mm_end = LM["pe_q_done"]
        prev_ac_end = C["ac"]
        prev_dv_norm = 0
        HORD = list(range(H - 1, -1, -1))
        pe_attn_pos = {}
        dv_attn_pos = {}
        for hi, h in enumerate(HORD):
            QH_TOT[hi % 2] += 32
            qh_wait = QH_TOT[hi % 2]
            bias_ok = BIAS_OK[h]
            for role in range(2):
                base, Tp = COLS[(h, role)]
                ngrp = Tp // 4
                npair = Tp // 2
                ndve = 4 if bias_ok else Tp
                d = {"base": base, "Tp": Tp, "ngrp": ngrp, "npair": npair,
                     "bias_ok": bias_ok, "ndve": ndve,
                     "qh_wait": qh_wait, "prev_pe_mm_end": prev_pe_mm_end,
                     "prev_ac_end": prev_ac_end, "prev_dv_norm": prev_dv_norm}
                dv_pt = {i: dvs + i + 1 for i in range(ndve)}
                kv_wait = {}
                ring_w = {}
                ring_rs = {}
                os_wait = {}
                for j in range(ngrp):
                    st = j % 2
                    KV_TOT[st] += 32
                    kv_wait[j] = KV_TOT[st]
                    if j >= 1 and not bias_ok:
                        rs = gctr % 2
                        ring_rs[j] = rs
                        ring_w[j] = ring_lm[rs]
                        OS_TOT[rs] += 16
                        os_wait[j] = OS_TOT[rs]
                        ring_lm[rs] = dv_pt[min(4 * j + 3, Tp - 1)]
                        gctr += 1
                pe_S = {}
                pe_lav = {}
                for i in range(Tp):
                    pes += 1
                    pe_S[i] = pes
                    if i >= 2:
                        pes += 1
                        pe_lav[i - 2] = pes
                for iv in (Tp - 2, Tp - 1):
                    pes += 1
                    pe_lav[iv] = pes
                d.update(kv_wait=kv_wait, ring_w=ring_w, ring_rs=ring_rs,
                         os_wait=os_wait,
                         pe_S=pe_S, pe_lav=pe_lav, pe_mm_end=pes,
                         ac_exp={i: acs + i + 1 for i in range(Tp)},
                         ac_recb=acs + Tp + 1, dv_pt=dv_pt,
                         dv_recip=dvs + ndve + 1, dv_norm=dvs + ndve + 2,
                         pe_bcast=pes + 1)
                if not bias_ok:
                    d["dve_rs"] = {i: ring_rs[i // 4] for i in range(4, Tp)}
                acs += Tp + 1
                dvs += ndve + 2
                pes += 1
                prev_pe_mm_end = d["pe_mm_end"]
                prev_ac_end = d["ac_recb"]
                prev_dv_norm = d["dv_norm"]
                SCH[(h, role)] = d
            pe_attn_pos[hi] = pes
            dv_attn_pos[hi] = dvs

        def emit_copy(eng, pid, k, t):
            off = ((pid + (16 - t)) % 16) * 4194304
            eng.dma_start(out=rbuf.ap()[8192 * t:8192 * (t + 1), :],
                          in_=AP(gathered, off, [[512, 8192], [1, 512]])
                          ).then_inc(s_cp[k], 16)

        first_need = {}
        pos = 0
        for hi, h in enumerate(HORD):
            for role in range(2):
                d0 = SCH[(h, role)]
                for j in range(d0["ngrp"]):
                    t0_ = (j + 8 * role) & 15
                    first_need.setdefault(t0_, pos)
                pos += 1
        place_at = {}
        for k in range(6, 16):
            t0_ = CPORD[k]
            place_at.setdefault(max(first_need.get(t0_, 0) - 2, 0), []).append(k)

        @block.sync
        def _(sy):
            sy.wait_ge(s_cc, LM["cc_ag"])
            sy.wait_ge(s_sp, LM["sp_Q_done"])
            pid_s = sy.partition_id()
            for k in range(6):
                if k >= 2:
                    sy.wait_ge(s_cp[k - 2], 16)
                emit_copy(sy, pid_s, k, CPORD[k])
            for hi, h in enumerate(HORD):
                if hi >= 2:
                    sy.wait_ge(s_pe, pe_attn_pos[hi - 2])
                    sy.wait_ge(s_dv, dv_attn_pos[hi - 2])
                sl_ = hi % 2
                sy.dma_start(out=qb[sl_][:, :],
                             in_=qd.ap()[128 * h:128 * (h + 1), :]).then_inc(s_qh2[sl_], 16)
                sy.dma_start(out=dsb[sl_][:, :],
                             in_=dstrip_in[128 * h:128 * (h + 1), :]).then_inc(s_qh2[sl_], 16)
                for role in range(2):
                    d = SCH[(h, role)]
                    for j in range(d["ngrp"]):
                        gslot = j % 2
                        st = j % 2
                        if j >= 2:
                            sy.wait_ge(s_pe, d["pe_lav"][4 * (j - 2) + 3])
                        else:
                            sy.wait_ge(s_pe, d["prev_pe_mm_end"])
                        t_rel = (j + 8 * role) & 15
                        sy.wait_ge(s_cp[CPK[t_rel]], 16)
                        krow = 8192 * t_rel + 128 * h
                        sy.dma_start(out=kbb[gslot][:, :],
                                     in_=rbuf.ap()[krow:krow + 128, :]
                                     ).then_inc(s_kv2[st], 16)
                        vbase = 8192 * t_rel * 512 + 2097152 + 128 * h
                        sy.dma_start(out=vbb[gslot].ap().rearrange("p (m d) -> p m d", d=128),
                                     in_=AP(rbuf, vbase, [[4096, 128], [524288, 4], [1, 128]])
                                     ).then_inc(s_kv2[st], 16)
                        if j >= 1 and not d["bias_ok"]:
                            rs = d["ring_rs"][j]
                            sy.wait_ge(s_dv, d["ring_w"][j])
                            off = OS_OFF[h] + 512 * (j - 1)
                            sy.dma_start(out=osb[rs][:, :],
                                         in_=ostrip_in[:, off:off + 896]
                                         ).then_inc(s_os3[rs], 16)

        for hi, h in enumerate(HORD):
            for role in range(2):
                d = SCH[(h, role)]
                kjit = place_at.get(2 * hi + role, [])
                if kjit:
                    @block.scalar
                    def _(s, kjit=kjit):
                        pid_a = s.partition_id()
                        for k in kjit:
                            emit_copy(s, pid_a, k, CPORD[k])
                base, Tp = d["base"], d["Tp"]
                pe_S, pe_lav = d["pe_S"], d["pe_lav"]
                ac_exp, dv_pt = d["ac_exp"], d["dv_pt"]

                @block.tensor
                def _(t, h=h, hi=hi, role=role, d=d, Tp=Tp, pe_S=pe_S, pe_lav=pe_lav,
                      ac_exp=ac_exp, dv_pt=dv_pt):
                    t.wait_ge(s_qh2[hi % 2], d["qh_wait"])
                    t.wait_ge(s_dv, d["prev_dv_norm"])

                    def lav(iv):
                        if iv in dv_pt:
                            t.wait_ge(s_dv, dv_pt[iv])
                        else:
                            t.wait_ge(s_ac, ac_exp[iv])
                        jv = iv // 4
                        mv = 3 - (iv % 4)
                        col = 512 * (iv % 8)
                        t.matmul(PS2[0:1, 512:1024], onesb[:, :],
                                 ptb[:, col:col + 512],
                                 start=(iv == 0), stop=(iv == Tp - 1))
                        inc(t.matmul(PS2[:, 0:512], vbb[jv % 2][:, 128 * mv:128 * (mv + 1)],
                                     ptb[:, col:col + 512],
                                     start=(iv == 0), stop=(iv == Tp - 1)), "pe")

                    for i in range(Tp):
                        j = i // 4
                        m = 3 - (i % 4)
                        pair = i // 2
                        sp_ = PS1 if pair % 2 == 0 else PS3
                        if i % 4 == 0:
                            t.wait_ge(s_kv2[j % 2], d["kv_wait"][j])
                        if i % 2 == 0:
                            if pair >= 2:
                                t.wait_ge(s_ac, ac_exp[2 * pair - 3])
                            else:
                                t.wait_ge(s_ac, d["prev_ac_end"])
                        inc(t.matmul(sp_[:, 512 * (i % 2):512 * (i % 2 + 1)],
                                     kbb[j % 2][:, 128 * m:128 * (m + 1)],
                                     qb[hi % 2][:, 512 * role:512 * (role + 1)],
                                     start=True, stop=True), "pe")
                        if i >= 2:
                            lav(i - 2)
                    lav(Tp - 2)
                    lav(Tp - 1)
                    assert C["pe"] == d["pe_mm_end"], (C["pe"], d["pe_mm_end"])

                @block.scalar
                def _(s, d=d, base=base, pe_S=pe_S, pe_lav=pe_lav):
                    for i in range(d["Tp"]):
                        pair = i // 2
                        s.wait_ge(s_pe, pe_S[i])
                        if i >= 8:
                            s.wait_ge(s_pe, pe_lav[i - 8])
                        else:
                            s.wait_ge(s_pe, d["prev_pe_mm_end"])
                        sp_ = PS1 if pair % 2 == 0 else PS3
                        col = 512 * (i % 8)
                        pcol = sp_[:, 512 * (i % 2):512 * (i % 2 + 1)]
                        if d["bias_ok"] and i >= 4:
                            inc(s.activation(ptb[:, col:col + 512], pcol, AF.Exp,
                                             bias=ktab[:, base + i:base + i + 1]), "ac")
                        else:
                            inc(s.activation(ptb[:, col:col + 512], pcol, AF.Exp), "ac")

                @block.vector
                def _(v, hi=hi, d=d, base=base, ac_exp=ac_exp):
                    for i in range(d["ndve"]):
                        v.wait_ge(s_ac, ac_exp[i])
                        col = 512 * (i % 8)
                        if d["bias_ok"]:
                            strip = dsb[hi % 2][:, 512 * i:512 * i + 512]
                            inc(v.tensor_tensor(ptb[:, col:col + 512],
                                                ptb[:, col:col + 512],
                                                strip, ALU.mult), "dv")
                            continue
                        if i < 4:
                            strip = dsb[hi % 2][:, 128 * i:128 * i + 512]
                        else:
                            j = i // 4
                            rs = d["dve_rs"][i]
                            v.wait_ge(s_os3[rs], d["os_wait"][j])
                            loc = 128 * (i - 4) - 512 * (j - 1)
                            strip = osb[rs][:, loc:loc + 512]
                        inc(v.scalar_tensor_tensor(ptb[:, col:col + 512],
                                                   ptb[:, col:col + 512],
                                                   ktab[:, base + i:base + i + 1],
                                                   strip, ALU.mult, ALU.mult), "dv")

                @block.vector
                def _(v, d=d, Tp=Tp, pe_lav=pe_lav):
                    v.wait_ge(s_pe, pe_lav[Tp - 1])
                    inc(v.reciprocal(recip[:, :], PS2[0:1, 512:1024]), "dv")

                @block.tensor
                def _(t, d=d):
                    t.wait_ge(s_dv, d["dv_recip"])
                    inc(t.matmul(PS2[:, 512:1024], onesf[:, :], recip[:, :],
                                 start=True, stop=True), "pe")

                @block.scalar
                def _(s, h=h, role=role, d=d):
                    s.wait_ge(s_pe, d["pe_bcast"])
                    inc(s.activation(recb[:, :], PS2[:, 512:1024], AF.Copy), "ac")
                    if "d_l" in dbg:
                        s.dma_start(out=dbg["d_l"][2 * h + role:2 * h + role + 1, :],
                                    in_=recip[:, :]).then_inc(s_sp, 16)
                        C["sp"] += 16

                @block.vector
                def _(v, h=h, role=role, d=d):
                    v.wait_ge(s_ac, d["ac_recb"])
                    if h == 0 and role == 0:
                        v.wait_ge(s_pe, LM["pe_q_done"])
                    inc(v.scalar_tensor_tensor(ao_arena(h, 512 * role, 512),
                                               PS2[:, 0:512], 1.0, recb[:, :],
                                               ALU.mult, ALU.mult), "dv")

                assert C["pe"] == d["pe_bcast"], (C["pe"], d["pe_bcast"])
                assert C["ac"] == d["ac_recb"] and C["dv"] == d["dv_norm"]
            LM[f"pe_attn_p{hi}"] = C["pe"]
            LM[f"dv_attn_p{hi}"] = C["dv"]
        LM["attn_done_dv"] = C["dv"]

        if stop_after_attn:
            @block.sync
            def _(sy):
                sy.wait_ge(s_dv, LM["attn_done_dv"])
                if "d_rb" in dbg:
                    sy.dma_start(out=dbg["d_rb"], in_=rbuf.ap()).then_inc(s_sp, 16)
                    C["sp"] += 16
                if "d_ao" in dbg:
                    sy.dma_start(out=dbg["d_ao"], in_=arena[:, :]).then_inc(s_sp, 16)
                    C["sp"] += 16
                sy.wait_ge(s_sp, C["sp"])
            return nc

        LM["ac_attn_end"] = C["ac"]
        for oc in range(32):
            slot = oc % 2
            emit_wload(300 + oc, woutT[:, 128 * oc:128 * (oc + 1)], projm.get(slot),
                       cc_wait=(oc < 2))

            @block.sync
            def _(sy, oc=oc, slot=slot):
                if oc >= 2:
                    sy.wait_ge(s_sp, LM[f"sp_xp_{oc - 2}"])
                sy.dma_start(out=xo_sb[slot].ap().rearrange("p (tt o) -> p tt o", o=128),
                             in_=x_in[:, 128 * oc:128 * (oc + 1)]
                             .rearrange("(tt p) o -> p tt o", p=128)).then_inc(s_x, 16)
                C["x"] += 16
            LM[f"x_xo_{oc}"] = C["x"]

            @block.tensor
            def _(t, oc=oc, slot=slot):
                t.wait_ge(s_w, LM[f"w_chunk_{300 + oc}"])
                t.wait_ge(s_dv, LM["attn_done_dv"])
                if oc >= 2:
                    t.wait_ge(s_ac, LM[f"ac_op_{oc - 2}"])
                else:
                    t.wait_ge(s_ac, LM["ac_attn_end"])
                psum = PS1 if slot == 0 else PS2
                last = None
                for dc in range(32):
                    for half in range(2):
                        last = t.matmul(psum[:, 512 * half:512 * (half + 1)],
                                        wbuf[slot][:, 128 * dc:128 * (dc + 1)],
                                        arena[:, 1024 * dc + 512 * half:1024 * dc + 512 * (half + 1)],
                                        start=(dc == 0), stop=(dc == 31))
                inc(last, "pe")
            projm[slot] = C["pe"]
            LM[f"pe_op_{oc}"] = C["pe"]

            @block.scalar
            def _(s, oc=oc, slot=slot):
                s.wait_ge(s_pe, LM[f"pe_op_{oc}"])
                if oc >= 2:
                    s.wait_ge(s_pe, LM[f"pe_opT_{oc - 2}"])
                inc(s.activation(evb[slot][:, :], (PS1 if slot == 0 else PS2)[:, :],
                                 AF.Identity, bias=bo_sb[:, oc:oc + 1]), "ac")
            LM[f"ac_op_{oc}"] = C["ac"]

            @block.tensor
            def _(t, oc=oc, slot=slot):
                t.wait_ge(s_ac, LM[f"ac_op_{oc}"])
                for half in range(2):
                    t.wait_ge(s_dv, LM.get(f"dv_xp_{oc - 1}_{half}", LM["attn_done_dv"]))
                    last = None
                    for k in range(4):
                        tt = 4 * half + k
                        last = t.transpose(PSB[:, 1024 * half + 128 * k:1024 * half + 128 * (k + 1)],
                                           evb[slot][:, 128 * tt:128 * (tt + 1)], idn[:, :])
                    inc(last, "pe")
                    LM[f"pe_opT_{oc}_{half}"] = C["pe"]
                LM[f"pe_opT_{oc}"] = C["pe"]

            @block.vector
            def _(v, oc=oc, slot=slot):
                v.wait_ge(s_x, LM[f"x_xo_{oc}"])
                if oc >= 2:
                    v.wait_ge(s_sp, LM[f"sp_xp_{oc - 2}"])
                for half in range(2):
                    v.wait_ge(s_pe, LM[f"pe_opT_{oc}_{half}"])
                    inc(v.tensor_tensor(xps[slot][:, 512 * half:512 * (half + 1)],
                                        PSB[:, 1024 * half:1024 * half + 512],
                                        xo_sb[slot][:, 512 * half:512 * (half + 1)],
                                        ALU.add), "dv")
                    LM[f"dv_xp_{oc}_{half}"] = C["dv"]
                LM[f"dv_xp_{oc}"] = C["dv"]

            @block.sync
            def _(sy, oc=oc, slot=slot):
                sy.wait_ge(s_dv, LM[f"dv_xp_{oc}"])
                sy.dma_start(out=xp_d.ap()[:, 128 * oc:128 * (oc + 1)]
                             .rearrange("(tt p) o -> p tt o", p=128),
                             in_=xps[slot].ap().rearrange("p (tt o) -> p tt o", o=128)).then_inc(s_sp, 16)
                C["sp"] += 16
            LM[f"sp_xp_{oc}"] = C["sp"]
        LM["sp_xp_done"] = C["sp"]
        LM["pe_op_done"] = LM["pe_opT_31"]

        emit_ln_phase("l2", xp_d.ap())

        for ngi in range(16):
            gslot = ngi % 2
            pe0f = C["pe"]
            pe_ffn = {kp: pe0f + kp + 1 for kp in range(32)}

            @block.sync
            def _(sy, ngi=ngi, gslot=gslot):
                for ncc in range(2):
                    n = 2 * ngi + ncc
                    dst = xo_sb[gslot] if ncc == 0 else xps[gslot]
                    if ngi >= 2:
                        sy.wait_ge(s_sp, LM[f"sp_y_{2 * (ngi - 2) + ncc}"])
                    else:
                        sy.wait_ge(s_sp, LM["sp_xp_done"])
                    sy.dma_start(out=dst.ap().rearrange("p (tt o) -> p tt o", o=128),
                                 in_=xp_d.ap()[:, 128 * n:128 * (n + 1)]
                                 .rearrange("(tt p) o -> p tt o", p=128)).then_inc(s_x, 16)
                    C["x"] += 16
            LM[f"x_ffn_{ngi}"] = C["x"]

            @block.scalar
            def _(s, ngi=ngi):
                if ngi == 0:
                    s.wait_ge(s_sp, LM["sp_dq_done"])
                for kp in range(32):
                    wslot = kp % 2
                    if kp >= 2:
                        s.wait_ge(s_pe, pe_ffn[kp - 2])
                    elif ngi >= 1:
                        s.wait_ge(s_pe, LM[f"pe_ffn_{ngi - 1}_{30 + kp}"])
                    s.dma_start_transpose(wf_sb[wslot][:, :],
                                          wdq.ap()[256 * ngi:256 * (ngi + 1),
                                                   128 * kp:128 * (kp + 1)]).then_inc(s_kv2[wslot], 16)
                    KV_TOT[wslot] += 16
                    LM[f"kv_wf_{ngi}_{kp}"] = KV_TOT[wslot]

            @block.tensor
            def _(t, ngi=ngi):
                t.wait_ge(s_dv, LM["dv_l2T_done"])
                t.wait_ge(s_sp, LM["sp_dq_done"])
                if ngi >= 1:
                    t.wait_ge(s_ac, LM[f"ac_ffn_{ngi - 1}"])
                else:
                    t.wait_ge(s_ac, LM["ac_op_31"])
                for kp in range(32):
                    wslot = kp % 2
                    t.wait_ge(s_kv2[wslot], LM[f"kv_wf_{ngi}_{kp}"])
                    last = None
                    for ncc in range(2):
                        psum = PS1 if ncc == 0 else PS2
                        for half in range(2):
                            last = t.matmul(psum[:, 512 * half:512 * (half + 1)],
                                            wf_sb[wslot][:, 128 * ncc:128 * (ncc + 1)],
                                            arena_v[:, 4 * half:4 * half + 4, 128 * kp:128 * (kp + 1)],
                                            start=(kp == 0), stop=(kp == 31))
                    inc(last, "pe")
                    LM[f"pe_ffn_{ngi}_{kp}"] = C["pe"]
            assert C["pe"] == pe0f + 32

            for ncc in range(2):
                n = 2 * ngi + ncc
                eslot = n % 2

                @block.scalar
                def _(s, ngi=ngi, ncc=ncc, n=n, eslot=eslot):
                    s.wait_ge(s_pe, LM[f"pe_ffn_{ngi}_31"])
                    if n >= 2:
                        s.wait_ge(s_pe, LM[f"pe_yT_{n - 2}"])
                    inc(s.activation(evb[eslot][:, :], (PS1 if ncc == 0 else PS2)[:, :],
                                     AF.Identity, bias=fb_sb[:, n:n + 1]), "ac")
                LM[f"ac_ffn_n{n}"] = C["ac"]
                if ncc == 1:
                    LM[f"ac_ffn_{ngi}"] = C["ac"]

                @block.tensor
                def _(t, n=n, eslot=eslot):
                    t.wait_ge(s_ac, LM[f"ac_ffn_n{n}"])
                    for half in range(2):
                        t.wait_ge(s_dv, LM.get(f"dv_y_{n - 1}_{half}", LM["dv_l2T_done"]))
                        last = None
                        for k in range(4):
                            tt = 4 * half + k
                            last = t.transpose(PSB[:, 1024 * half + 128 * k:1024 * half + 128 * (k + 1)],
                                               evb[eslot][:, 128 * tt:128 * (tt + 1)], idn[:, :])
                        inc(last, "pe")
                        LM[f"pe_yT_{n}_{half}"] = C["pe"]
                    LM[f"pe_yT_{n}"] = C["pe"]

                @block.vector
                def _(v, ngi=ngi, ncc=ncc, n=n, gslot=gslot):
                    v.wait_ge(s_x, LM[f"x_ffn_{ngi}"])
                    buf = xo_sb[gslot] if ncc == 0 else xps[gslot]
                    for half in range(2):
                        v.wait_ge(s_pe, LM[f"pe_yT_{n}_{half}"])
                        inc(v.tensor_tensor(buf[:, 512 * half:512 * (half + 1)],
                                            PSB[:, 1024 * half:1024 * half + 512],
                                            buf[:, 512 * half:512 * (half + 1)],
                                            ALU.add), "dv")
                        LM[f"dv_y_{n}_{half}"] = C["dv"]
                    LM[f"dv_y_n{n}"] = C["dv"]

                @block.sync
                def _(sy, n=n, ncc=ncc, gslot=gslot):
                    sy.wait_ge(s_dv, LM[f"dv_y_n{n}"])
                    src = xo_sb[gslot] if ncc == 0 else xps[gslot]
                    sy.dma_start(out=y_out[:, 128 * n:128 * (n + 1)]
                                 .rearrange("(tt p) o -> p tt o", p=128),
                                 in_=src.ap().rearrange("p (tt o) -> p tt o", o=128)).then_inc(s_sp, 16)
                    C["sp"] += 16
                LM[f"sp_y_{n}"] = C["sp"]
        LM["sp_y_done"] = C["sp"]

        if dbg:
            @block.sync
            def _(sy):
                sy.wait_ge(s_sp, LM["sp_y_done"])
                if "d_arena" in dbg:
                    sy.dma_start(out=dbg["d_arena"], in_=arena[:, :]).then_inc(s_sp, 16)
                    C["sp"] += 16
                if "d_xp" in dbg:
                    sy.dma_start(out=dbg["d_xp"], in_=xp_d.ap()).then_inc(s_sp, 16)
                    C["sp"] += 16
                if "d_qd" in dbg:
                    sy.dma_start(out=dbg["d_qd"], in_=qd.ap()).then_inc(s_sp, 16)
                    C["sp"] += 16
                if "d_gath" in dbg:
                    sy.dma_start(out=dbg["d_gath"], in_=gathered.ap()).then_inc(s_sp, 16)
                    C["sp"] += 16
                if "d_wdq" in dbg:
                    sy.dma_start(out=dbg["d_wdq"], in_=wdq.ap()).then_inc(s_sp, 16)
                    C["sp"] += 16

        @block.sync
        def _(sy):
            sy.wait_ge(s_sp, C["sp"])
    return nc


def make_in_maps():
    dstrip, ostrip, ident = host_tables()
    return dstrip, ostrip, ident


def kernel(L0_attn_x, L0_attn_w_qkv, L0_attn_b_qkv, L0_attn_w_out, L0_attn_b_out,
           L0_ffn_qweight, L0_ffn_weight_scale, L0_ffn_weight_zero_point, L0_ffn_bias):
    x = np.ascontiguousarray(np.asarray(L0_attn_x, dtype=np.float32).reshape(S, D))
    wqkvT_a = np.ascontiguousarray(np.asarray(L0_attn_w_qkv, dtype=np.float32).T)
    bqkv = np.ascontiguousarray(np.asarray(L0_attn_b_qkv, dtype=np.float32))
    woutT_a = np.ascontiguousarray(np.asarray(L0_attn_w_out, dtype=np.float32).T)
    bout = np.ascontiguousarray(np.asarray(L0_attn_b_out, dtype=np.float32))
    qw = np.asarray(L0_ffn_qweight, dtype=np.int32).reshape(D, 1024, 3)
    qw_r = np.ascontiguousarray(qw.transpose(0, 2, 1).reshape(D, 3072))
    wsc = np.ascontiguousarray(np.asarray(L0_ffn_weight_scale, dtype=np.float32).reshape(D))
    wzp = np.ascontiguousarray(np.asarray(L0_ffn_weight_zero_point, dtype=np.float32).reshape(D))
    fb = np.ascontiguousarray(np.asarray(L0_ffn_bias, dtype=np.float32))
    dstrip, ostrip, ident = host_tables()

    nc = build()
    in_maps = []
    for c in range(NCORE):
        xc = np.ascontiguousarray(
            np.concatenate([x[512 * c:512 * (c + 1)], x[512 * (c + 8):512 * (c + 9)]], axis=0))
        in_maps.append({
            "x_c": xc, "wqkvT": wqkvT_a, "bqkv": bqkv, "woutT": woutT_a, "bout": bout,
            "qw_r": qw_r, "wsc": wsc, "wzp": wzp, "fb": fb,
            "dstrip": dstrip, "ostrip": ostrip, "ident": ident,
            "ktab": kill_table(c),
        })
    res = run_bass_kernel_spmd(nc, in_maps, core_ids=list(range(NCORE)))
    out = np.empty((S, D), dtype=np.float32)
    for c in range(NCORE):
        yc = res.results[c]["y"]
        out[512 * c:512 * (c + 1)] = yc[0:512]
        out[512 * (c + 8):512 * (c + 9)] = yc[512:1024]
    return out.reshape(1, S, D)


# version 07 (final)
# speedup vs baseline: 1.0068x; 1.0068x over previous
import math
import numpy as np
import ml_dtypes
import concourse.bass as bass
import concourse.mybir as mybir
from concourse.expressions_rust import RuntimeValue
from concourse.ap import AP
from concourse.bass_utils import run_bass_kernel_spmd
from contextlib import ExitStack

F32 = mybir.dt.float32
BF16 = mybir.dt.bfloat16
I32 = mybir.dt.int32
AF = mybir.ActivationFunctionType
ALU = mybir.AluOpType
ds = bass.ds

H = 32
D = 4096
S = 8192
NCORE = 8
TLOC = 1024
CUT = 96.0
EPS = 1e-5
QKV_SCALE = 1.0 / math.sqrt(128.0)


def _slopes():
    m = np.arange(1, H + 1, dtype=np.float64) * (8.0 / H)
    return 0.5 ** m


def _plan():
    sl = _slopes()
    w = np.floor((511.0 + CUT / sl) / 128.0).astype(np.int64) + 1
    w = np.maximum(w, 5)
    t_lo = np.minimum(32, w)
    t_hi = np.minimum(64, w)
    cols = {}
    nt = 0
    for h in range(H):
        for role in range(2):
            T = int(t_lo[h] if role == 0 else t_hi[h])
            Tp = ((T + 3) // 4) * 4
            cols[(h, role)] = (nt, Tp)
            nt += Tp
    return sl, t_lo, t_hi, cols, nt


SL, T_LO, T_HI, COLS, NT = _plan()
OSW = [max(128 * (int(T_HI[h]) - 5) + 512, 512) for h in range(H)]
OS_OFF = np.cumsum([0] + OSW).tolist()
BIAS_OK = [bool(SL[h] * 511.0 <= 80.0) for h in range(H)]


def host_tables():
    sl = SL
    p = np.arange(128.0)[:, None]
    c = np.arange(896.0)[None, :]
    t = p - c + 384.0
    dstrip = np.zeros((H * 128, 2048), dtype=ml_dtypes.bfloat16)
    for h in range(H):
        if BIAS_OK[h]:
            c5 = np.arange(512.0)[None, :]
            for i in range(4):
                koff = p + 384.0 - 128.0 * i
                v = np.where(koff <= c5, np.exp(sl[h] * (koff - 511.0)), 0.0)
                dstrip[128 * h:128 * (h + 1), 512 * i:512 * (i + 1)] = \
                    v.astype(ml_dtypes.bfloat16)
        else:
            v = np.where(t <= 0, np.exp(sl[h] * t), 0.0)
            dstrip[128 * h:128 * (h + 1), 0:896] = v.astype(ml_dtypes.bfloat16)
    ostrip = np.zeros((128, OS_OFF[-1] + 896), dtype=ml_dtypes.bfloat16)
    for h in range(H):
        if not BIAS_OK[h]:
            cc = np.arange(float(OSW[h]))[None, :]
            v = np.exp(sl[h] * (p - cc - 128.0))
            ostrip[:, OS_OFF[h]:OS_OFF[h + 1]] = v.astype(ml_dtypes.bfloat16)
    ident = np.eye(128, dtype=ml_dtypes.bfloat16)
    return dstrip, ostrip, ident


def kill_table(core):
    kt = np.zeros((128, NT), dtype=np.float32)
    p = np.arange(128.0)
    for h in range(H):
        for role in range(2):
            base, Tp = COLS[(h, role)]
            B = core if role == 0 else core + 8
            T = int(T_LO[h] if role == 0 else T_HI[h])
            live = min(4 * (B + 1), T)
            if BIAS_OK[h]:
                kt[:, base:base + Tp] = -300.0
                for i in range(4, live):
                    kt[:, base + i] = SL[h] * (p + 384.0 - 128.0 * i - 511.0)
            else:
                kt[:, base:base + live] = 1.0
    return kt


def build(debug=(), stop_after_q=False, skip_dq=False, skip_ag=False, stop_after_attn=False):
    nc = bass.Bass("TRN2", target_bir_lowering=False)

    x_in = nc.dram_tensor("x_c", [TLOC, D], F32, kind="ExternalInput").ap()
    wqkvT = nc.dram_tensor("wqkvT", [D, 3 * D], F32, kind="ExternalInput").ap()
    bqkv_in = nc.dram_tensor("bqkv", [3 * D], F32, kind="ExternalInput").ap()
    woutT = nc.dram_tensor("woutT", [D, D], F32, kind="ExternalInput").ap()
    bout_in = nc.dram_tensor("bout", [D], F32, kind="ExternalInput").ap()
    qw_in = nc.dram_tensor("qw_r", [D, 3072], I32, kind="ExternalInput").ap()
    wsc_in = nc.dram_tensor("wsc", [D], F32, kind="ExternalInput").ap()
    wzp_in = nc.dram_tensor("wzp", [D], F32, kind="ExternalInput").ap()
    fb_in = nc.dram_tensor("fb", [D], F32, kind="ExternalInput").ap()
    dstrip_in = nc.dram_tensor("dstrip", [H * 128, 2048], BF16, kind="ExternalInput").ap()
    ostrip_in = nc.dram_tensor("ostrip", [128, OS_OFF[-1] + 896], BF16, kind="ExternalInput").ap()
    ident_in = nc.dram_tensor("ident", [128, 128], BF16, kind="ExternalInput").ap()
    ktab_in = nc.dram_tensor("ktab", [128, NT], F32, kind="ExternalInput").ap()

    y_out = nc.dram_tensor("y", [TLOC, D], F32, kind="ExternalOutput").ap()
    dbg = {}
    for name, shape, dt_ in debug:
        dbg[name] = nc.dram_tensor(name, shape, dt_, kind="ExternalOutput").ap()

    qd = nc.dram_tensor("qd", [D, TLOC], BF16)
    contrib = nc.dram_tensor("contrib", [4 * D, 512], BF16)
    gathered = nc.dram_tensor("gathered", [NCORE * 4 * D, 512], BF16,
                              addr_space="Shared")
    rbuf = nc.dram_tensor("rbuf", [NCORE * 4 * D, 512], BF16)
    wdq = nc.dram_tensor("wdq", [D, D], BF16)
    xp_d = nc.dram_tensor("xp_d", [TLOC, D], F32)

    gv = gathered.ap()
    gw = gathered.ap().rearrange("(n w) c -> n w c", w=8)
    cw = contrib.ap().rearrange("(n w) c -> n w c", w=8)

    with ExitStack() as ctx:
        e = ctx.enter_context
        arena = e(nc.sbuf_tensor("arena", [128, 32 * TLOC], BF16))
        wbuf = [e(nc.sbuf_tensor(f"wbuf{i}", [128, 4096], BF16)) for i in range(2)]
        xt = e(nc.sbuf_tensor("xt", [128, D], F32))
        lnt = e(nc.sbuf_tensor("lnt", [128, D], BF16))
        evb = [e(nc.sbuf_tensor(f"evb{i}", [128, TLOC], BF16)) for i in range(2)]
        vst = e(nc.sbuf_tensor("vst", [128, TLOC], BF16))
        stats = e(nc.sbuf_tensor("stats", [128, 8], F32))
        bq_sb = e(nc.sbuf_tensor("bq_sb", [128, 96], F32))
        bqq_sb = e(nc.sbuf_tensor("bqq_sb", [128, 32], F32))
        bo_sb = e(nc.sbuf_tensor("bo_sb", [128, 32], F32))
        fb_sb = e(nc.sbuf_tensor("fb_sb", [128, 32], F32))
        wsc_sb = e(nc.sbuf_tensor("wsc_sb", [128, 32], F32))
        wzp_sb = e(nc.sbuf_tensor("wzp_sb", [128, 32], F32))
        nzs_sb = e(nc.sbuf_tensor("nzs_sb", [128, 32], F32))
        idn = e(nc.sbuf_tensor("idn", [128, 128], BF16))
        onesb = e(nc.sbuf_tensor("onesb", [128, 1], BF16))
        onesf = e(nc.sbuf_tensor("onesf", [1, 128], F32))
        eps_sb = e(nc.sbuf_tensor("eps_sb", [128, 1], F32))
        ktab = e(nc.sbuf_tensor("ktab_sb", [128, NT], F32))
        osb = [e(nc.sbuf_tensor(f"osb{i}", [128, 896], BF16)) for i in range(2)]
        dsb = [e(nc.sbuf_tensor(f"dsb{i}", [128, 2048], BF16)) for i in range(2)]
        qb = [e(nc.sbuf_tensor(f"qb{i}", [128, TLOC], BF16)) for i in range(2)]
        kbb = [e(nc.sbuf_tensor(f"kb{i}", [128, 512], BF16)) for i in range(2)]
        vbb = [e(nc.sbuf_tensor(f"vb{i}", [128, 512], BF16)) for i in range(2)]
        ptb = e(nc.sbuf_tensor("ptb", [128, 4096], BF16))
        recip = e(nc.sbuf_tensor("recip", [1, 512], F32))
        recb = e(nc.sbuf_tensor("recb", [128, 512], F32))
        qwb = e(nc.sbuf_tensor("qwb", [128, 3072], I32))
        uqb = e(nc.sbuf_tensor("uqb", [128, D], I32))
        wdq_sb = e(nc.sbuf_tensor("wdq_sb", [128, D], BF16))
        wf_sb = [e(nc.sbuf_tensor(f"wf_sb{i}", [128, 256], BF16)) for i in range(2)]
        xo_sb = [e(nc.sbuf_tensor(f"xo_sb{i}", [128, TLOC], F32)) for i in range(2)]
        xps = [e(nc.sbuf_tensor(f"xps{i}", [128, TLOC], F32)) for i in range(2)]

        PS1 = e(nc.psum_tensor("PS1", [128, 1024], F32))
        PS2 = e(nc.psum_tensor("PS2", [128, 1024], F32))
        PS3 = e(nc.psum_tensor("PS3", [128, 1024], F32))
        PSB = e(nc.psum_tensor("PSB", [128, 2048], BF16))

        s_x = e(nc.semaphore("s_x"))
        s_w = e(nc.semaphore("s_w"))
        s_kv = e(nc.semaphore("s_kv"))
        s_sp = e(nc.semaphore("s_sp"))
        s_cc = e(nc.semaphore("s_cc"))
        s_pe = e(nc.semaphore("s_pe"))
        s_ac = e(nc.semaphore("s_ac"))
        s_dv = e(nc.semaphore("s_dv"))
        s_kv2 = [e(nc.semaphore(f"s_kva{i}")) for i in range(2)]
        s_qh2 = [e(nc.semaphore(f"s_qh{i}")) for i in range(2)]
        s_os3 = [e(nc.semaphore(f"s_os{i}")) for i in range(2)]
        s_cp = [e(nc.semaphore(f"s_cp{i}")) for i in range(16)]
        block = e(nc.Block())

        C = {"x": 0, "w": 0, "kv": 0, "sp": 0, "cc": 0, "pe": 0, "ac": 0, "dv": 0}
        SEM = {"x": s_x, "w": s_w, "kv": s_kv, "sp": s_sp, "cc": s_cc,
               "pe": s_pe, "ac": s_ac, "dv": s_dv}
        LM = {}

        def inc(instr, key, n=1):
            instr.then_inc(SEM[key], n)
            C[key] += n
            return C[key]

        def ln_arena(tt, k0, nk=1):
            return arena[:, 4096 * tt + 128 * k0: 4096 * tt + 128 * (k0 + nk)]

        arena_v = arena.ap().rearrange("p (t x) -> p t x", x=4096)

        def ln_rhs(kc, half):
            return arena_v[:, 4 * half:4 * half + 4, 128 * kc:128 * (kc + 1)]

        def ao_arena(h, c0, n):
            return arena[:, 1024 * h + c0: 1024 * h + c0 + n]

        @block.gpsimd
        def _(g):
            g.dma_start(out=idn[:, :], in_=ident_in).then_inc(s_w, 16)
            g.dma_start(out=bq_sb[:, :], in_=bqkv_in.rearrange("(o p) -> p o", p=128),
                        allow_slow_non_contiguous=True).then_inc(s_w, 16)
            g.dma_start(out=bo_sb[:, :], in_=bout_in.rearrange("(o p) -> p o", p=128),
                        allow_slow_non_contiguous=True).then_inc(s_w, 16)
            g.dma_start(out=fb_sb[:, :], in_=fb_in.rearrange("(o p) -> p o", p=128),
                        allow_slow_non_contiguous=True).then_inc(s_w, 16)
            g.dma_start(out=wsc_sb[:, :], in_=wsc_in.rearrange("(o p) -> p o", p=128),
                        allow_slow_non_contiguous=True).then_inc(s_w, 16)
            g.dma_start(out=wzp_sb[:, :], in_=wzp_in.rearrange("(o p) -> p o", p=128),
                        allow_slow_non_contiguous=True).then_inc(s_w, 16)
            g.dma_start(out=ktab[:, :], in_=ktab_in).then_inc(s_w, 16)
            C["w"] += 7 * 16
        LM["w_const"] = C["w"]

        @block.vector
        def _(v):
            v.wait_ge(s_w, LM["w_const"])
            v.memset(onesb[:, :], 1.0)
            v.memset(onesf[:, :], 1.0)
            v.memset(eps_sb[:, :], EPS)
            v.tensor_scalar(bqq_sb[:, :], bq_sb[:, 0:32], QKV_SCALE, 0.0, ALU.mult, ALU.add)
            inc(v.scalar_tensor_tensor(nzs_sb[:, :], wzp_sb[:, :], -1.0, wsc_sb[:, :],
                                       ALU.mult, ALU.mult), "dv")
        LM["dv_const"] = C["dv"]

        def ln_stats_dve(v, xbuf, tag):
            ac0 = LM[f"ac0_{tag}"]
            dv0 = LM[f"dv0_{tag}"]
            inc(v.tensor_scalar(ptb[:, :], xbuf[:, :], 0.0, 0.0, ALU.add, ALU.add,
                                accum_out=stats[:, 0:1]), "dv")
            v.wait_ge(s_ac, ac0 + 1)
            v.wait_ge(s_dv, dv0 + 1)
            inc(v.tensor_scalar(stats[:, 2:3], stats[:, 0:1], 1.0 / D, 0.0,
                                ALU.mult, ALU.add), "dv")
            v.wait_ge(s_dv, dv0 + 2)
            inc(v.scalar_tensor_tensor(stats[:, 3:4], stats[:, 2:3], -1.0, stats[:, 2:3],
                                       ALU.mult, ALU.mult), "dv")
            v.wait_ge(s_dv, dv0 + 3)
            inc(v.scalar_tensor_tensor(stats[:, 4:5], stats[:, 1:2], 1.0 / D, stats[:, 3:4],
                                       ALU.mult, ALU.add), "dv")
            v.wait_ge(s_ac, ac0 + 3)
            inc(v.scalar_tensor_tensor(stats[:, 7:8], stats[:, 2:3], -1.0, stats[:, 6:7],
                                       ALU.mult, ALU.mult), "dv")
            assert C["dv"] == dv0 + 5

        def ln_acts(s, xbuf, tag, prev_lnT_lm):
            dv0 = LM[f"dv0_{tag}"]
            ac0 = LM[f"ac0_{tag}"]
            if prev_lnT_lm:
                s.wait_ge(s_pe, prev_lnT_lm)
            inc(s.activation(lnt[:, :], xbuf[:, :], AF.Square, accum_out=stats[:, 1:2]), "ac")
            s.wait_ge(s_dv, dv0 + 4)
            inc(s.activation(stats[:, 5:6], stats[:, 4:5], AF.Ln, bias=eps_sb[:, 0:1]), "ac")
            s.wait_ge(s_ac, ac0 + 2)
            inc(s.activation(stats[:, 6:7], stats[:, 5:6], AF.Exp, scale=-0.5), "ac")
            s.wait_ge(s_dv, dv0 + 5)
            inc(s.activation(lnt[:, :], xbuf[:, :], AF.Identity,
                             scale=stats[:, 6:7], bias=stats[:, 7:8]), "ac")
            assert C["ac"] == ac0 + 4
            LM[f"ac_ln_{tag}"] = C["ac"]

        def emit_ln_phase(phase, src_ap, extra_first_wait=None):
            for tt in range(8):
                tag = f"{phase}t{tt}"
                LM[f"ac0_{tag}"] = C["ac"]
                LM[f"dv0_{tag}"] = C["dv"]
                for q in range(4):
                    LM[f"dv_lnTev_{tag}_{q}"] = LM[f"dv0_{tag}"] + 6 + q

                @block.sync
                def _(sy, tt=tt, tag=tag):
                    if phase == "l2" and tt == 0:
                        sy.wait_ge(s_sp, LM["sp_xp_done"])
                    if tt >= 1:
                        sy.wait_ge(s_ac, LM[f"ac_ln_{phase}t{tt - 1}"])
                    sy.dma_start(out=xt[:, :],
                                 in_=src_ap[128 * tt:128 * (tt + 1), :]).then_inc(s_x, 16)
                    C["x"] += 16
                LM[f"x_{tag}"] = C["x"]

                @block.vector
                def _(v, tt=tt, tag=tag):
                    v.wait_ge(s_x, LM[f"x_{tag}"])
                    if extra_first_wait and tt == 0:
                        v.wait_ge(*extra_first_wait)
                    ln_stats_dve(v, xt, tag)

                @block.scalar
                def _(s, tt=tt, tag=tag):
                    s.wait_ge(s_x, LM[f"x_{tag}"])
                    prev = LM.get(f"pe_lnT_{phase}t{tt - 1}") if tt >= 1 else None
                    ln_acts(s, xt, tag, prev)

                @block.tensor
                def _(t, tt=tt, tag=tag):
                    t.wait_ge(s_ac, LM[f"ac_ln_{tag}"])
                    for q in range(4):
                        if q >= 2:
                            prev_ev = LM.get(f"dv_lnTev_{tag}_{q - 2}", 0)
                        else:
                            prev_ev = LM.get(f"dv_lnTev_{phase}t{tt - 1}_{q + 2}",
                                             LM.get("dv_xp_31", 0) if phase == "l2" else 0)
                        t.wait_ge(s_dv, prev_ev)
                        last = None
                        for k in range(8):
                            kp = 8 * q + k
                            if phase == "l1":
                                src = lnt[:, 128 * kp:128 * (kp + 1)]
                            else:
                                jj, i8 = kp // 8, kp % 8
                                src = lnt[:, 512 * i8 + jj:512 * (i8 + 1):4]
                            last = t.transpose(PSB[:, 1024 * (q % 2) + 128 * k:1024 * (q % 2) + 128 * (k + 1)],
                                               src, idn[:, :])
                        inc(last, "pe")
                        LM[f"pe_lnT_{tag}_{q}"] = C["pe"]
                    LM[f"pe_lnT_{tag}"] = C["pe"]

                @block.vector
                def _(v, tt=tt, tag=tag):
                    if phase == "l2" and tt == 0:
                        v.wait_ge(s_pe, LM["pe_op_done"])
                    for q in range(4):
                        v.wait_ge(s_pe, LM[f"pe_lnT_{tag}_{q}"])
                        inc(v.tensor_copy(ln_arena(tt, 8 * q, 8),
                                          PSB[:, 1024 * (q % 2):1024 * (q % 2 + 1)]), "dv")
                        assert C["dv"] == LM[f"dv_lnTev_{tag}_{q}"]
            LM[f"dv_{phase}T_done"] = C["dv"]

        emit_ln_phase("l1", x_in)

        def emit_wload(key, src, prev_mm_lm, cc_wait=False):
            @block.gpsimd
            def _(g):
                if cc_wait:
                    g.wait_ge(s_cc, LM["cc_ag"])
                if prev_mm_lm:
                    g.wait_ge(s_pe, prev_mm_lm)
                g.dma_start(out=wbuf[key % 2].ap().rearrange("p (kc o) -> p kc o", o=128),
                            in_=src.rearrange("(kc p) o -> p kc o", p=128)).then_inc(s_w, 16)
                C["w"] += 16
            LM[f"w_chunk_{key}"] = C["w"]

        projm = {}

        def emit_proj_mm(key, psum, prev_evac, rhs_fn):
            @block.tensor
            def _(t):
                t.wait_ge(s_w, LM[f"w_chunk_{key}"])
                t.wait_ge(s_dv, LM["dv_l1T_done"])
                if prev_evac:
                    t.wait_ge(s_ac, prev_evac)
                last = None
                for kc in range(32):
                    for half in range(2):
                        last = t.matmul(psum[:, 512 * half:512 * (half + 1)],
                                        wbuf[key % 2][:, 128 * kc:128 * (kc + 1)],
                                        rhs_fn(kc, half),
                                        start=(kc == 0), stop=(kc == 31))
                inc(last, "pe")
            projm[key % 2] = C["pe"]
            LM[f"pe_proj_{key}"] = C["pe"]

        for oc in range(32, 64):
            slot = oc % 2
            emit_wload(oc, wqkvT[:, 128 * oc:128 * (oc + 1)], projm.get(slot))
            emit_proj_mm(oc, PS1 if slot == 0 else PS2,
                         LM.get(f"ac_evac_{oc - 2}"), ln_rhs)

            @block.scalar
            def _(s, oc=oc, slot=slot):
                s.wait_ge(s_pe, LM[f"pe_proj_{oc}"])
                if oc - 2 >= 32:
                    s.wait_ge(s_sp, LM[f"sp_K_{oc - 2}"])
                inc(s.activation(evb[slot][:, :], (PS1 if slot == 0 else PS2)[:, :],
                                 AF.Identity, bias=bq_sb[:, oc:oc + 1]), "ac")
            LM[f"ac_evac_{oc}"] = C["ac"]

            @block.sync
            def _(sy, oc=oc, slot=slot):
                sy.wait_ge(s_ac, LM[f"ac_evac_{oc}"])
                op = oc - 32
                sy.dma_start(out=contrib.ap()[128 * op:128 * (op + 1), :],
                             in_=evb[slot][:, 0:512]).then_inc(s_sp, 16)
                sy.dma_start(out=contrib.ap()[2 * D + 128 * op:2 * D + 128 * (op + 1), :],
                             in_=evb[slot][:, 512:1024]).then_inc(s_sp, 16)
                C["sp"] += 32
            LM[f"sp_K_{oc}"] = C["sp"]

        for oc in range(64, 96):
            slot = oc % 2
            emit_wload(oc, wqkvT[:, 128 * oc:128 * (oc + 1)], projm.get(slot))
            emit_proj_mm(oc, PS1 if slot == 0 else PS2,
                         LM.get(f"ac_evac_{oc - 2}"), ln_rhs)

            @block.scalar
            def _(s, oc=oc, slot=slot):
                s.wait_ge(s_pe, LM[f"pe_proj_{oc}"])
                if oc - 2 >= 64:
                    s.wait_ge(s_pe, LM[f"pe_vT_{oc - 2}"])
                elif oc - 2 >= 32:
                    s.wait_ge(s_sp, LM[f"sp_K_{oc - 2}"])
                inc(s.activation(evb[slot][:, :], (PS1 if slot == 0 else PS2)[:, :],
                                 AF.Identity, bias=bq_sb[:, oc:oc + 1]), "ac")
            LM[f"ac_evac_{oc}"] = C["ac"]

            @block.tensor
            def _(t, oc=oc, slot=slot):
                t.wait_ge(s_ac, LM[f"ac_evac_{oc}"])
                for half in range(2):
                    t.wait_ge(s_dv, LM.get(f"dv_vT_{oc - 1}_{half}", 0))
                    last = None
                    for k in range(4):
                        tt = 4 * half + k
                        last = t.transpose(PSB[:, 1024 * half + 128 * k:1024 * half + 128 * (k + 1)],
                                           evb[slot][:, 128 * tt:128 * (tt + 1)], idn[:, :])
                    inc(last, "pe")
                    LM[f"pe_vT_{oc}_{half}"] = C["pe"]
                LM[f"pe_vT_{oc}"] = C["pe"]

            @block.vector
            def _(v, oc=oc):
                for half in range(2):
                    v.wait_ge(s_pe, LM[f"pe_vT_{oc}_{half}"])
                    v.wait_ge(s_sp, LM.get(f"sp_V_{oc - 1}_{half}", 0))
                    inc(v.tensor_copy(vst[:, 512 * half:512 * (half + 1)],
                                      PSB[:, 1024 * half:1024 * half + 512]), "dv")
                    LM[f"dv_vT_{oc}_{half}"] = C["dv"]

            @block.sync
            def _(sy, oc=oc):
                op = oc - 64
                for half in range(2):
                    sy.wait_ge(s_dv, LM[f"dv_vT_{oc}_{half}"])
                    for k in range(4):
                        tt = 4 * half + k
                        blk = 1 if tt < 4 else 3
                        trow = 128 * (tt % 4)
                        sy.dma_start(
                            out=cw[512 * blk + trow:512 * blk + trow + 128,
                                   op // 4, 128 * (op % 4):128 * (op % 4) + 128],
                            in_=vst[:, 512 * half + 128 * k:512 * half + 128 * (k + 1)],
                        ).then_inc(s_sp, 16)
                        C["sp"] += 16
                    LM[f"sp_V_{oc}_{half}"] = C["sp"]
        LM["sp_contrib_done"] = C["sp"]



        for oc in range(0, 32):
            slot = oc % 2
            emit_wload(200 + oc, wqkvT[:, 128 * oc:128 * (oc + 1)], projm.get(slot))
            prev_evac = LM.get(f"ac_evac_q{oc - 2}") if oc >= 2 else LM[f"ac_evac_{94 + oc}"]
            emit_proj_mm(200 + oc, PS1 if slot == 0 else PS2, prev_evac, ln_rhs)
            LM[f"pe_proj_q{oc}"] = LM[f"pe_proj_{200 + oc}"]

            @block.scalar
            def _(s, oc=oc, slot=slot):
                s.wait_ge(s_pe, LM[f"pe_proj_q{oc}"])
                if oc >= 2:
                    s.wait_ge(s_sp, LM[f"sp_Q_{oc - 2}"])
                else:
                    s.wait_ge(s_pe, LM[f"pe_vT_{94 + oc}"])
                inc(s.activation(evb[slot][:, :], (PS1 if slot == 0 else PS2)[:, :],
                                 AF.Identity, scale=QKV_SCALE, bias=bqq_sb[:, oc:oc + 1]), "ac")
            LM[f"ac_evac_q{oc}"] = C["ac"]

            @block.sync
            def _(sy, oc=oc, slot=slot):
                sy.wait_ge(s_ac, LM[f"ac_evac_q{oc}"])
                sy.dma_start(out=qd.ap()[128 * oc:128 * (oc + 1), :],
                             in_=evb[slot][:, :]).then_inc(s_sp, 16)
                C["sp"] += 16
            LM[f"sp_Q_{oc}"] = C["sp"]
        LM["sp_Q_done"] = C["sp"]
        LM["pe_q_done"] = C["pe"]

        if not skip_ag:
            @block.gpsimd
            def _(g):
                g.wait_ge(s_sp, LM["sp_contrib_done"])
                inc(g.collective_compute(
                    "AllGather", ALU.bypass,
                    replica_groups=[list(range(NCORE))],
                    ins=[contrib.ap()[0:8192, :]],
                    outs=[gathered.ap()[0:65536, :]],
                ), "cc")
                inc(g.collective_compute(
                    "AllGather", ALU.bypass,
                    replica_groups=[list(range(NCORE))],
                    ins=[contrib.ap()[8192:16384, :]],
                    outs=[gathered.ap()[65536:131072, :]],
                ), "cc")
        else:
            @block.gpsimd
            def _(g):
                g.wait_ge(s_sp, LM["sp_contrib_done"])
                inc(g.nop(), "cc")
        LM["cc_ag"] = C["cc"]

        for n in ([] if skip_dq else range(32)):
            @block.sync
            def _(sy, n=n):
                if n >= 1:
                    sy.wait_ge(s_dv, LM[f"dv_uq_{n - 1}"])
                sy.dma_start(out=qwb[:, :],
                             in_=qw_in[128 * n:128 * (n + 1), :]).then_inc(s_x, 16)
                C["x"] += 16
            LM[f"x_qw_{n}"] = C["x"]

            @block.vector
            def _(v, n=n):
                v.wait_ge(s_x, LM[f"x_qw_{n}"])
                if n >= 1:
                    v.wait_ge(s_ac, LM[f"ac_dq_{n - 1}"])
                v.tensor_single_scalar(uqb[:, 0:1024], qwb[:, 0:1024], 63, ALU.bitwise_and)
                v.tensor_single_scalar(uqb[:, 1024:2048], qwb[:, 0:1024], 6, ALU.logical_shift_right)
                v.tensor_single_scalar(uqb[:, 2048:3072], qwb[:, 1024:2048], 63, ALU.bitwise_and)
                inc(v.tensor_single_scalar(uqb[:, 3072:4096], qwb[:, 2048:3072], 63,
                                           ALU.bitwise_and), "dv")
            LM[f"dv_uq_{n}"] = C["dv"]

            @block.scalar
            def _(s, n=n):
                s.wait_ge(s_dv, LM[f"dv_uq_{n}"])
                if n >= 1:
                    s.wait_ge(s_sp, LM[f"sp_dq_{n - 1}"])
                inc(s.activation(wdq_sb[:, :], uqb[:, :], AF.Identity,
                                 scale=wsc_sb[:, n:n + 1], bias=nzs_sb[:, n:n + 1]), "ac")
            LM[f"ac_dq_{n}"] = C["ac"]

            @block.sync
            def _(sy, n=n):
                sy.wait_ge(s_ac, LM[f"ac_dq_{n}"])
                sy.dma_start(out=wdq.ap()[128 * n:128 * (n + 1), :],
                             in_=wdq_sb[:, :]).then_inc(s_sp, 16)
                C["sp"] += 16
            LM[f"sp_dq_{n}"] = C["sp"]
        LM["sp_dq_done"] = C["sp"]
        if skip_dq:
            LM[f"ac_dq_done"] = C["ac"]

        if stop_after_q:
            if dbg:
                @block.sync
                def _(sy):
                    sy.wait_ge(s_sp, LM["sp_Q_done"])
                    if "d_qd" in dbg:
                        sy.dma_start(out=dbg["d_qd"], in_=qd.ap()).then_inc(s_sp, 16)
                        C["sp"] += 16
                    if "d_gath" in dbg:
                        sy.wait_ge(s_cc, LM["cc_ag"])
                        sy.dma_start(out=dbg["d_gath"], in_=gathered.ap()).then_inc(s_sp, 16)
                        C["sp"] += 16
                    if "d_wdq" in dbg:
                        sy.wait_ge(s_sp, LM["sp_dq_done"])
                        sy.dma_start(out=dbg["d_wdq"], in_=wdq.ap()).then_inc(s_sp, 16)
                        C["sp"] += 16

            @block.sync
            def _(sy):
                sy.wait_ge(s_sp, C["sp"])
                sy.wait_ge(s_cc, LM["cc_ag"])
            return nc

        CPORD = [0, 8, 1, 9, 2, 10, 3, 11, 4, 12, 5, 13, 6, 14, 7, 15]
        CPK = {t: k for k, t in enumerate(CPORD)}

        SCH = {}
        pes, acs, dvs = C["pe"], C["ac"], C["dv"]
        pe_attn_h = {}
        dv_attn_h = {}
        ring_lm = [0, 0]
        KV_TOT = [0, 0]
        QH_TOT = [0, 0]
        OS_TOT = [0, 0]
        OS_HIST = []
        gctr = 0
        prev_pe_mm_end = LM["pe_q_done"]
        prev_ac_end = C["ac"]
        prev_dv_norm = 0
        HORD = list(range(H - 1, -1, -1))
        pe_attn_pos = {}
        dv_attn_pos = {}
        for hi, h in enumerate(HORD):
            QH_TOT[hi % 2] += 32
            qh_wait = QH_TOT[hi % 2]
            bias_ok = BIAS_OK[h]
            for role in range(2):
                base, Tp = COLS[(h, role)]
                ngrp = Tp // 4
                npair = Tp // 2
                ndve = 4 if bias_ok else Tp
                d = {"base": base, "Tp": Tp, "ngrp": ngrp, "npair": npair,
                     "bias_ok": bias_ok, "ndve": ndve,
                     "qh_wait": qh_wait, "prev_pe_mm_end": prev_pe_mm_end,
                     "prev_ac_end": prev_ac_end, "prev_dv_norm": prev_dv_norm}
                dv_pt = {i: dvs + i + 1 for i in range(ndve)}
                kv_wait = {}
                ring_w = {}
                ring_rs = {}
                os_wait = {}
                for j in range(ngrp):
                    st = j % 2
                    KV_TOT[st] += 32
                    kv_wait[j] = KV_TOT[st]
                    if j >= 1 and not bias_ok:
                        rs = gctr % 2
                        ring_rs[j] = rs
                        ring_w[j] = ring_lm[rs]
                        OS_TOT[rs] += 16
                        os_wait[j] = OS_TOT[rs]
                        ring_lm[rs] = dv_pt[min(4 * j + 3, Tp - 1)]
                        gctr += 1
                pe_S = {}
                pe_lav = {}
                for i in range(Tp):
                    pes += 1
                    pe_S[i] = pes
                    if i >= 2:
                        pes += 1
                        pe_lav[i - 2] = pes
                for iv in (Tp - 2, Tp - 1):
                    pes += 1
                    pe_lav[iv] = pes
                d.update(kv_wait=kv_wait, ring_w=ring_w, ring_rs=ring_rs,
                         os_wait=os_wait,
                         pe_S=pe_S, pe_lav=pe_lav, pe_mm_end=pes,
                         ac_exp={i: acs + i + 1 for i in range(Tp)},
                         ac_end=acs + Tp, dv_pt=dv_pt,
                         dv_recip=dvs + ndve + 1, dv_recb=dvs + ndve + 2,
                         dv_norm=dvs + ndve + 3,
                         pe_bcast=pes + 1)
                if not bias_ok:
                    d["dve_rs"] = {i: ring_rs[i // 4] for i in range(4, Tp)}
                acs += Tp
                dvs += ndve + 3
                pes += 1
                prev_pe_mm_end = d["pe_mm_end"]
                prev_ac_end = d["ac_end"]
                prev_dv_norm = d["dv_norm"]
                SCH[(h, role)] = d
            pe_attn_pos[hi] = pes
            dv_attn_pos[hi] = dvs

        def emit_copy(eng, pid, k, t):
            off = ((pid + (16 - t)) % 16) * 4194304
            eng.dma_start(out=rbuf.ap()[8192 * t:8192 * (t + 1), :],
                          in_=AP(gathered, off, [[512, 8192], [1, 512]])
                          ).then_inc(s_cp[k], 16)

        first_need = {}
        pos = 0
        for hi, h in enumerate(HORD):
            for role in range(2):
                d0 = SCH[(h, role)]
                for j in range(d0["ngrp"]):
                    t0_ = (j + 8 * role) & 15
                    first_need.setdefault(t0_, pos)
                pos += 1
        place_at = {}
        for k in range(6, 16):
            t0_ = CPORD[k]
            place_at.setdefault(max(first_need.get(t0_, 0) - 2, 0), []).append(k)

        @block.sync
        def _(sy):
            sy.wait_ge(s_cc, LM["cc_ag"])
            sy.wait_ge(s_sp, LM["sp_Q_done"])
            pid_s = sy.partition_id()
            for k in range(6):
                if k >= 2:
                    sy.wait_ge(s_cp[k - 2], 16)
                emit_copy(sy, pid_s, k, CPORD[k])
            for hi, h in enumerate(HORD):
                if hi >= 2:
                    sy.wait_ge(s_pe, pe_attn_pos[hi - 2])
                    sy.wait_ge(s_dv, dv_attn_pos[hi - 2])
                sl_ = hi % 2
                sy.dma_start(out=qb[sl_][:, :],
                             in_=qd.ap()[128 * h:128 * (h + 1), :]).then_inc(s_qh2[sl_], 16)
                sy.dma_start(out=dsb[sl_][:, :],
                             in_=dstrip_in[128 * h:128 * (h + 1), :]).then_inc(s_qh2[sl_], 16)
                for role in range(2):
                    d = SCH[(h, role)]
                    for j in range(d["ngrp"]):
                        gslot = j % 2
                        st = j % 2
                        if j >= 2:
                            sy.wait_ge(s_pe, d["pe_lav"][4 * (j - 2) + 3])
                        else:
                            sy.wait_ge(s_pe, d["prev_pe_mm_end"])
                        t_rel = (j + 8 * role) & 15
                        sy.wait_ge(s_cp[CPK[t_rel]], 16)
                        krow = 8192 * t_rel + 128 * h
                        sy.dma_start(out=kbb[gslot][:, :],
                                     in_=rbuf.ap()[krow:krow + 128, :]
                                     ).then_inc(s_kv2[st], 16)
                        if j >= 1 and not d["bias_ok"]:
                            rs = d["ring_rs"][j]
                            sy.wait_ge(s_dv, d["ring_w"][j])
                            off = OS_OFF[h] + 512 * (j - 1)
                            sy.dma_start(out=osb[rs][:, :],
                                         in_=ostrip_in[:, off:off + 896]
                                         ).then_inc(s_os3[rs], 16)

        for hi, h in enumerate(HORD):
            for role in range(2):
                d = SCH[(h, role)]
                kjit = place_at.get(2 * hi + role, [])
                if kjit:
                    @block.scalar
                    def _(s, kjit=kjit):
                        pid_a = s.partition_id()
                        for k in kjit:
                            emit_copy(s, pid_a, k, CPORD[k])
                base, Tp = d["base"], d["Tp"]
                pe_S, pe_lav = d["pe_S"], d["pe_lav"]
                ac_exp, dv_pt = d["ac_exp"], d["dv_pt"]

                @block.tensor
                def _(t, h=h, hi=hi, role=role, d=d, Tp=Tp, pe_S=pe_S, pe_lav=pe_lav,
                      ac_exp=ac_exp, dv_pt=dv_pt):
                    t.wait_ge(s_qh2[hi % 2], d["qh_wait"])
                    t.wait_ge(s_dv, d["prev_dv_norm"])

                    def lav(iv):
                        if iv in dv_pt:
                            t.wait_ge(s_dv, dv_pt[iv])
                        else:
                            t.wait_ge(s_ac, ac_exp[iv])
                        jv = iv // 4
                        mv = 3 - (iv % 4)
                        col = 512 * (iv % 8)
                        t.matmul(PS2[0:1, 512:1024], onesb[:, :],
                                 ptb[:, col:col + 512],
                                 start=(iv == 0), stop=(iv == Tp - 1))
                        inc(t.matmul(PS2[:, 0:512], vbb[jv % 2][:, 128 * mv:128 * (mv + 1)],
                                     ptb[:, col:col + 512],
                                     start=(iv == 0), stop=(iv == Tp - 1)), "pe")

                    for i in range(Tp):
                        j = i // 4
                        m = 3 - (i % 4)
                        pair = i // 2
                        sp_ = PS1 if pair % 2 == 0 else PS3
                        if i % 4 == 0:
                            t.wait_ge(s_kv2[j % 2], d["kv_wait"][j])
                        if i % 2 == 0:
                            if pair >= 2:
                                t.wait_ge(s_ac, ac_exp[2 * pair - 3])
                            else:
                                t.wait_ge(s_ac, d["prev_ac_end"])
                        inc(t.matmul(sp_[:, 512 * (i % 2):512 * (i % 2 + 1)],
                                     kbb[j % 2][:, 128 * m:128 * (m + 1)],
                                     qb[hi % 2][:, 512 * role:512 * (role + 1)],
                                     start=True, stop=True), "pe")
                        if i >= 2:
                            lav(i - 2)
                    lav(Tp - 2)
                    lav(Tp - 1)
                    assert C["pe"] == d["pe_mm_end"], (C["pe"], d["pe_mm_end"])

                @block.scalar
                def _(s, h=h, role=role, d=d, base=base, pe_S=pe_S, pe_lav=pe_lav):
                    def vload(j):
                        gslot = j % 2
                        if j >= 2:
                            s.wait_ge(s_pe, pe_lav[4 * (j - 2) + 3])
                        else:
                            s.wait_ge(s_pe, d["prev_pe_mm_end"])
                        t_rel = (j + 8 * role) & 15
                        s.wait_ge(s_cp[CPK[t_rel]], 16)
                        vbase = 8192 * t_rel * 512 + 2097152 + 128 * h
                        s.dma_start(out=vbb[gslot].ap().rearrange("p (m d) -> p m d", d=128),
                                    in_=AP(rbuf, vbase, [[4096, 128], [524288, 4], [1, 128]])
                                    ).then_inc(s_kv2[j % 2], 16)
                    vload(0)
                    if d["ngrp"] >= 2:
                        vload(1)
                    for i in range(d["Tp"]):
                        pair = i // 2
                        if i % 4 == 0 and i >= 4 and i // 4 + 1 < d["ngrp"]:
                            vload(i // 4 + 1)
                        s.wait_ge(s_pe, pe_S[i])
                        if i >= 8:
                            s.wait_ge(s_pe, pe_lav[i - 8])
                        else:
                            s.wait_ge(s_pe, d["prev_pe_mm_end"])
                        sp_ = PS1 if pair % 2 == 0 else PS3
                        col = 512 * (i % 8)
                        pcol = sp_[:, 512 * (i % 2):512 * (i % 2 + 1)]
                        if d["bias_ok"] and i >= 4:
                            inc(s.activation(ptb[:, col:col + 512], pcol, AF.Exp,
                                             bias=ktab[:, base + i:base + i + 1]), "ac")
                        else:
                            inc(s.activation(ptb[:, col:col + 512], pcol, AF.Exp), "ac")

                @block.vector
                def _(v, hi=hi, d=d, base=base, ac_exp=ac_exp):
                    for i in range(d["ndve"]):
                        v.wait_ge(s_ac, ac_exp[i])
                        col = 512 * (i % 8)
                        if d["bias_ok"]:
                            strip = dsb[hi % 2][:, 512 * i:512 * i + 512]
                            inc(v.tensor_tensor(ptb[:, col:col + 512],
                                                ptb[:, col:col + 512],
                                                strip, ALU.mult), "dv")
                            continue
                        if i < 4:
                            strip = dsb[hi % 2][:, 128 * i:128 * i + 512]
                        else:
                            j = i // 4
                            rs = d["dve_rs"][i]
                            v.wait_ge(s_os3[rs], d["os_wait"][j])
                            loc = 128 * (i - 4) - 512 * (j - 1)
                            strip = osb[rs][:, loc:loc + 512]
                        inc(v.scalar_tensor_tensor(ptb[:, col:col + 512],
                                                   ptb[:, col:col + 512],
                                                   ktab[:, base + i:base + i + 1],
                                                   strip, ALU.mult, ALU.mult), "dv")

                @block.vector
                def _(v, d=d, Tp=Tp, pe_lav=pe_lav):
                    v.wait_ge(s_pe, pe_lav[Tp - 1])
                    inc(v.reciprocal(recip[:, :], PS2[0:1, 512:1024]), "dv")

                @block.tensor
                def _(t, d=d):
                    t.wait_ge(s_dv, d["dv_recip"])
                    inc(t.matmul(PS2[:, 512:1024], onesf[:, :], recip[:, :],
                                 start=True, stop=True), "pe")

                @block.scalar
                def _(s, h=h, role=role, d=d):
                    if "d_l" in dbg:
                        s.wait_ge(s_pe, d["pe_bcast"])
                        s.dma_start(out=dbg["d_l"][2 * h + role:2 * h + role + 1, :],
                                    in_=recip[:, :]).then_inc(s_sp, 16)
                        C["sp"] += 16

                @block.vector
                def _(v, h=h, hi=hi, role=role, d=d):
                    v.wait_ge(s_pe, d["pe_bcast"])
                    inc(v.tensor_copy(recb[:, :], PS2[:, 512:1024]), "dv")
                    v.wait_ge(s_dv, d["dv_recb"])
                    if hi == 0 and role == 0:
                        v.wait_ge(s_pe, LM["pe_q_done"])
                    inc(v.scalar_tensor_tensor(ao_arena(h, 512 * role, 512),
                                               PS2[:, 0:512], 1.0, recb[:, :],
                                               ALU.mult, ALU.mult), "dv")

                assert C["pe"] == d["pe_bcast"], (C["pe"], d["pe_bcast"])
                assert C["ac"] == d["ac_end"] and C["dv"] == d["dv_norm"]
            LM[f"pe_attn_p{hi}"] = C["pe"]
            LM[f"dv_attn_p{hi}"] = C["dv"]
        LM["attn_done_dv"] = C["dv"]

        if stop_after_attn:
            @block.sync
            def _(sy):
                sy.wait_ge(s_dv, LM["attn_done_dv"])
                if "d_rb" in dbg:
                    sy.dma_start(out=dbg["d_rb"], in_=rbuf.ap()).then_inc(s_sp, 16)
                    C["sp"] += 16
                if "d_ao" in dbg:
                    sy.dma_start(out=dbg["d_ao"], in_=arena[:, :]).then_inc(s_sp, 16)
                    C["sp"] += 16
                sy.wait_ge(s_sp, C["sp"])
            return nc

        LM["ac_attn_end"] = C["ac"]
        for oc in range(32):
            slot = oc % 2
            emit_wload(300 + oc, woutT[:, 128 * oc:128 * (oc + 1)], projm.get(slot),
                       cc_wait=(oc < 2))

            @block.sync
            def _(sy, oc=oc, slot=slot):
                if oc >= 2:
                    sy.wait_ge(s_sp, LM[f"sp_xp_{oc - 2}"])
                sy.dma_start(out=xo_sb[slot].ap().rearrange("p (tt o) -> p tt o", o=128),
                             in_=x_in[:, 128 * oc:128 * (oc + 1)]
                             .rearrange("(tt p) o -> p tt o", p=128)).then_inc(s_x, 16)
                C["x"] += 16
            LM[f"x_xo_{oc}"] = C["x"]

            @block.tensor
            def _(t, oc=oc, slot=slot):
                t.wait_ge(s_w, LM[f"w_chunk_{300 + oc}"])
                t.wait_ge(s_dv, LM["attn_done_dv"])
                if oc >= 2:
                    t.wait_ge(s_ac, LM[f"ac_op_{oc - 2}"])
                else:
                    t.wait_ge(s_ac, LM["ac_attn_end"])
                psum = PS1 if slot == 0 else PS2
                last = None
                for dc in range(32):
                    for half in range(2):
                        last = t.matmul(psum[:, 512 * half:512 * (half + 1)],
                                        wbuf[slot][:, 128 * dc:128 * (dc + 1)],
                                        arena[:, 1024 * dc + 512 * half:1024 * dc + 512 * (half + 1)],
                                        start=(dc == 0), stop=(dc == 31))
                inc(last, "pe")
            projm[slot] = C["pe"]
            LM[f"pe_op_{oc}"] = C["pe"]

            @block.scalar
            def _(s, oc=oc, slot=slot):
                s.wait_ge(s_pe, LM[f"pe_op_{oc}"])
                if oc >= 2:
                    s.wait_ge(s_pe, LM[f"pe_opT_{oc - 2}"])
                inc(s.activation(evb[slot][:, :], (PS1 if slot == 0 else PS2)[:, :],
                                 AF.Identity, bias=bo_sb[:, oc:oc + 1]), "ac")
            LM[f"ac_op_{oc}"] = C["ac"]

            @block.tensor
            def _(t, oc=oc, slot=slot):
                t.wait_ge(s_ac, LM[f"ac_op_{oc}"])
                for half in range(2):
                    t.wait_ge(s_dv, LM.get(f"dv_xp_{oc - 1}_{half}", LM["attn_done_dv"]))
                    last = None
                    for k in range(4):
                        tt = 4 * half + k
                        last = t.transpose(PSB[:, 1024 * half + 128 * k:1024 * half + 128 * (k + 1)],
                                           evb[slot][:, 128 * tt:128 * (tt + 1)], idn[:, :])
                    inc(last, "pe")
                    LM[f"pe_opT_{oc}_{half}"] = C["pe"]
                LM[f"pe_opT_{oc}"] = C["pe"]

            @block.vector
            def _(v, oc=oc, slot=slot):
                v.wait_ge(s_x, LM[f"x_xo_{oc}"])
                if oc >= 2:
                    v.wait_ge(s_sp, LM[f"sp_xp_{oc - 2}"])
                for half in range(2):
                    v.wait_ge(s_pe, LM[f"pe_opT_{oc}_{half}"])
                    inc(v.tensor_tensor(xps[slot][:, 512 * half:512 * (half + 1)],
                                        PSB[:, 1024 * half:1024 * half + 512],
                                        xo_sb[slot][:, 512 * half:512 * (half + 1)],
                                        ALU.add), "dv")
                    LM[f"dv_xp_{oc}_{half}"] = C["dv"]
                LM[f"dv_xp_{oc}"] = C["dv"]

            @block.sync
            def _(sy, oc=oc, slot=slot):
                sy.wait_ge(s_dv, LM[f"dv_xp_{oc}"])
                sy.dma_start(out=xp_d.ap()[:, 128 * oc:128 * (oc + 1)]
                             .rearrange("(tt p) o -> p tt o", p=128),
                             in_=xps[slot].ap().rearrange("p (tt o) -> p tt o", o=128)).then_inc(s_sp, 16)
                C["sp"] += 16
            LM[f"sp_xp_{oc}"] = C["sp"]
        LM["sp_xp_done"] = C["sp"]
        LM["pe_op_done"] = LM["pe_opT_31"]

        emit_ln_phase("l2", xp_d.ap())

        for ngi in range(16):
            gslot = ngi % 2
            pe0f = C["pe"]
            pe_ffn = {kp: pe0f + kp + 1 for kp in range(32)}

            @block.sync
            def _(sy, ngi=ngi, gslot=gslot):
                for ncc in range(2):
                    n = 2 * ngi + ncc
                    dst = xo_sb[gslot] if ncc == 0 else xps[gslot]
                    if ngi >= 2:
                        sy.wait_ge(s_sp, LM[f"sp_y_{2 * (ngi - 2) + ncc}"])
                    else:
                        sy.wait_ge(s_sp, LM["sp_xp_done"])
                    sy.dma_start(out=dst.ap().rearrange("p (tt o) -> p tt o", o=128),
                                 in_=xp_d.ap()[:, 128 * n:128 * (n + 1)]
                                 .rearrange("(tt p) o -> p tt o", p=128)).then_inc(s_x, 16)
                    C["x"] += 16
            LM[f"x_ffn_{ngi}"] = C["x"]

            @block.scalar
            def _(s, ngi=ngi):
                if ngi == 0:
                    s.wait_ge(s_sp, LM["sp_dq_done"])
                for kp in range(32):
                    wslot = kp % 2
                    if kp >= 2:
                        s.wait_ge(s_pe, pe_ffn[kp - 2])
                    elif ngi >= 1:
                        s.wait_ge(s_pe, LM[f"pe_ffn_{ngi - 1}_{30 + kp}"])
                    s.dma_start_transpose(wf_sb[wslot][:, :],
                                          wdq.ap()[256 * ngi:256 * (ngi + 1),
                                                   128 * kp:128 * (kp + 1)]).then_inc(s_kv2[wslot], 16)
                    KV_TOT[wslot] += 16
                    LM[f"kv_wf_{ngi}_{kp}"] = KV_TOT[wslot]

            @block.tensor
            def _(t, ngi=ngi):
                t.wait_ge(s_dv, LM["dv_l2T_done"])
                t.wait_ge(s_sp, LM["sp_dq_done"])
                if ngi >= 1:
                    t.wait_ge(s_ac, LM[f"ac_ffn_{ngi - 1}"])
                else:
                    t.wait_ge(s_ac, LM["ac_op_31"])
                for kp in range(32):
                    wslot = kp % 2
                    t.wait_ge(s_kv2[wslot], LM[f"kv_wf_{ngi}_{kp}"])
                    last = None
                    for ncc in range(2):
                        psum = PS1 if ncc == 0 else PS2
                        for half in range(2):
                            last = t.matmul(psum[:, 512 * half:512 * (half + 1)],
                                            wf_sb[wslot][:, 128 * ncc:128 * (ncc + 1)],
                                            arena_v[:, 4 * half:4 * half + 4, 128 * kp:128 * (kp + 1)],
                                            start=(kp == 0), stop=(kp == 31))
                    inc(last, "pe")
                    LM[f"pe_ffn_{ngi}_{kp}"] = C["pe"]
            assert C["pe"] == pe0f + 32

            for ncc in range(2):
                n = 2 * ngi + ncc
                eslot = n % 2

                @block.scalar
                def _(s, ngi=ngi, ncc=ncc, n=n, eslot=eslot):
                    s.wait_ge(s_pe, LM[f"pe_ffn_{ngi}_31"])
                    if n >= 2:
                        s.wait_ge(s_pe, LM[f"pe_yT_{n - 2}"])
                    inc(s.activation(evb[eslot][:, :], (PS1 if ncc == 0 else PS2)[:, :],
                                     AF.Identity, bias=fb_sb[:, n:n + 1]), "ac")
                LM[f"ac_ffn_n{n}"] = C["ac"]
                if ncc == 1:
                    LM[f"ac_ffn_{ngi}"] = C["ac"]

                @block.tensor
                def _(t, n=n, eslot=eslot):
                    t.wait_ge(s_ac, LM[f"ac_ffn_n{n}"])
                    for half in range(2):
                        t.wait_ge(s_dv, LM.get(f"dv_y_{n - 1}_{half}", LM["dv_l2T_done"]))
                        last = None
                        for k in range(4):
                            tt = 4 * half + k
                            last = t.transpose(PSB[:, 1024 * half + 128 * k:1024 * half + 128 * (k + 1)],
                                               evb[eslot][:, 128 * tt:128 * (tt + 1)], idn[:, :])
                        inc(last, "pe")
                        LM[f"pe_yT_{n}_{half}"] = C["pe"]
                    LM[f"pe_yT_{n}"] = C["pe"]

                @block.vector
                def _(v, ngi=ngi, ncc=ncc, n=n, gslot=gslot):
                    v.wait_ge(s_x, LM[f"x_ffn_{ngi}"])
                    buf = xo_sb[gslot] if ncc == 0 else xps[gslot]
                    for half in range(2):
                        v.wait_ge(s_pe, LM[f"pe_yT_{n}_{half}"])
                        inc(v.tensor_tensor(buf[:, 512 * half:512 * (half + 1)],
                                            PSB[:, 1024 * half:1024 * half + 512],
                                            buf[:, 512 * half:512 * (half + 1)],
                                            ALU.add), "dv")
                        LM[f"dv_y_{n}_{half}"] = C["dv"]
                    LM[f"dv_y_n{n}"] = C["dv"]

                @block.sync
                def _(sy, n=n, ncc=ncc, gslot=gslot):
                    sy.wait_ge(s_dv, LM[f"dv_y_n{n}"])
                    src = xo_sb[gslot] if ncc == 0 else xps[gslot]
                    sy.dma_start(out=y_out[:, 128 * n:128 * (n + 1)]
                                 .rearrange("(tt p) o -> p tt o", p=128),
                                 in_=src.ap().rearrange("p (tt o) -> p tt o", o=128)).then_inc(s_sp, 16)
                    C["sp"] += 16
                LM[f"sp_y_{n}"] = C["sp"]
        LM["sp_y_done"] = C["sp"]

        if dbg:
            @block.sync
            def _(sy):
                sy.wait_ge(s_sp, LM["sp_y_done"])
                if "d_arena" in dbg:
                    sy.dma_start(out=dbg["d_arena"], in_=arena[:, :]).then_inc(s_sp, 16)
                    C["sp"] += 16
                if "d_xp" in dbg:
                    sy.dma_start(out=dbg["d_xp"], in_=xp_d.ap()).then_inc(s_sp, 16)
                    C["sp"] += 16
                if "d_qd" in dbg:
                    sy.dma_start(out=dbg["d_qd"], in_=qd.ap()).then_inc(s_sp, 16)
                    C["sp"] += 16
                if "d_gath" in dbg:
                    sy.dma_start(out=dbg["d_gath"], in_=gathered.ap()).then_inc(s_sp, 16)
                    C["sp"] += 16
                if "d_wdq" in dbg:
                    sy.dma_start(out=dbg["d_wdq"], in_=wdq.ap()).then_inc(s_sp, 16)
                    C["sp"] += 16

        @block.sync
        def _(sy):
            sy.wait_ge(s_sp, C["sp"])
    return nc


def make_in_maps():
    dstrip, ostrip, ident = host_tables()
    return dstrip, ostrip, ident


def kernel(L0_attn_x, L0_attn_w_qkv, L0_attn_b_qkv, L0_attn_w_out, L0_attn_b_out,
           L0_ffn_qweight, L0_ffn_weight_scale, L0_ffn_weight_zero_point, L0_ffn_bias):
    x = np.ascontiguousarray(np.asarray(L0_attn_x, dtype=np.float32).reshape(S, D))
    wqkvT_a = np.ascontiguousarray(np.asarray(L0_attn_w_qkv, dtype=np.float32).T)
    bqkv = np.ascontiguousarray(np.asarray(L0_attn_b_qkv, dtype=np.float32))
    woutT_a = np.ascontiguousarray(np.asarray(L0_attn_w_out, dtype=np.float32).T)
    bout = np.ascontiguousarray(np.asarray(L0_attn_b_out, dtype=np.float32))
    qw = np.asarray(L0_ffn_qweight, dtype=np.int32).reshape(D, 1024, 3)
    qw_r = np.ascontiguousarray(qw.transpose(0, 2, 1).reshape(D, 3072))
    wsc = np.ascontiguousarray(np.asarray(L0_ffn_weight_scale, dtype=np.float32).reshape(D))
    wzp = np.ascontiguousarray(np.asarray(L0_ffn_weight_zero_point, dtype=np.float32).reshape(D))
    fb = np.ascontiguousarray(np.asarray(L0_ffn_bias, dtype=np.float32))
    dstrip, ostrip, ident = host_tables()

    nc = build()
    in_maps = []
    for c in range(NCORE):
        xc = np.ascontiguousarray(
            np.concatenate([x[512 * c:512 * (c + 1)], x[512 * (c + 8):512 * (c + 9)]], axis=0))
        in_maps.append({
            "x_c": xc, "wqkvT": wqkvT_a, "bqkv": bqkv, "woutT": woutT_a, "bout": bout,
            "qw_r": qw_r, "wsc": wsc, "wzp": wzp, "fb": fb,
            "dstrip": dstrip, "ostrip": ostrip, "ident": ident,
            "ktab": kill_table(c),
        })
    res = run_bass_kernel_spmd(nc, in_maps, core_ids=list(range(NCORE)))
    out = np.empty((S, D), dtype=np.float32)
    for c in range(NCORE):
        yc = res.results[c]["y"]
        out[512 * c:512 * (c + 1)] = yc[0:512]
        out[512 * (c + 8):512 * (c + 9)] = yc[512:1024]
    return out.reshape(1, S, D)


# version 08
# speedup vs baseline: 1.0714x; 1.0643x over previous
import math
import numpy as np
import ml_dtypes
import concourse.bass as bass
import concourse.mybir as mybir
from concourse.expressions_rust import RuntimeValue
from concourse.ap import AP
from concourse.bass_utils import run_bass_kernel_spmd
from contextlib import ExitStack

F32 = mybir.dt.float32
BF16 = mybir.dt.bfloat16
I32 = mybir.dt.int32
AF = mybir.ActivationFunctionType
ALU = mybir.AluOpType
ds = bass.ds

H = 32
D = 4096
S = 8192
NCORE = 8
TLOC = 1024
CUT = 96.0
EPS = 1e-5
QKV_SCALE = 1.0 / math.sqrt(128.0)


def _slopes():
    m = np.arange(1, H + 1, dtype=np.float64) * (8.0 / H)
    return 0.5 ** m


def _plan():
    sl = _slopes()
    w = np.floor((511.0 + CUT / sl) / 128.0).astype(np.int64) + 1
    w = np.maximum(w, 5)
    t_lo = np.minimum(32, w)
    t_hi = np.minimum(64, w)
    cols = {}
    nt = 0
    for h in range(H):
        for role in range(2):
            T = int(t_lo[h] if role == 0 else t_hi[h])
            Tp = ((T + 3) // 4) * 4
            cols[(h, role)] = (nt, Tp)
            nt += Tp
    return sl, t_lo, t_hi, cols, nt


SL, T_LO, T_HI, COLS, NT = _plan()
OSW = [max(128 * (int(T_HI[h]) - 5) + 512, 512) for h in range(H)]
OS_OFF = np.cumsum([0] + OSW).tolist()
BIAS_OK = [bool(SL[h] * 511.0 <= 80.0) for h in range(H)]


def host_tables():
    sl = SL
    p = np.arange(128.0)[:, None]
    c = np.arange(896.0)[None, :]
    t = p - c + 384.0
    dstrip = np.zeros((H * 128, 2048), dtype=ml_dtypes.bfloat16)
    for h in range(H):
        if BIAS_OK[h]:
            c5 = np.arange(512.0)[None, :]
            for i in range(4):
                koff = p + 384.0 - 128.0 * i
                v = np.where(koff <= c5, np.exp(sl[h] * (koff - 511.0)), 0.0)
                dstrip[128 * h:128 * (h + 1), 512 * i:512 * (i + 1)] = \
                    v.astype(ml_dtypes.bfloat16)
        else:
            v = np.where(t <= 0, np.exp(sl[h] * t), 0.0)
            dstrip[128 * h:128 * (h + 1), 0:896] = v.astype(ml_dtypes.bfloat16)
    ostrip = np.zeros((128, OS_OFF[-1] + 896), dtype=ml_dtypes.bfloat16)
    for h in range(H):
        if not BIAS_OK[h]:
            cc = np.arange(float(OSW[h]))[None, :]
            v = np.exp(sl[h] * (p - cc - 128.0))
            ostrip[:, OS_OFF[h]:OS_OFF[h + 1]] = v.astype(ml_dtypes.bfloat16)
    ident = np.eye(128, dtype=ml_dtypes.bfloat16)
    return dstrip, ostrip, ident


def kill_table(core):
    kt = np.zeros((128, NT), dtype=np.float32)
    p = np.arange(128.0)
    for h in range(H):
        for role in range(2):
            base, Tp = COLS[(h, role)]
            B = core if role == 0 else core + 8
            T = int(T_LO[h] if role == 0 else T_HI[h])
            live = min(4 * (B + 1), T)
            if BIAS_OK[h]:
                kt[:, base:base + Tp] = -300.0
                for i in range(4, live):
                    kt[:, base + i] = SL[h] * (p + 384.0 - 128.0 * i - 511.0)
            else:
                kt[:, base:base + live] = 1.0
    return kt


def build(debug=(), stop_after_q=False, skip_dq=False, skip_ag=False, stop_after_attn=False):
    nc = bass.Bass("TRN2", target_bir_lowering=False)

    x_in = nc.dram_tensor("x_c", [TLOC, D], F32, kind="ExternalInput").ap()
    wqkvT = nc.dram_tensor("wqkvT", [D, 3 * D], F32, kind="ExternalInput").ap()
    bqkv_in = nc.dram_tensor("bqkv", [3 * D], F32, kind="ExternalInput").ap()
    woutT = nc.dram_tensor("woutT", [D, D], F32, kind="ExternalInput").ap()
    bout_in = nc.dram_tensor("bout", [D], F32, kind="ExternalInput").ap()
    qw_in = nc.dram_tensor("qw_r", [D, 3072], I32, kind="ExternalInput").ap()
    wsc_in = nc.dram_tensor("wsc", [D], F32, kind="ExternalInput").ap()
    wzp_in = nc.dram_tensor("wzp", [D], F32, kind="ExternalInput").ap()
    fb_in = nc.dram_tensor("fb", [D], F32, kind="ExternalInput").ap()
    dstrip_in = nc.dram_tensor("dstrip", [H * 128, 2048], BF16, kind="ExternalInput").ap()
    ostrip_in = nc.dram_tensor("ostrip", [128, OS_OFF[-1] + 896], BF16, kind="ExternalInput").ap()
    ident_in = nc.dram_tensor("ident", [128, 128], BF16, kind="ExternalInput").ap()
    ktab_in = nc.dram_tensor("ktab", [128, NT], F32, kind="ExternalInput").ap()

    y_out = nc.dram_tensor("y", [TLOC, D], F32, kind="ExternalOutput").ap()
    dbg = {}
    for name, shape, dt_ in debug:
        dbg[name] = nc.dram_tensor(name, shape, dt_, kind="ExternalOutput").ap()

    qd = nc.dram_tensor("qd", [D, TLOC], BF16)
    contrib = nc.dram_tensor("contrib", [4 * D, 512], BF16)
    gathered = nc.dram_tensor("gathered", [NCORE * 4 * D, 512], BF16,
                              addr_space="Shared")
    rbuf = nc.dram_tensor("rbuf", [NCORE * 4 * D, 512], BF16)
    wdq = nc.dram_tensor("wdq", [D, D], BF16)
    xp_d = nc.dram_tensor("xp_d", [TLOC, D], F32)

    gv = gathered.ap()
    gw = gathered.ap().rearrange("(n w) c -> n w c", w=8)
    cw = contrib.ap().rearrange("(n w) c -> n w c", w=8)

    with ExitStack() as ctx:
        e = ctx.enter_context
        arena = e(nc.sbuf_tensor("arena", [128, 32 * TLOC], BF16))
        wbuf = [e(nc.sbuf_tensor(f"wbuf{i}", [128, 4096], BF16)) for i in range(2)]
        xt = e(nc.sbuf_tensor("xt", [128, D], F32))
        lnt = e(nc.sbuf_tensor("lnt", [128, D], BF16))
        evb = [e(nc.sbuf_tensor(f"evb{i}", [128, TLOC], BF16)) for i in range(2)]
        vst = e(nc.sbuf_tensor("vst", [128, TLOC], BF16))
        stats = e(nc.sbuf_tensor("stats", [128, 8], F32))
        bq_sb = e(nc.sbuf_tensor("bq_sb", [128, 96], F32))
        bqq_sb = e(nc.sbuf_tensor("bqq_sb", [128, 32], F32))
        bo_sb = e(nc.sbuf_tensor("bo_sb", [128, 32], F32))
        fb_sb = e(nc.sbuf_tensor("fb_sb", [128, 32], F32))
        wsc_sb = e(nc.sbuf_tensor("wsc_sb", [128, 32], F32))
        wzp_sb = e(nc.sbuf_tensor("wzp_sb", [128, 32], F32))
        nzs_sb = e(nc.sbuf_tensor("nzs_sb", [128, 32], F32))
        idn = e(nc.sbuf_tensor("idn", [128, 128], BF16))
        onesb = e(nc.sbuf_tensor("onesb", [128, 1], BF16))
        onesf = e(nc.sbuf_tensor("onesf", [1, 128], F32))
        eps_sb = e(nc.sbuf_tensor("eps_sb", [128, 1], F32))
        ktab = e(nc.sbuf_tensor("ktab_sb", [128, NT], F32))
        osb = [e(nc.sbuf_tensor(f"osb{i}", [128, 896], BF16)) for i in range(2)]
        dsb = [e(nc.sbuf_tensor(f"dsb{i}", [128, 2048], BF16)) for i in range(2)]
        qb = [e(nc.sbuf_tensor(f"qb{i}", [128, TLOC], BF16)) for i in range(2)]
        kbb = [e(nc.sbuf_tensor(f"kb{i}", [128, 512], BF16)) for i in range(2)]
        vbb = [e(nc.sbuf_tensor(f"vb{i}", [128, 512], BF16)) for i in range(2)]
        ptb = e(nc.sbuf_tensor("ptb", [128, 4096], BF16))
        recip = e(nc.sbuf_tensor("recip", [1, 512], F32))
        recb = e(nc.sbuf_tensor("recb", [128, 512], F32))
        qwb = e(nc.sbuf_tensor("qwb", [128, 3072], I32))
        uqb = e(nc.sbuf_tensor("uqb", [128, D], I32))
        wdq_sb = e(nc.sbuf_tensor("wdq_sb", [128, D], BF16))
        wf_sb = [e(nc.sbuf_tensor(f"wf_sb{i}", [128, 256], BF16)) for i in range(2)]
        xo_sb = [e(nc.sbuf_tensor(f"xo_sb{i}", [128, TLOC], F32)) for i in range(2)]
        xps = [e(nc.sbuf_tensor(f"xps{i}", [128, TLOC], F32)) for i in range(2)]

        PS1 = e(nc.psum_tensor("PS1", [128, 1024], F32))
        PS2 = e(nc.psum_tensor("PS2", [128, 1024], F32))
        PS3 = e(nc.psum_tensor("PS3", [128, 1024], F32))
        PSB = e(nc.psum_tensor("PSB", [128, 2048], BF16))

        s_x = e(nc.semaphore("s_x"))
        s_w = e(nc.semaphore("s_w"))
        s_kv = e(nc.semaphore("s_kv"))
        s_sp = e(nc.semaphore("s_sp"))
        s_cc = e(nc.semaphore("s_cc"))
        s_pe = e(nc.semaphore("s_pe"))
        s_ac = e(nc.semaphore("s_ac"))
        s_dv = e(nc.semaphore("s_dv"))
        s_kv2 = [e(nc.semaphore(f"s_kva{i}")) for i in range(2)]
        s_qh2 = [e(nc.semaphore(f"s_qh{i}")) for i in range(2)]
        s_os3 = [e(nc.semaphore(f"s_os{i}")) for i in range(2)]
        s_cp = [e(nc.semaphore(f"s_cp{i}")) for i in range(16)]
        block = e(nc.Block())

        C = {"x": 0, "w": 0, "kv": 0, "sp": 0, "cc": 0, "pe": 0, "ac": 0, "dv": 0}
        SEM = {"x": s_x, "w": s_w, "kv": s_kv, "sp": s_sp, "cc": s_cc,
               "pe": s_pe, "ac": s_ac, "dv": s_dv}
        LM = {}

        def inc(instr, key, n=1):
            instr.then_inc(SEM[key], n)
            C[key] += n
            return C[key]

        def ln_arena(tt, k0, nk=1):
            return arena[:, 4096 * tt + 128 * k0: 4096 * tt + 128 * (k0 + nk)]

        arena_v = arena.ap().rearrange("p (t x) -> p t x", x=4096)

        def ln_rhs(kc, half):
            return arena_v[:, 4 * half:4 * half + 4, 128 * kc:128 * (kc + 1)]

        def ao_arena(h, c0, n):
            return arena[:, 1024 * h + c0: 1024 * h + c0 + n]

        @block.gpsimd
        def _(g):
            g.dma_start(out=idn[:, :], in_=ident_in).then_inc(s_w, 16)
            g.dma_start(out=bq_sb[:, :], in_=bqkv_in.rearrange("(o p) -> p o", p=128),
                        allow_slow_non_contiguous=True).then_inc(s_w, 16)
            g.dma_start(out=bo_sb[:, :], in_=bout_in.rearrange("(o p) -> p o", p=128),
                        allow_slow_non_contiguous=True).then_inc(s_w, 16)
            g.dma_start(out=fb_sb[:, :], in_=fb_in.rearrange("(o p) -> p o", p=128),
                        allow_slow_non_contiguous=True).then_inc(s_w, 16)
            g.dma_start(out=wsc_sb[:, :], in_=wsc_in.rearrange("(o p) -> p o", p=128),
                        allow_slow_non_contiguous=True).then_inc(s_w, 16)
            g.dma_start(out=wzp_sb[:, :], in_=wzp_in.rearrange("(o p) -> p o", p=128),
                        allow_slow_non_contiguous=True).then_inc(s_w, 16)
            g.dma_start(out=ktab[:, :], in_=ktab_in).then_inc(s_w, 16)
            C["w"] += 7 * 16
        LM["w_const"] = C["w"]

        @block.vector
        def _(v):
            v.wait_ge(s_w, LM["w_const"])
            v.memset(onesb[:, :], 1.0)
            v.memset(onesf[:, :], 1.0)
            v.memset(eps_sb[:, :], EPS)
            v.tensor_scalar(bqq_sb[:, :], bq_sb[:, 0:32], QKV_SCALE, 0.0, ALU.mult, ALU.add)
            inc(v.scalar_tensor_tensor(nzs_sb[:, :], wzp_sb[:, :], -1.0, wsc_sb[:, :],
                                       ALU.mult, ALU.mult), "dv")
        LM["dv_const"] = C["dv"]

        def ln_stats_dve(v, xbuf, tag):
            ac0 = LM[f"ac0_{tag}"]
            dv0 = LM[f"dv0_{tag}"]
            inc(v.tensor_scalar(ptb[:, :], xbuf[:, :], 0.0, 0.0, ALU.add, ALU.add,
                                accum_out=stats[:, 0:1]), "dv")
            v.wait_ge(s_ac, ac0 + 1)
            v.wait_ge(s_dv, dv0 + 1)
            inc(v.tensor_scalar(stats[:, 2:3], stats[:, 0:1], 1.0 / D, 0.0,
                                ALU.mult, ALU.add), "dv")
            v.wait_ge(s_dv, dv0 + 2)
            inc(v.scalar_tensor_tensor(stats[:, 3:4], stats[:, 2:3], -1.0, stats[:, 2:3],
                                       ALU.mult, ALU.mult), "dv")
            v.wait_ge(s_dv, dv0 + 3)
            inc(v.scalar_tensor_tensor(stats[:, 4:5], stats[:, 1:2], 1.0 / D, stats[:, 3:4],
                                       ALU.mult, ALU.add), "dv")
            v.wait_ge(s_ac, ac0 + 3)
            inc(v.scalar_tensor_tensor(stats[:, 7:8], stats[:, 2:3], -1.0, stats[:, 6:7],
                                       ALU.mult, ALU.mult), "dv")
            assert C["dv"] == dv0 + 5

        def ln_acts(s, xbuf, tag, prev_lnT_lm):
            dv0 = LM[f"dv0_{tag}"]
            ac0 = LM[f"ac0_{tag}"]
            if prev_lnT_lm:
                s.wait_ge(s_pe, prev_lnT_lm)
            inc(s.activation(lnt[:, :], xbuf[:, :], AF.Square, accum_out=stats[:, 1:2]), "ac")
            s.wait_ge(s_dv, dv0 + 4)
            inc(s.activation(stats[:, 5:6], stats[:, 4:5], AF.Ln, bias=eps_sb[:, 0:1]), "ac")
            s.wait_ge(s_ac, ac0 + 2)
            inc(s.activation(stats[:, 6:7], stats[:, 5:6], AF.Exp, scale=-0.5), "ac")
            s.wait_ge(s_dv, dv0 + 5)
            inc(s.activation(lnt[:, :], xbuf[:, :], AF.Identity,
                             scale=stats[:, 6:7], bias=stats[:, 7:8]), "ac")
            assert C["ac"] == ac0 + 4
            LM[f"ac_ln_{tag}"] = C["ac"]

        def emit_ln_phase(phase, src_ap, extra_first_wait=None):
            for tt in range(8):
                tag = f"{phase}t{tt}"
                LM[f"ac0_{tag}"] = C["ac"]
                LM[f"dv0_{tag}"] = C["dv"]
                for q in range(4):
                    LM[f"dv_lnTev_{tag}_{q}"] = LM[f"dv0_{tag}"] + 6 + q

                @block.sync
                def _(sy, tt=tt, tag=tag):
                    if phase == "l2" and tt == 0:
                        sy.wait_ge(s_sp, LM["sp_xp_done"])
                    if tt >= 1:
                        sy.wait_ge(s_ac, LM[f"ac_ln_{phase}t{tt - 1}"])
                    sy.dma_start(out=xt[:, :],
                                 in_=src_ap[128 * tt:128 * (tt + 1), :]).then_inc(s_x, 16)
                    C["x"] += 16
                LM[f"x_{tag}"] = C["x"]

                @block.vector
                def _(v, tt=tt, tag=tag):
                    v.wait_ge(s_x, LM[f"x_{tag}"])
                    if extra_first_wait and tt == 0:
                        v.wait_ge(*extra_first_wait)
                    ln_stats_dve(v, xt, tag)

                @block.scalar
                def _(s, tt=tt, tag=tag):
                    s.wait_ge(s_x, LM[f"x_{tag}"])
                    prev = LM.get(f"pe_lnT_{phase}t{tt - 1}") if tt >= 1 else None
                    ln_acts(s, xt, tag, prev)

                @block.tensor
                def _(t, tt=tt, tag=tag):
                    t.wait_ge(s_ac, LM[f"ac_ln_{tag}"])
                    for q in range(4):
                        if q >= 2:
                            prev_ev = LM.get(f"dv_lnTev_{tag}_{q - 2}", 0)
                        else:
                            prev_ev = LM.get(f"dv_lnTev_{phase}t{tt - 1}_{q + 2}",
                                             LM.get("dv_xp_31", 0) if phase == "l2" else 0)
                        t.wait_ge(s_dv, prev_ev)
                        last = None
                        for k in range(8):
                            kp = 8 * q + k
                            if phase == "l1":
                                src = lnt[:, 128 * kp:128 * (kp + 1)]
                            else:
                                jj, i8 = kp // 8, kp % 8
                                src = lnt[:, 512 * i8 + jj:512 * (i8 + 1):4]
                            last = t.transpose(PSB[:, 1024 * (q % 2) + 128 * k:1024 * (q % 2) + 128 * (k + 1)],
                                               src, idn[:, :])
                        inc(last, "pe")
                        LM[f"pe_lnT_{tag}_{q}"] = C["pe"]
                    LM[f"pe_lnT_{tag}"] = C["pe"]

                @block.vector
                def _(v, tt=tt, tag=tag):
                    if phase == "l2" and tt == 0:
                        v.wait_ge(s_pe, LM["pe_op_done"])
                    for q in range(4):
                        v.wait_ge(s_pe, LM[f"pe_lnT_{tag}_{q}"])
                        inc(v.tensor_copy(ln_arena(tt, 8 * q, 8),
                                          PSB[:, 1024 * (q % 2):1024 * (q % 2 + 1)]), "dv")
                        assert C["dv"] == LM[f"dv_lnTev_{tag}_{q}"]
            LM[f"dv_{phase}T_done"] = C["dv"]

        emit_ln_phase("l1", x_in)

        def emit_wload(key, src, prev_mm_lm, cc_wait=False):
            @block.gpsimd
            def _(g):
                if cc_wait:
                    g.wait_ge(s_cc, LM["cc_ag"])
                if prev_mm_lm:
                    g.wait_ge(s_pe, prev_mm_lm)
                g.dma_start(out=wbuf[key % 2].ap().rearrange("p (kc o) -> p kc o", o=128),
                            in_=src.rearrange("(kc p) o -> p kc o", p=128)).then_inc(s_w, 16)
                C["w"] += 16
            LM[f"w_chunk_{key}"] = C["w"]

        projm = {}

        def emit_proj_mm(key, psum, prev_evac, rhs_fn):
            @block.tensor
            def _(t):
                t.wait_ge(s_w, LM[f"w_chunk_{key}"])
                t.wait_ge(s_dv, LM["dv_l1T_done"])
                if prev_evac:
                    t.wait_ge(s_ac, prev_evac)
                last = None
                for kc in range(32):
                    for half in range(2):
                        last = t.matmul(psum[:, 512 * half:512 * (half + 1)],
                                        wbuf[key % 2][:, 128 * kc:128 * (kc + 1)],
                                        rhs_fn(kc, half),
                                        start=(kc == 0), stop=(kc == 31))
                inc(last, "pe")
            projm[key % 2] = C["pe"]
            LM[f"pe_proj_{key}"] = C["pe"]

        for oc in range(32, 64):
            slot = oc % 2
            emit_wload(oc, wqkvT[:, 128 * oc:128 * (oc + 1)], projm.get(slot))
            emit_proj_mm(oc, PS1 if slot == 0 else PS2,
                         LM.get(f"ac_evac_{oc - 2}"), ln_rhs)

            @block.scalar
            def _(s, oc=oc, slot=slot):
                s.wait_ge(s_pe, LM[f"pe_proj_{oc}"])
                if oc - 2 >= 32:
                    s.wait_ge(s_sp, LM[f"sp_K_{oc - 2}"])
                inc(s.activation(evb[slot][:, :], (PS1 if slot == 0 else PS2)[:, :],
                                 AF.Identity, bias=bq_sb[:, oc:oc + 1]), "ac")
            LM[f"ac_evac_{oc}"] = C["ac"]

            @block.sync
            def _(sy, oc=oc, slot=slot):
                sy.wait_ge(s_ac, LM[f"ac_evac_{oc}"])
                op = oc - 32
                sy.dma_start(out=contrib.ap()[128 * op:128 * (op + 1), :],
                             in_=evb[slot][:, 0:512]).then_inc(s_sp, 16)
                sy.dma_start(out=contrib.ap()[2 * D + 128 * op:2 * D + 128 * (op + 1), :],
                             in_=evb[slot][:, 512:1024]).then_inc(s_sp, 16)
                C["sp"] += 32
            LM[f"sp_K_{oc}"] = C["sp"]

        for oc in range(64, 96):
            slot = oc % 2
            emit_wload(oc, wqkvT[:, 128 * oc:128 * (oc + 1)], projm.get(slot))
            emit_proj_mm(oc, PS1 if slot == 0 else PS2,
                         LM.get(f"ac_evac_{oc - 2}"), ln_rhs)

            @block.scalar
            def _(s, oc=oc, slot=slot):
                s.wait_ge(s_pe, LM[f"pe_proj_{oc}"])
                if oc - 2 >= 64:
                    s.wait_ge(s_pe, LM[f"pe_vT_{oc - 2}"])
                elif oc - 2 >= 32:
                    s.wait_ge(s_sp, LM[f"sp_K_{oc - 2}"])
                inc(s.activation(evb[slot][:, :], (PS1 if slot == 0 else PS2)[:, :],
                                 AF.Identity, bias=bq_sb[:, oc:oc + 1]), "ac")
            LM[f"ac_evac_{oc}"] = C["ac"]

            @block.tensor
            def _(t, oc=oc, slot=slot):
                t.wait_ge(s_ac, LM[f"ac_evac_{oc}"])
                for half in range(2):
                    t.wait_ge(s_dv, LM.get(f"dv_vT_{oc - 1}_{half}", 0))
                    last = None
                    for k in range(4):
                        tt = 4 * half + k
                        last = t.transpose(PSB[:, 1024 * half + 128 * k:1024 * half + 128 * (k + 1)],
                                           evb[slot][:, 128 * tt:128 * (tt + 1)], idn[:, :])
                    inc(last, "pe")
                    LM[f"pe_vT_{oc}_{half}"] = C["pe"]
                LM[f"pe_vT_{oc}"] = C["pe"]

            @block.vector
            def _(v, oc=oc):
                for half in range(2):
                    v.wait_ge(s_pe, LM[f"pe_vT_{oc}_{half}"])
                    v.wait_ge(s_sp, LM.get(f"sp_V_{oc - 1}_{half}", 0))
                    inc(v.tensor_copy(vst[:, 512 * half:512 * (half + 1)],
                                      PSB[:, 1024 * half:1024 * half + 512]), "dv")
                    LM[f"dv_vT_{oc}_{half}"] = C["dv"]

            @block.sync
            def _(sy, oc=oc):
                op = oc - 64
                for half in range(2):
                    sy.wait_ge(s_dv, LM[f"dv_vT_{oc}_{half}"])
                    for k in range(4):
                        tt = 4 * half + k
                        blk = 1 if tt < 4 else 3
                        trow = 128 * (tt % 4)
                        sy.dma_start(
                            out=cw[512 * blk + trow:512 * blk + trow + 128,
                                   op // 4, 128 * (op % 4):128 * (op % 4) + 128],
                            in_=vst[:, 512 * half + 128 * k:512 * half + 128 * (k + 1)],
                        ).then_inc(s_sp, 16)
                        C["sp"] += 16
                    LM[f"sp_V_{oc}_{half}"] = C["sp"]
        LM["sp_contrib_done"] = C["sp"]



        for oc in range(0, 32):
            slot = oc % 2
            emit_wload(200 + oc, wqkvT[:, 128 * oc:128 * (oc + 1)], projm.get(slot))
            prev_evac = LM.get(f"ac_evac_q{oc - 2}") if oc >= 2 else LM[f"ac_evac_{94 + oc}"]
            emit_proj_mm(200 + oc, PS1 if slot == 0 else PS2, prev_evac, ln_rhs)
            LM[f"pe_proj_q{oc}"] = LM[f"pe_proj_{200 + oc}"]

            @block.scalar
            def _(s, oc=oc, slot=slot):
                s.wait_ge(s_pe, LM[f"pe_proj_q{oc}"])
                if oc >= 2:
                    s.wait_ge(s_sp, LM[f"sp_Q_{oc - 2}"])
                else:
                    s.wait_ge(s_pe, LM[f"pe_vT_{94 + oc}"])
                inc(s.activation(evb[slot][:, :], (PS1 if slot == 0 else PS2)[:, :],
                                 AF.Identity, scale=QKV_SCALE, bias=bqq_sb[:, oc:oc + 1]), "ac")
            LM[f"ac_evac_q{oc}"] = C["ac"]

            @block.sync
            def _(sy, oc=oc, slot=slot):
                sy.wait_ge(s_ac, LM[f"ac_evac_q{oc}"])
                sy.dma_start(out=qd.ap()[128 * oc:128 * (oc + 1), :],
                             in_=evb[slot][:, :]).then_inc(s_sp, 16)
                C["sp"] += 16
            LM[f"sp_Q_{oc}"] = C["sp"]
        LM["sp_Q_done"] = C["sp"]
        LM["pe_q_done"] = C["pe"]

        if not skip_ag:
            @block.gpsimd
            def _(g):
                g.wait_ge(s_sp, LM["sp_contrib_done"])
                inc(g.collective_compute(
                    "AllGather", ALU.bypass,
                    replica_groups=[list(range(NCORE))],
                    ins=[contrib.ap()[0:8192, :]],
                    outs=[gathered.ap()[0:65536, :]],
                ), "cc")
                inc(g.collective_compute(
                    "AllGather", ALU.bypass,
                    replica_groups=[list(range(NCORE))],
                    ins=[contrib.ap()[8192:16384, :]],
                    outs=[gathered.ap()[65536:131072, :]],
                ), "cc")
        else:
            @block.gpsimd
            def _(g):
                g.wait_ge(s_sp, LM["sp_contrib_done"])
                inc(g.nop(), "cc")
        LM["cc_ag"] = C["cc"]

        for n in ([] if skip_dq else range(32)):
            @block.sync
            def _(sy, n=n):
                if n >= 1:
                    sy.wait_ge(s_dv, LM[f"dv_uq_{n - 1}"])
                sy.dma_start(out=qwb[:, :],
                             in_=qw_in[128 * n:128 * (n + 1), :]).then_inc(s_x, 16)
                C["x"] += 16
            LM[f"x_qw_{n}"] = C["x"]

            @block.vector
            def _(v, n=n):
                v.wait_ge(s_x, LM[f"x_qw_{n}"])
                if n >= 1:
                    v.wait_ge(s_ac, LM[f"ac_dq_{n - 1}"])
                v.tensor_single_scalar(uqb[:, 0:1024], qwb[:, 0:1024], 63, ALU.bitwise_and)
                v.tensor_single_scalar(uqb[:, 1024:2048], qwb[:, 0:1024], 6, ALU.logical_shift_right)
                v.tensor_single_scalar(uqb[:, 2048:3072], qwb[:, 1024:2048], 63, ALU.bitwise_and)
                inc(v.tensor_single_scalar(uqb[:, 3072:4096], qwb[:, 2048:3072], 63,
                                           ALU.bitwise_and), "dv")
            LM[f"dv_uq_{n}"] = C["dv"]

            @block.scalar
            def _(s, n=n):
                s.wait_ge(s_dv, LM[f"dv_uq_{n}"])
                if n >= 1:
                    s.wait_ge(s_sp, LM[f"sp_dq_{n - 1}"])
                inc(s.activation(wdq_sb[:, :], uqb[:, :], AF.Identity,
                                 scale=wsc_sb[:, n:n + 1], bias=nzs_sb[:, n:n + 1]), "ac")
            LM[f"ac_dq_{n}"] = C["ac"]

            @block.sync
            def _(sy, n=n):
                sy.wait_ge(s_ac, LM[f"ac_dq_{n}"])
                sy.dma_start(out=wdq.ap()[128 * n:128 * (n + 1), :],
                             in_=wdq_sb[:, :]).then_inc(s_sp, 16)
                C["sp"] += 16
            LM[f"sp_dq_{n}"] = C["sp"]
        LM["sp_dq_done"] = C["sp"]
        if skip_dq:
            LM[f"ac_dq_done"] = C["ac"]

        if stop_after_q:
            if dbg:
                @block.sync
                def _(sy):
                    sy.wait_ge(s_sp, LM["sp_Q_done"])
                    if "d_qd" in dbg:
                        sy.dma_start(out=dbg["d_qd"], in_=qd.ap()).then_inc(s_sp, 16)
                        C["sp"] += 16
                    if "d_gath" in dbg:
                        sy.wait_ge(s_cc, LM["cc_ag"])
                        sy.dma_start(out=dbg["d_gath"], in_=gathered.ap()).then_inc(s_sp, 16)
                        C["sp"] += 16
                    if "d_wdq" in dbg:
                        sy.wait_ge(s_sp, LM["sp_dq_done"])
                        sy.dma_start(out=dbg["d_wdq"], in_=wdq.ap()).then_inc(s_sp, 16)
                        C["sp"] += 16

            @block.sync
            def _(sy):
                sy.wait_ge(s_sp, C["sp"])
                sy.wait_ge(s_cc, LM["cc_ag"])
            return nc

        CPORD = [0, 8, 1, 9, 2, 10, 3, 11, 4, 12, 5, 13, 6, 14, 7, 15]
        CPK = {t: k for k, t in enumerate(CPORD)}

        SCH = {}
        pes, acs, dvs = C["pe"], C["ac"], C["dv"]
        pe_attn_h = {}
        dv_attn_h = {}
        ring_lm = [0, 0]
        KV_TOT = [0, 0]
        QH_TOT = [0, 0]
        OS_TOT = [0, 0]
        OS_HIST = []
        gctr = 0
        prev_pe_mm_end = LM["pe_q_done"]
        prev_ac_end = C["ac"]
        prev_dv_norm = 0
        HORD = list(range(H - 1, -1, -1))
        pe_attn_pos = {}
        dv_attn_pos = {}
        for hi, h in enumerate(HORD):
            QH_TOT[hi % 2] += 32
            qh_wait = QH_TOT[hi % 2]
            bias_ok = BIAS_OK[h]
            for role in range(2):
                base, Tp = COLS[(h, role)]
                ngrp = Tp // 4
                npair = Tp // 2
                ndve = 4 if bias_ok else Tp
                d = {"base": base, "Tp": Tp, "ngrp": ngrp, "npair": npair,
                     "bias_ok": bias_ok, "ndve": ndve,
                     "qh_wait": qh_wait, "prev_pe_mm_end": prev_pe_mm_end,
                     "prev_ac_end": prev_ac_end, "prev_dv_norm": prev_dv_norm}
                dv_pt = {i: dvs + i + 1 for i in range(ndve)}
                kv_wait = {}
                ring_w = {}
                ring_rs = {}
                os_wait = {}
                for j in range(ngrp):
                    st = j % 2
                    KV_TOT[st] += 32
                    kv_wait[j] = KV_TOT[st]
                    if j >= 1 and not bias_ok:
                        rs = gctr % 2
                        ring_rs[j] = rs
                        ring_w[j] = ring_lm[rs]
                        OS_TOT[rs] += 16
                        os_wait[j] = OS_TOT[rs]
                        ring_lm[rs] = dv_pt[min(4 * j + 3, Tp - 1)]
                        gctr += 1
                pe_S = {}
                pe_lav = {}
                for i in range(Tp):
                    pes += 1
                    pe_S[i] = pes
                    if i >= 2:
                        pes += 1
                        pe_lav[i - 2] = pes
                for iv in (Tp - 2, Tp - 1):
                    pes += 1
                    pe_lav[iv] = pes
                d.update(kv_wait=kv_wait, ring_w=ring_w, ring_rs=ring_rs,
                         os_wait=os_wait,
                         pe_S=pe_S, pe_lav=pe_lav, pe_mm_end=pes,
                         ac_exp={i: acs + i + 1 for i in range(Tp)},
                         ac_end=acs + Tp, dv_pt=dv_pt,
                         dv_recip=dvs + ndve + 1, dv_recb=dvs + ndve + 2,
                         dv_norm=dvs + ndve + 3,
                         pe_bcast=pes + 1)
                if not bias_ok:
                    d["dve_rs"] = {i: ring_rs[i // 4] for i in range(4, Tp)}
                acs += Tp
                dvs += ndve + 3
                pes += 1
                prev_pe_mm_end = d["pe_mm_end"]
                prev_ac_end = d["ac_end"]
                prev_dv_norm = d["dv_norm"]
                SCH[(h, role)] = d
            pe_attn_pos[hi] = pes
            dv_attn_pos[hi] = dvs

        def emit_copy(eng, pid, k, t):
            off = ((pid + (16 - t)) % 16) * 4194304
            eng.dma_start(out=rbuf.ap()[8192 * t:8192 * (t + 1), :],
                          in_=AP(gathered, off, [[512, 8192], [1, 512]])
                          ).then_inc(s_cp[k], 16)

        first_need = {}
        pos = 0
        for hi, h in enumerate(HORD):
            for role in range(2):
                d0 = SCH[(h, role)]
                for j in range(d0["ngrp"]):
                    t0_ = (j + 8 * role) & 15
                    first_need.setdefault(t0_, pos)
                pos += 1
        place_at = {}
        for k in range(6, 16):
            t0_ = CPORD[k]
            place_at.setdefault(max(first_need.get(t0_, 0) - 2, 0), []).append(k)

        @block.sync
        def _(sy):
            sy.wait_ge(s_cc, LM["cc_ag"])
            sy.wait_ge(s_sp, LM["sp_Q_done"])
            pid_s = sy.partition_id()
            for k in range(6):
                if k >= 2:
                    sy.wait_ge(s_cp[k - 2], 16)
                emit_copy(sy, pid_s, k, CPORD[k])
            for hi, h in enumerate(HORD):
                if hi >= 2:
                    sy.wait_ge(s_pe, pe_attn_pos[hi - 2])
                    sy.wait_ge(s_dv, dv_attn_pos[hi - 2])
                sl_ = hi % 2
                sy.dma_start(out=qb[sl_][:, :],
                             in_=qd.ap()[128 * h:128 * (h + 1), :]).then_inc(s_qh2[sl_], 16)
                sy.dma_start(out=dsb[sl_][:, :],
                             in_=dstrip_in[128 * h:128 * (h + 1), :]).then_inc(s_qh2[sl_], 16)
                for role in range(2):
                    d = SCH[(h, role)]
                    for j in range(d["ngrp"]):
                        gslot = j % 2
                        st = j % 2
                        if j >= 2:
                            sy.wait_ge(s_pe, d["pe_lav"][4 * (j - 2) + 3])
                        else:
                            sy.wait_ge(s_pe, d["prev_pe_mm_end"])
                        t_rel = (j + 8 * role) & 15
                        sy.wait_ge(s_cp[CPK[t_rel]], 16)
                        krow = 8192 * t_rel + 128 * h
                        sy.dma_start(out=kbb[gslot][:, :],
                                     in_=rbuf.ap()[krow:krow + 128, :]
                                     ).then_inc(s_kv2[st], 16)
                        if j >= 1 and not d["bias_ok"]:
                            rs = d["ring_rs"][j]
                            sy.wait_ge(s_dv, d["ring_w"][j])
                            off = OS_OFF[h] + 512 * (j - 1)
                            sy.dma_start(out=osb[rs][:, :],
                                         in_=ostrip_in[:, off:off + 896]
                                         ).then_inc(s_os3[rs], 16)

        for hi, h in enumerate(HORD):
            for role in range(2):
                d = SCH[(h, role)]
                kjit = place_at.get(2 * hi + role, [])
                if kjit:
                    @block.scalar
                    def _(s, kjit=kjit):
                        pid_a = s.partition_id()
                        for k in kjit:
                            emit_copy(s, pid_a, k, CPORD[k])
                base, Tp = d["base"], d["Tp"]
                pe_S, pe_lav = d["pe_S"], d["pe_lav"]
                ac_exp, dv_pt = d["ac_exp"], d["dv_pt"]

                @block.tensor
                def _(t, h=h, hi=hi, role=role, d=d, Tp=Tp, pe_S=pe_S, pe_lav=pe_lav,
                      ac_exp=ac_exp, dv_pt=dv_pt):
                    t.wait_ge(s_qh2[hi % 2], d["qh_wait"])
                    t.wait_ge(s_dv, d["prev_dv_norm"])

                    def lav(iv):
                        if iv in dv_pt:
                            t.wait_ge(s_dv, dv_pt[iv])
                        else:
                            t.wait_ge(s_ac, ac_exp[iv])
                        jv = iv // 4
                        mv = 3 - (iv % 4)
                        col = 512 * (iv % 8)
                        t.matmul(PS2[0:1, 512:1024], onesb[:, :],
                                 ptb[:, col:col + 512],
                                 start=(iv == 0), stop=(iv == Tp - 1))
                        inc(t.matmul(PS2[:, 0:512], vbb[jv % 2][:, 128 * mv:128 * (mv + 1)],
                                     ptb[:, col:col + 512],
                                     start=(iv == 0), stop=(iv == Tp - 1)), "pe")

                    for i in range(Tp):
                        j = i // 4
                        m = 3 - (i % 4)
                        pair = i // 2
                        sp_ = PS1 if pair % 2 == 0 else PS3
                        if i % 4 == 0:
                            t.wait_ge(s_kv2[j % 2], d["kv_wait"][j])
                        if i % 2 == 0:
                            if pair >= 2:
                                t.wait_ge(s_ac, ac_exp[2 * pair - 3])
                            else:
                                t.wait_ge(s_ac, d["prev_ac_end"])
                        inc(t.matmul(sp_[:, 512 * (i % 2):512 * (i % 2 + 1)],
                                     kbb[j % 2][:, 128 * m:128 * (m + 1)],
                                     qb[hi % 2][:, 512 * role:512 * (role + 1)],
                                     start=True, stop=True), "pe")
                        if i >= 2:
                            lav(i - 2)
                    lav(Tp - 2)
                    lav(Tp - 1)
                    assert C["pe"] == d["pe_mm_end"], (C["pe"], d["pe_mm_end"])

                @block.scalar
                def _(s, h=h, role=role, d=d, base=base, pe_S=pe_S, pe_lav=pe_lav):
                    def vload(j):
                        gslot = j % 2
                        if j >= 2:
                            s.wait_ge(s_pe, pe_lav[4 * (j - 2) + 3])
                        else:
                            s.wait_ge(s_pe, d["prev_pe_mm_end"])
                        t_rel = (j + 8 * role) & 15
                        s.wait_ge(s_cp[CPK[t_rel]], 16)
                        vbase = 8192 * t_rel * 512 + 2097152 + 128 * h
                        s.dma_start(out=vbb[gslot].ap().rearrange("p (m d) -> p m d", d=128),
                                    in_=AP(rbuf, vbase, [[4096, 128], [524288, 4], [1, 128]])
                                    ).then_inc(s_kv2[j % 2], 16)
                    vload(0)
                    if d["ngrp"] >= 2:
                        vload(1)
                    for i in range(d["Tp"]):
                        pair = i // 2
                        if i % 4 == 0 and i >= 4 and i // 4 + 1 < d["ngrp"]:
                            vload(i // 4 + 1)
                        s.wait_ge(s_pe, pe_S[i])
                        if i >= 8:
                            s.wait_ge(s_pe, pe_lav[i - 8])
                        else:
                            s.wait_ge(s_pe, d["prev_pe_mm_end"])
                        sp_ = PS1 if pair % 2 == 0 else PS3
                        col = 512 * (i % 8)
                        pcol = sp_[:, 512 * (i % 2):512 * (i % 2 + 1)]
                        if d["bias_ok"] and i >= 4:
                            inc(s.activation(ptb[:, col:col + 512], pcol, AF.Exp,
                                             bias=ktab[:, base + i:base + i + 1]), "ac")
                        else:
                            inc(s.activation(ptb[:, col:col + 512], pcol, AF.Exp), "ac")

                @block.vector
                def _(v, hi=hi, d=d, base=base, ac_exp=ac_exp):
                    for i in range(d["ndve"]):
                        v.wait_ge(s_ac, ac_exp[i])
                        col = 512 * (i % 8)
                        if d["bias_ok"]:
                            strip = dsb[hi % 2][:, 512 * i:512 * i + 512]
                            inc(v.tensor_tensor(ptb[:, col:col + 512],
                                                ptb[:, col:col + 512],
                                                strip, ALU.mult), "dv")
                            continue
                        if i < 4:
                            strip = dsb[hi % 2][:, 128 * i:128 * i + 512]
                        else:
                            j = i // 4
                            rs = d["dve_rs"][i]
                            v.wait_ge(s_os3[rs], d["os_wait"][j])
                            loc = 128 * (i - 4) - 512 * (j - 1)
                            strip = osb[rs][:, loc:loc + 512]
                        inc(v.scalar_tensor_tensor(ptb[:, col:col + 512],
                                                   ptb[:, col:col + 512],
                                                   ktab[:, base + i:base + i + 1],
                                                   strip, ALU.mult, ALU.mult), "dv")

                @block.vector
                def _(v, d=d, Tp=Tp, pe_lav=pe_lav):
                    v.wait_ge(s_pe, pe_lav[Tp - 1])
                    inc(v.reciprocal(recip[:, :], PS2[0:1, 512:1024]), "dv")

                @block.tensor
                def _(t, d=d):
                    t.wait_ge(s_dv, d["dv_recip"])
                    inc(t.matmul(PS2[:, 512:1024], onesf[:, :], recip[:, :],
                                 start=True, stop=True), "pe")

                @block.scalar
                def _(s, h=h, role=role, d=d):
                    if "d_l" in dbg:
                        s.wait_ge(s_pe, d["pe_bcast"])
                        s.dma_start(out=dbg["d_l"][2 * h + role:2 * h + role + 1, :],
                                    in_=recip[:, :]).then_inc(s_sp, 16)
                        C["sp"] += 16

                @block.vector
                def _(v, h=h, hi=hi, role=role, d=d):
                    v.wait_ge(s_pe, d["pe_bcast"])
                    inc(v.tensor_copy(recb[:, :], PS2[:, 512:1024]), "dv")
                    v.wait_ge(s_dv, d["dv_recb"])
                    if hi == 0 and role == 0:
                        v.wait_ge(s_pe, LM["pe_q_done"])
                    inc(v.scalar_tensor_tensor(ao_arena(h, 512 * role, 512),
                                               PS2[:, 0:512], 1.0, recb[:, :],
                                               ALU.mult, ALU.mult), "dv")

                assert C["pe"] == d["pe_bcast"], (C["pe"], d["pe_bcast"])
                assert C["ac"] == d["ac_end"] and C["dv"] == d["dv_norm"]
            LM[f"pe_attn_p{hi}"] = C["pe"]
            LM[f"dv_attn_p{hi}"] = C["dv"]
        LM["attn_done_dv"] = C["dv"]

        if stop_after_attn:
            @block.sync
            def _(sy):
                sy.wait_ge(s_dv, LM["attn_done_dv"])
                if "d_rb" in dbg:
                    sy.dma_start(out=dbg["d_rb"], in_=rbuf.ap()).then_inc(s_sp, 16)
                    C["sp"] += 16
                if "d_ao" in dbg:
                    sy.dma_start(out=dbg["d_ao"], in_=arena[:, :]).then_inc(s_sp, 16)
                    C["sp"] += 16
                sy.wait_ge(s_sp, C["sp"])
            return nc

        LM["ac_attn_end"] = C["ac"]
        for oc in range(32):
            slot = oc % 2
            emit_wload(300 + oc, woutT[:, 128 * oc:128 * (oc + 1)], projm.get(slot),
                       cc_wait=(oc < 2))

            @block.sync
            def _(sy, oc=oc, slot=slot):
                if oc >= 2:
                    sy.wait_ge(s_sp, LM[f"sp_xp_{oc - 2}"])
                sy.dma_start(out=xo_sb[slot].ap().rearrange("p (tt o) -> p tt o", o=128),
                             in_=x_in[:, 128 * oc:128 * (oc + 1)]
                             .rearrange("(tt p) o -> p tt o", p=128)).then_inc(s_x, 16)
                C["x"] += 16
            LM[f"x_xo_{oc}"] = C["x"]

            @block.tensor
            def _(t, oc=oc, slot=slot):
                t.wait_ge(s_w, LM[f"w_chunk_{300 + oc}"])
                t.wait_ge(s_dv, LM["attn_done_dv"])
                if oc >= 2:
                    t.wait_ge(s_ac, LM[f"ac_op_{oc - 2}"])
                else:
                    t.wait_ge(s_ac, LM["ac_attn_end"])
                psum = PS1 if slot == 0 else PS2
                last = None
                for dc in range(32):
                    for half in range(2):
                        last = t.matmul(psum[:, 512 * half:512 * (half + 1)],
                                        wbuf[slot][:, 128 * dc:128 * (dc + 1)],
                                        arena[:, 1024 * dc + 512 * half:1024 * dc + 512 * (half + 1)],
                                        start=(dc == 0), stop=(dc == 31))
                inc(last, "pe")
            projm[slot] = C["pe"]
            LM[f"pe_op_{oc}"] = C["pe"]

            @block.scalar
            def _(s, oc=oc, slot=slot):
                s.wait_ge(s_pe, LM[f"pe_op_{oc}"])
                if oc >= 2:
                    s.wait_ge(s_pe, LM[f"pe_opT_{oc - 2}"])
                inc(s.activation(evb[slot][:, :], (PS1 if slot == 0 else PS2)[:, :],
                                 AF.Identity, bias=bo_sb[:, oc:oc + 1]), "ac")
            LM[f"ac_op_{oc}"] = C["ac"]

            @block.tensor
            def _(t, oc=oc, slot=slot):
                t.wait_ge(s_ac, LM[f"ac_op_{oc}"])
                for half in range(2):
                    t.wait_ge(s_dv, LM.get(f"dv_xp_{oc - 1}_{half}", LM["attn_done_dv"]))
                    last = None
                    for k in range(4):
                        tt = 4 * half + k
                        last = t.transpose(PSB[:, 1024 * half + 128 * k:1024 * half + 128 * (k + 1)],
                                           evb[slot][:, 128 * tt:128 * (tt + 1)], idn[:, :])
                    inc(last, "pe")
                    LM[f"pe_opT_{oc}_{half}"] = C["pe"]
                LM[f"pe_opT_{oc}"] = C["pe"]

            @block.vector
            def _(v, oc=oc, slot=slot):
                v.wait_ge(s_x, LM[f"x_xo_{oc}"])
                if oc >= 2:
                    v.wait_ge(s_sp, LM[f"sp_xp_{oc - 2}"])
                for half in range(2):
                    v.wait_ge(s_pe, LM[f"pe_opT_{oc}_{half}"])
                    inc(v.tensor_tensor(xps[slot][:, 512 * half:512 * (half + 1)],
                                        PSB[:, 1024 * half:1024 * half + 512],
                                        xo_sb[slot][:, 512 * half:512 * (half + 1)],
                                        ALU.add), "dv")
                    LM[f"dv_xp_{oc}_{half}"] = C["dv"]
                LM[f"dv_xp_{oc}"] = C["dv"]

            @block.sync
            def _(sy, oc=oc, slot=slot):
                sy.wait_ge(s_dv, LM[f"dv_xp_{oc}"])
                sy.dma_start(out=xp_d.ap()[:, 128 * oc:128 * (oc + 1)]
                             .rearrange("(tt p) o -> p tt o", p=128),
                             in_=xps[slot].ap().rearrange("p (tt o) -> p tt o", o=128)).then_inc(s_sp, 16)
                C["sp"] += 16
            LM[f"sp_xp_{oc}"] = C["sp"]
        LM["sp_xp_done"] = C["sp"]
        LM["pe_op_done"] = LM["pe_opT_31"]

        emit_ln_phase("l2", xp_d.ap())

        for ngi in range(16):
            gslot = ngi % 2
            pe0f = C["pe"]
            pe_ffn = {kp: pe0f + kp + 1 for kp in range(32)}

            @block.sync
            def _(sy, ngi=ngi, gslot=gslot):
                for ncc in range(2):
                    n = 2 * ngi + ncc
                    dst = xo_sb[gslot] if ncc == 0 else xps[gslot]
                    if ngi >= 2:
                        sy.wait_ge(s_sp, LM[f"sp_y_{2 * (ngi - 2) + ncc}"])
                    else:
                        sy.wait_ge(s_sp, LM["sp_xp_done"])
                    sy.dma_start(out=dst.ap().rearrange("p (tt o) -> p tt o", o=128),
                                 in_=xp_d.ap()[:, 128 * n:128 * (n + 1)]
                                 .rearrange("(tt p) o -> p tt o", p=128)).then_inc(s_x, 16)
                    C["x"] += 16
            LM[f"x_ffn_{ngi}"] = C["x"]

            @block.sync
            def _(sy, ngi=ngi):
                if ngi == 0:
                    sy.wait_ge(s_sp, LM["sp_dq_done"])
                for kp in range(32):
                    wslot = kp % 2
                    if kp >= 2:
                        sy.wait_ge(s_pe, pe_ffn[kp - 2])
                    elif ngi >= 1:
                        sy.wait_ge(s_pe, LM[f"pe_ffn_{ngi - 1}_{30 + kp}"])
                    sy.dma_start_transpose(wf_sb[wslot][:, :],
                                           wdq.ap()[256 * ngi:256 * (ngi + 1),
                                                    128 * kp:128 * (kp + 1)]).then_inc(s_kv2[wslot], 16)
                    KV_TOT[wslot] += 16
                    LM[f"kv_wf_{ngi}_{kp}"] = KV_TOT[wslot]

            @block.tensor
            def _(t, ngi=ngi):
                t.wait_ge(s_dv, LM["dv_l2T_done"])
                t.wait_ge(s_sp, LM["sp_dq_done"])
                if ngi >= 1:
                    t.wait_ge(s_ac, LM[f"ac_ffn_{ngi - 1}"])
                else:
                    t.wait_ge(s_ac, LM["ac_op_31"])
                for kp in range(32):
                    wslot = kp % 2
                    t.wait_ge(s_kv2[wslot], LM[f"kv_wf_{ngi}_{kp}"])
                    last = None
                    for ncc in range(2):
                        psum = PS1 if ncc == 0 else PS2
                        for half in range(2):
                            last = t.matmul(psum[:, 512 * half:512 * (half + 1)],
                                            wf_sb[wslot][:, 128 * ncc:128 * (ncc + 1)],
                                            arena_v[:, 4 * half:4 * half + 4, 128 * kp:128 * (kp + 1)],
                                            start=(kp == 0), stop=(kp == 31))
                    inc(last, "pe")
                    LM[f"pe_ffn_{ngi}_{kp}"] = C["pe"]
            assert C["pe"] == pe0f + 32

            for ncc in range(2):
                n = 2 * ngi + ncc
                eslot = n % 2

                @block.scalar
                def _(s, ngi=ngi, ncc=ncc, n=n, eslot=eslot):
                    s.wait_ge(s_pe, LM[f"pe_ffn_{ngi}_31"])
                    if n >= 2:
                        s.wait_ge(s_pe, LM[f"pe_yT_{n - 2}"])
                    inc(s.activation(evb[eslot][:, :], (PS1 if ncc == 0 else PS2)[:, :],
                                     AF.Identity, bias=fb_sb[:, n:n + 1]), "ac")
                LM[f"ac_ffn_n{n}"] = C["ac"]
                if ncc == 1:
                    LM[f"ac_ffn_{ngi}"] = C["ac"]

                @block.tensor
                def _(t, n=n, eslot=eslot):
                    t.wait_ge(s_ac, LM[f"ac_ffn_n{n}"])
                    for half in range(2):
                        t.wait_ge(s_dv, LM.get(f"dv_y_{n - 1}_{half}", LM["dv_l2T_done"]))
                        last = None
                        for k in range(4):
                            tt = 4 * half + k
                            last = t.transpose(PSB[:, 1024 * half + 128 * k:1024 * half + 128 * (k + 1)],
                                               evb[eslot][:, 128 * tt:128 * (tt + 1)], idn[:, :])
                        inc(last, "pe")
                        LM[f"pe_yT_{n}_{half}"] = C["pe"]
                    LM[f"pe_yT_{n}"] = C["pe"]

                @block.vector
                def _(v, ngi=ngi, ncc=ncc, n=n, gslot=gslot):
                    v.wait_ge(s_x, LM[f"x_ffn_{ngi}"])
                    buf = xo_sb[gslot] if ncc == 0 else xps[gslot]
                    for half in range(2):
                        v.wait_ge(s_pe, LM[f"pe_yT_{n}_{half}"])
                        inc(v.tensor_tensor(buf[:, 512 * half:512 * (half + 1)],
                                            PSB[:, 1024 * half:1024 * half + 512],
                                            buf[:, 512 * half:512 * (half + 1)],
                                            ALU.add), "dv")
                        LM[f"dv_y_{n}_{half}"] = C["dv"]
                    LM[f"dv_y_n{n}"] = C["dv"]

                @block.sync
                def _(sy, n=n, ncc=ncc, gslot=gslot):
                    sy.wait_ge(s_dv, LM[f"dv_y_n{n}"])
                    src = xo_sb[gslot] if ncc == 0 else xps[gslot]
                    sy.dma_start(out=y_out[:, 128 * n:128 * (n + 1)]
                                 .rearrange("(tt p) o -> p tt o", p=128),
                                 in_=src.ap().rearrange("p (tt o) -> p tt o", o=128)).then_inc(s_sp, 16)
                    C["sp"] += 16
                LM[f"sp_y_{n}"] = C["sp"]
        LM["sp_y_done"] = C["sp"]

        if dbg:
            @block.sync
            def _(sy):
                sy.wait_ge(s_sp, LM["sp_y_done"])
                if "d_arena" in dbg:
                    sy.dma_start(out=dbg["d_arena"], in_=arena[:, :]).then_inc(s_sp, 16)
                    C["sp"] += 16
                if "d_xp" in dbg:
                    sy.dma_start(out=dbg["d_xp"], in_=xp_d.ap()).then_inc(s_sp, 16)
                    C["sp"] += 16
                if "d_qd" in dbg:
                    sy.dma_start(out=dbg["d_qd"], in_=qd.ap()).then_inc(s_sp, 16)
                    C["sp"] += 16
                if "d_gath" in dbg:
                    sy.dma_start(out=dbg["d_gath"], in_=gathered.ap()).then_inc(s_sp, 16)
                    C["sp"] += 16
                if "d_wdq" in dbg:
                    sy.dma_start(out=dbg["d_wdq"], in_=wdq.ap()).then_inc(s_sp, 16)
                    C["sp"] += 16

        @block.sync
        def _(sy):
            sy.wait_ge(s_sp, C["sp"])
    return nc


def make_in_maps():
    dstrip, ostrip, ident = host_tables()
    return dstrip, ostrip, ident


def kernel(L0_attn_x, L0_attn_w_qkv, L0_attn_b_qkv, L0_attn_w_out, L0_attn_b_out,
           L0_ffn_qweight, L0_ffn_weight_scale, L0_ffn_weight_zero_point, L0_ffn_bias):
    x = np.ascontiguousarray(np.asarray(L0_attn_x, dtype=np.float32).reshape(S, D))
    wqkvT_a = np.ascontiguousarray(np.asarray(L0_attn_w_qkv, dtype=np.float32).T)
    bqkv = np.ascontiguousarray(np.asarray(L0_attn_b_qkv, dtype=np.float32))
    woutT_a = np.ascontiguousarray(np.asarray(L0_attn_w_out, dtype=np.float32).T)
    bout = np.ascontiguousarray(np.asarray(L0_attn_b_out, dtype=np.float32))
    qw = np.asarray(L0_ffn_qweight, dtype=np.int32).reshape(D, 1024, 3)
    qw_r = np.ascontiguousarray(qw.transpose(0, 2, 1).reshape(D, 3072))
    wsc = np.ascontiguousarray(np.asarray(L0_ffn_weight_scale, dtype=np.float32).reshape(D))
    wzp = np.ascontiguousarray(np.asarray(L0_ffn_weight_zero_point, dtype=np.float32).reshape(D))
    fb = np.ascontiguousarray(np.asarray(L0_ffn_bias, dtype=np.float32))
    dstrip, ostrip, ident = host_tables()

    nc = build()
    in_maps = []
    for c in range(NCORE):
        xc = np.ascontiguousarray(
            np.concatenate([x[512 * c:512 * (c + 1)], x[512 * (c + 8):512 * (c + 9)]], axis=0))
        in_maps.append({
            "x_c": xc, "wqkvT": wqkvT_a, "bqkv": bqkv, "woutT": woutT_a, "bout": bout,
            "qw_r": qw_r, "wsc": wsc, "wzp": wzp, "fb": fb,
            "dstrip": dstrip, "ostrip": ostrip, "ident": ident,
            "ktab": kill_table(c),
        })
    res = run_bass_kernel_spmd(nc, in_maps, core_ids=list(range(NCORE)))
    out = np.empty((S, D), dtype=np.float32)
    for c in range(NCORE):
        yc = res.results[c]["y"]
        out[512 * c:512 * (c + 1)] = yc[0:512]
        out[512 * (c + 8):512 * (c + 9)] = yc[512:1024]
    return out.reshape(1, S, D)


# version 09
# speedup vs baseline: 1.0855x; 1.0131x over previous
import math
import numpy as np
import ml_dtypes
import concourse.bass as bass
import concourse.mybir as mybir
from concourse.expressions_rust import RuntimeValue
from concourse.ap import AP
from concourse.bass_utils import run_bass_kernel_spmd
from contextlib import ExitStack

F32 = mybir.dt.float32
BF16 = mybir.dt.bfloat16
I32 = mybir.dt.int32
AF = mybir.ActivationFunctionType
ALU = mybir.AluOpType
ds = bass.ds

H = 32
D = 4096
S = 8192
NCORE = 8
TLOC = 1024
CUT = 96.0
EPS = 1e-5
QKV_SCALE = 1.0 / math.sqrt(128.0)


def _slopes():
    m = np.arange(1, H + 1, dtype=np.float64) * (8.0 / H)
    return 0.5 ** m


def _plan():
    sl = _slopes()
    w = np.floor((511.0 + CUT / sl) / 128.0).astype(np.int64) + 1
    w = np.maximum(w, 5)
    t_lo = np.minimum(32, w)
    t_hi = np.minimum(64, w)
    cols = {}
    nt = 0
    for h in range(H):
        for role in range(2):
            T = int(t_lo[h] if role == 0 else t_hi[h])
            Tp = ((T + 3) // 4) * 4
            cols[(h, role)] = (nt, Tp)
            nt += Tp
    return sl, t_lo, t_hi, cols, nt


SL, T_LO, T_HI, COLS, NT = _plan()
OSW = [max(128 * (int(T_HI[h]) - 5) + 512, 512) for h in range(H)]
OS_OFF = np.cumsum([0] + OSW).tolist()
BIAS_OK = [bool(SL[h] * 511.0 <= 80.0) for h in range(H)]


def host_tables():
    sl = SL
    p = np.arange(128.0)[:, None]
    c = np.arange(896.0)[None, :]
    t = p - c + 384.0
    dstrip = np.zeros((H * 128, 2048), dtype=ml_dtypes.bfloat16)
    for h in range(H):
        if BIAS_OK[h]:
            c5 = np.arange(512.0)[None, :]
            for i in range(4):
                koff = p + 384.0 - 128.0 * i
                v = np.where(koff <= c5, np.exp(sl[h] * (koff - 511.0)), 0.0)
                dstrip[128 * h:128 * (h + 1), 512 * i:512 * (i + 1)] = \
                    v.astype(ml_dtypes.bfloat16)
        else:
            v = np.where(t <= 0, np.exp(sl[h] * t), 0.0)
            dstrip[128 * h:128 * (h + 1), 0:896] = v.astype(ml_dtypes.bfloat16)
    ostrip = np.zeros((128, OS_OFF[-1] + 896), dtype=ml_dtypes.bfloat16)
    for h in range(H):
        if not BIAS_OK[h]:
            cc = np.arange(float(OSW[h]))[None, :]
            v = np.exp(sl[h] * (p - cc - 128.0))
            ostrip[:, OS_OFF[h]:OS_OFF[h + 1]] = v.astype(ml_dtypes.bfloat16)
    ident = np.eye(128, dtype=ml_dtypes.bfloat16)
    return dstrip, ostrip, ident


def kill_table(core):
    kt = np.zeros((128, NT), dtype=np.float32)
    p = np.arange(128.0)
    for h in range(H):
        for role in range(2):
            base, Tp = COLS[(h, role)]
            B = core if role == 0 else core + 8
            T = int(T_LO[h] if role == 0 else T_HI[h])
            live = min(4 * (B + 1), T)
            if BIAS_OK[h]:
                kt[:, base:base + Tp] = -300.0
                for i in range(4, live):
                    kt[:, base + i] = SL[h] * (p + 384.0 - 128.0 * i - 511.0)
            else:
                kt[:, base:base + live] = 1.0
    return kt


def build(debug=(), stop_after_q=False, skip_dq=False, skip_ag=False, stop_after_attn=False):
    nc = bass.Bass("TRN2", target_bir_lowering=False)

    x_in = nc.dram_tensor("x_c", [TLOC, D], F32, kind="ExternalInput").ap()
    wqkvT = nc.dram_tensor("wqkvT", [D, 3 * D], F32, kind="ExternalInput").ap()
    bqkv_in = nc.dram_tensor("bqkv", [3 * D], F32, kind="ExternalInput").ap()
    woutT = nc.dram_tensor("woutT", [D, D], F32, kind="ExternalInput").ap()
    bout_in = nc.dram_tensor("bout", [D], F32, kind="ExternalInput").ap()
    qw_in = nc.dram_tensor("qw_r", [D, 3072], I32, kind="ExternalInput").ap()
    wsc_in = nc.dram_tensor("wsc", [D], F32, kind="ExternalInput").ap()
    wzp_in = nc.dram_tensor("wzp", [D], F32, kind="ExternalInput").ap()
    fb_in = nc.dram_tensor("fb", [D], F32, kind="ExternalInput").ap()
    dstrip_in = nc.dram_tensor("dstrip", [H * 128, 2048], BF16, kind="ExternalInput").ap()
    ostrip_in = nc.dram_tensor("ostrip", [128, OS_OFF[-1] + 896], BF16, kind="ExternalInput").ap()
    ident_in = nc.dram_tensor("ident", [128, 128], BF16, kind="ExternalInput").ap()
    ktab_in = nc.dram_tensor("ktab", [128, NT], F32, kind="ExternalInput").ap()

    y_out = nc.dram_tensor("y", [TLOC, D], F32, kind="ExternalOutput").ap()
    dbg = {}
    for name, shape, dt_ in debug:
        dbg[name] = nc.dram_tensor(name, shape, dt_, kind="ExternalOutput").ap()

    qd = nc.dram_tensor("qd", [D, TLOC], BF16)
    contrib = nc.dram_tensor("contrib", [4 * D, 512], BF16)
    gathered = nc.dram_tensor("gathered", [NCORE * 4 * D, 512], BF16,
                              addr_space="Shared")
    rbuf = nc.dram_tensor("rbuf", [NCORE * 4 * D, 512], BF16)
    wdq = nc.dram_tensor("wdq", [D, D], BF16)
    xp_d = nc.dram_tensor("xp_d", [TLOC, D], F32)

    gv = gathered.ap()
    gw = gathered.ap().rearrange("(n w) c -> n w c", w=8)
    cw = contrib.ap().rearrange("(n w) c -> n w c", w=8)

    with ExitStack() as ctx:
        e = ctx.enter_context
        arena = e(nc.sbuf_tensor("arena", [128, 32 * TLOC], BF16))
        wbuf = [e(nc.sbuf_tensor(f"wbuf{i}", [128, 4096], BF16)) for i in range(2)]
        xt = e(nc.sbuf_tensor("xt", [128, D], F32))
        lnt = e(nc.sbuf_tensor("lnt", [128, D], BF16))
        evb = [e(nc.sbuf_tensor(f"evb{i}", [128, TLOC], BF16)) for i in range(2)]
        vst = e(nc.sbuf_tensor("vst", [128, TLOC], BF16))
        stats = e(nc.sbuf_tensor("stats", [128, 8], F32))
        bq_sb = e(nc.sbuf_tensor("bq_sb", [128, 96], F32))
        bqq_sb = e(nc.sbuf_tensor("bqq_sb", [128, 32], F32))
        bo_sb = e(nc.sbuf_tensor("bo_sb", [128, 32], F32))
        fb_sb = e(nc.sbuf_tensor("fb_sb", [128, 32], F32))
        wsc_sb = e(nc.sbuf_tensor("wsc_sb", [128, 32], F32))
        wzp_sb = e(nc.sbuf_tensor("wzp_sb", [128, 32], F32))
        nzs_sb = e(nc.sbuf_tensor("nzs_sb", [128, 32], F32))
        idn = e(nc.sbuf_tensor("idn", [128, 128], BF16))
        onesb = e(nc.sbuf_tensor("onesb", [128, 1], BF16))
        onesf = e(nc.sbuf_tensor("onesf", [1, 128], F32))
        eps_sb = e(nc.sbuf_tensor("eps_sb", [128, 1], F32))
        ktab = e(nc.sbuf_tensor("ktab_sb", [128, NT], F32))
        osb = [e(nc.sbuf_tensor(f"osb{i}", [128, 896], BF16)) for i in range(2)]
        dsb = [e(nc.sbuf_tensor(f"dsb{i}", [128, 2048], BF16)) for i in range(2)]
        qb = [e(nc.sbuf_tensor(f"qb{i}", [128, TLOC], BF16)) for i in range(2)]
        kbb = [e(nc.sbuf_tensor(f"kb{i}", [128, 512], BF16)) for i in range(2)]
        vbb = [e(nc.sbuf_tensor(f"vb{i}", [128, 512], BF16)) for i in range(2)]
        ptb = e(nc.sbuf_tensor("ptb", [128, 4096], BF16))
        recip = e(nc.sbuf_tensor("recip", [1, 512], F32))
        recb = e(nc.sbuf_tensor("recb", [128, 512], F32))
        qwb = e(nc.sbuf_tensor("qwb", [128, 3072], I32))
        uqb = e(nc.sbuf_tensor("uqb", [128, D], I32))
        wdq_sb = e(nc.sbuf_tensor("wdq_sb", [128, D], BF16))
        wf_sb = [e(nc.sbuf_tensor(f"wf_sb{i}", [128, 256], BF16)) for i in range(2)]
        xo_sb = [e(nc.sbuf_tensor(f"xo_sb{i}", [128, TLOC], F32)) for i in range(2)]
        xps = [e(nc.sbuf_tensor(f"xps{i}", [128, TLOC], F32)) for i in range(2)]

        PS1 = e(nc.psum_tensor("PS1", [128, 1024], F32))
        PS2 = e(nc.psum_tensor("PS2", [128, 1024], F32))
        PS3 = e(nc.psum_tensor("PS3", [128, 1024], F32))
        PSB = e(nc.psum_tensor("PSB", [128, 2048], BF16))

        s_x = e(nc.semaphore("s_x"))
        s_w = e(nc.semaphore("s_w"))
        s_kv = e(nc.semaphore("s_kv"))
        s_sp = e(nc.semaphore("s_sp"))
        s_cc = e(nc.semaphore("s_cc"))
        s_pe = e(nc.semaphore("s_pe"))
        s_ac = e(nc.semaphore("s_ac"))
        s_dv = e(nc.semaphore("s_dv"))
        s_kv2 = [e(nc.semaphore(f"s_kva{i}")) for i in range(2)]
        s_qh2 = [e(nc.semaphore(f"s_qh{i}")) for i in range(2)]
        s_os3 = [e(nc.semaphore(f"s_os{i}")) for i in range(2)]
        s_cp = [e(nc.semaphore(f"s_cp{i}")) for i in range(16)]
        block = e(nc.Block())

        C = {"x": 0, "w": 0, "kv": 0, "sp": 0, "cc": 0, "pe": 0, "ac": 0, "dv": 0}
        SEM = {"x": s_x, "w": s_w, "kv": s_kv, "sp": s_sp, "cc": s_cc,
               "pe": s_pe, "ac": s_ac, "dv": s_dv}
        LM = {}

        def inc(instr, key, n=1):
            instr.then_inc(SEM[key], n)
            C[key] += n
            return C[key]

        def ln_arena(tt, k0, nk=1):
            return arena[:, 4096 * tt + 128 * k0: 4096 * tt + 128 * (k0 + nk)]

        arena_v = arena.ap().rearrange("p (t x) -> p t x", x=4096)

        def ln_rhs(kc, half):
            return arena_v[:, 4 * half:4 * half + 4, 128 * kc:128 * (kc + 1)]

        def ao_arena(h, c0, n):
            return arena[:, 1024 * h + c0: 1024 * h + c0 + n]

        @block.gpsimd
        def _(g):
            g.dma_start(out=idn[:, :], in_=ident_in).then_inc(s_w, 16)
            g.dma_start(out=bq_sb[:, :], in_=bqkv_in.rearrange("(o p) -> p o", p=128),
                        allow_slow_non_contiguous=True).then_inc(s_w, 16)
            g.dma_start(out=bo_sb[:, :], in_=bout_in.rearrange("(o p) -> p o", p=128),
                        allow_slow_non_contiguous=True).then_inc(s_w, 16)
            g.dma_start(out=fb_sb[:, :], in_=fb_in.rearrange("(o p) -> p o", p=128),
                        allow_slow_non_contiguous=True).then_inc(s_w, 16)
            g.dma_start(out=wsc_sb[:, :], in_=wsc_in.rearrange("(o p) -> p o", p=128),
                        allow_slow_non_contiguous=True).then_inc(s_w, 16)
            g.dma_start(out=wzp_sb[:, :], in_=wzp_in.rearrange("(o p) -> p o", p=128),
                        allow_slow_non_contiguous=True).then_inc(s_w, 16)
            g.dma_start(out=ktab[:, :], in_=ktab_in).then_inc(s_w, 16)
            C["w"] += 7 * 16
        LM["w_const"] = C["w"]

        @block.vector
        def _(v):
            v.wait_ge(s_w, LM["w_const"])
            v.memset(onesb[:, :], 1.0)
            v.memset(onesf[:, :], 1.0)
            v.memset(eps_sb[:, :], EPS)
            v.tensor_scalar(bqq_sb[:, :], bq_sb[:, 0:32], QKV_SCALE, 0.0, ALU.mult, ALU.add)
            inc(v.scalar_tensor_tensor(nzs_sb[:, :], wzp_sb[:, :], -1.0, wsc_sb[:, :],
                                       ALU.mult, ALU.mult), "dv")
        LM["dv_const"] = C["dv"]

        def ln_stats_dve(v, xbuf, tag):
            ac0 = LM[f"ac0_{tag}"]
            dv0 = LM[f"dv0_{tag}"]
            inc(v.tensor_scalar(ptb[:, :], xbuf[:, :], 0.0, 0.0, ALU.add, ALU.add,
                                accum_out=stats[:, 0:1]), "dv")
            v.wait_ge(s_ac, ac0 + 1)
            v.wait_ge(s_dv, dv0 + 1)
            inc(v.tensor_scalar(stats[:, 2:3], stats[:, 0:1], 1.0 / D, 0.0,
                                ALU.mult, ALU.add), "dv")
            v.wait_ge(s_dv, dv0 + 2)
            inc(v.scalar_tensor_tensor(stats[:, 3:4], stats[:, 2:3], -1.0, stats[:, 2:3],
                                       ALU.mult, ALU.mult), "dv")
            v.wait_ge(s_dv, dv0 + 3)
            inc(v.scalar_tensor_tensor(stats[:, 4:5], stats[:, 1:2], 1.0 / D, stats[:, 3:4],
                                       ALU.mult, ALU.add), "dv")
            v.wait_ge(s_ac, ac0 + 3)
            inc(v.scalar_tensor_tensor(stats[:, 7:8], stats[:, 2:3], -1.0, stats[:, 6:7],
                                       ALU.mult, ALU.mult), "dv")
            assert C["dv"] == dv0 + 5

        def ln_acts(s, xbuf, tag, prev_lnT_lm):
            dv0 = LM[f"dv0_{tag}"]
            ac0 = LM[f"ac0_{tag}"]
            if prev_lnT_lm:
                s.wait_ge(s_pe, prev_lnT_lm)
            inc(s.activation(lnt[:, :], xbuf[:, :], AF.Square, accum_out=stats[:, 1:2]), "ac")
            s.wait_ge(s_dv, dv0 + 4)
            inc(s.activation(stats[:, 5:6], stats[:, 4:5], AF.Ln, bias=eps_sb[:, 0:1]), "ac")
            s.wait_ge(s_ac, ac0 + 2)
            inc(s.activation(stats[:, 6:7], stats[:, 5:6], AF.Exp, scale=-0.5), "ac")
            s.wait_ge(s_dv, dv0 + 5)
            inc(s.activation(lnt[:, :], xbuf[:, :], AF.Identity,
                             scale=stats[:, 6:7], bias=stats[:, 7:8]), "ac")
            assert C["ac"] == ac0 + 4
            LM[f"ac_ln_{tag}"] = C["ac"]

        def emit_ln_phase(phase, src_ap, extra_first_wait=None):
            for tt in range(8):
                tag = f"{phase}t{tt}"
                LM[f"ac0_{tag}"] = C["ac"]
                LM[f"dv0_{tag}"] = C["dv"]
                for q in range(4):
                    LM[f"dv_lnTev_{tag}_{q}"] = LM[f"dv0_{tag}"] + 6 + q

                @block.sync
                def _(sy, tt=tt, tag=tag):
                    if phase == "l2" and tt == 0:
                        sy.wait_ge(s_sp, LM["sp_xp_done"])
                    if tt >= 1:
                        sy.wait_ge(s_ac, LM[f"ac_ln_{phase}t{tt - 1}"])
                    sy.dma_start(out=xt[:, :],
                                 in_=src_ap[128 * tt:128 * (tt + 1), :]).then_inc(s_x, 16)
                    C["x"] += 16
                LM[f"x_{tag}"] = C["x"]

                @block.vector
                def _(v, tt=tt, tag=tag):
                    v.wait_ge(s_x, LM[f"x_{tag}"])
                    if extra_first_wait and tt == 0:
                        v.wait_ge(*extra_first_wait)
                    ln_stats_dve(v, xt, tag)

                @block.scalar
                def _(s, tt=tt, tag=tag):
                    s.wait_ge(s_x, LM[f"x_{tag}"])
                    prev = LM.get(f"pe_lnT_{phase}t{tt - 1}") if tt >= 1 else None
                    ln_acts(s, xt, tag, prev)

                @block.tensor
                def _(t, tt=tt, tag=tag):
                    t.wait_ge(s_ac, LM[f"ac_ln_{tag}"])
                    for q in range(4):
                        if q >= 2:
                            prev_ev = LM.get(f"dv_lnTev_{tag}_{q - 2}", 0)
                        else:
                            prev_ev = LM.get(f"dv_lnTev_{phase}t{tt - 1}_{q + 2}",
                                             LM.get("dv_xp_31", 0) if phase == "l2" else 0)
                        t.wait_ge(s_dv, prev_ev)
                        last = None
                        for k in range(8):
                            kp = 8 * q + k
                            if phase == "l1":
                                src = lnt[:, 128 * kp:128 * (kp + 1)]
                            else:
                                jj, i8 = kp // 8, kp % 8
                                src = lnt[:, 512 * i8 + jj:512 * (i8 + 1):4]
                            last = t.transpose(PSB[:, 1024 * (q % 2) + 128 * k:1024 * (q % 2) + 128 * (k + 1)],
                                               src, idn[:, :])
                        inc(last, "pe")
                        LM[f"pe_lnT_{tag}_{q}"] = C["pe"]
                    LM[f"pe_lnT_{tag}"] = C["pe"]

                @block.vector
                def _(v, tt=tt, tag=tag):
                    if phase == "l2" and tt == 0:
                        v.wait_ge(s_pe, LM["pe_op_done"])
                    for q in range(4):
                        v.wait_ge(s_pe, LM[f"pe_lnT_{tag}_{q}"])
                        inc(v.tensor_copy(ln_arena(tt, 8 * q, 8),
                                          PSB[:, 1024 * (q % 2):1024 * (q % 2 + 1)]), "dv")
                        assert C["dv"] == LM[f"dv_lnTev_{tag}_{q}"]
            LM[f"dv_{phase}T_done"] = C["dv"]

        emit_ln_phase("l1", x_in)

        def emit_wload(key, src, prev_mm_lm, cc_wait=False):
            @block.gpsimd
            def _(g):
                if cc_wait:
                    g.wait_ge(s_cc, LM["cc_ag"])
                if prev_mm_lm:
                    g.wait_ge(s_pe, prev_mm_lm)
                g.dma_start(out=wbuf[key % 2].ap().rearrange("p (kc o) -> p kc o", o=128),
                            in_=src.rearrange("(kc p) o -> p kc o", p=128)).then_inc(s_w, 16)
                C["w"] += 16
            LM[f"w_chunk_{key}"] = C["w"]

        projm = {}

        def emit_proj_mm(key, psum, prev_evac, rhs_fn):
            @block.tensor
            def _(t):
                t.wait_ge(s_w, LM[f"w_chunk_{key}"])
                t.wait_ge(s_dv, LM["dv_l1T_done"])
                if prev_evac:
                    t.wait_ge(s_ac, prev_evac)
                last = None
                for kc in range(32):
                    for half in range(2):
                        last = t.matmul(psum[:, 512 * half:512 * (half + 1)],
                                        wbuf[key % 2][:, 128 * kc:128 * (kc + 1)],
                                        rhs_fn(kc, half),
                                        start=(kc == 0), stop=(kc == 31))
                inc(last, "pe")
            projm[key % 2] = C["pe"]
            LM[f"pe_proj_{key}"] = C["pe"]

        for oc in range(32, 64):
            slot = oc % 2
            emit_wload(oc, wqkvT[:, 128 * oc:128 * (oc + 1)], projm.get(slot))
            emit_proj_mm(oc, PS1 if slot == 0 else PS2,
                         LM.get(f"ac_evac_{oc - 2}"), ln_rhs)

            @block.scalar
            def _(s, oc=oc, slot=slot):
                s.wait_ge(s_pe, LM[f"pe_proj_{oc}"])
                if oc - 2 >= 32:
                    s.wait_ge(s_sp, LM[f"sp_K_{oc - 2}"])
                inc(s.activation(evb[slot][:, :], (PS1 if slot == 0 else PS2)[:, :],
                                 AF.Identity, bias=bq_sb[:, oc:oc + 1]), "ac")
            LM[f"ac_evac_{oc}"] = C["ac"]

            @block.sync
            def _(sy, oc=oc, slot=slot):
                sy.wait_ge(s_ac, LM[f"ac_evac_{oc}"])
                op = oc - 32
                sy.dma_start(out=contrib.ap()[128 * op:128 * (op + 1), :],
                             in_=evb[slot][:, 0:512]).then_inc(s_sp, 16)
                sy.dma_start(out=contrib.ap()[2 * D + 128 * op:2 * D + 128 * (op + 1), :],
                             in_=evb[slot][:, 512:1024]).then_inc(s_sp, 16)
                C["sp"] += 32
            LM[f"sp_K_{oc}"] = C["sp"]

        for oc in range(64, 96):
            slot = oc % 2
            emit_wload(oc, wqkvT[:, 128 * oc:128 * (oc + 1)], projm.get(slot))
            emit_proj_mm(oc, PS1 if slot == 0 else PS2,
                         LM.get(f"ac_evac_{oc - 2}"), ln_rhs)

            @block.scalar
            def _(s, oc=oc, slot=slot):
                s.wait_ge(s_pe, LM[f"pe_proj_{oc}"])
                if oc - 2 >= 64:
                    s.wait_ge(s_pe, LM[f"pe_vT_{oc - 2}"])
                elif oc - 2 >= 32:
                    s.wait_ge(s_sp, LM[f"sp_K_{oc - 2}"])
                inc(s.activation(evb[slot][:, :], (PS1 if slot == 0 else PS2)[:, :],
                                 AF.Identity, bias=bq_sb[:, oc:oc + 1]), "ac")
            LM[f"ac_evac_{oc}"] = C["ac"]

            @block.tensor
            def _(t, oc=oc, slot=slot):
                t.wait_ge(s_ac, LM[f"ac_evac_{oc}"])
                for half in range(2):
                    t.wait_ge(s_dv, LM.get(f"dv_vT_{oc - 1}_{half}", 0))
                    last = None
                    for k in range(4):
                        tt = 4 * half + k
                        last = t.transpose(PSB[:, 1024 * half + 128 * k:1024 * half + 128 * (k + 1)],
                                           evb[slot][:, 128 * tt:128 * (tt + 1)], idn[:, :])
                    inc(last, "pe")
                    LM[f"pe_vT_{oc}_{half}"] = C["pe"]
                LM[f"pe_vT_{oc}"] = C["pe"]

            @block.vector
            def _(v, oc=oc):
                for half in range(2):
                    v.wait_ge(s_pe, LM[f"pe_vT_{oc}_{half}"])
                    v.wait_ge(s_sp, LM.get(f"sp_V_{oc - 1}_{half}", 0))
                    inc(v.tensor_copy(vst[:, 512 * half:512 * (half + 1)],
                                      PSB[:, 1024 * half:1024 * half + 512]), "dv")
                    LM[f"dv_vT_{oc}_{half}"] = C["dv"]

            @block.sync
            def _(sy, oc=oc):
                op = oc - 64
                for half in range(2):
                    sy.wait_ge(s_dv, LM[f"dv_vT_{oc}_{half}"])
                    for k in range(4):
                        tt = 4 * half + k
                        blk = 1 if tt < 4 else 3
                        trow = 128 * (tt % 4)
                        sy.dma_start(
                            out=cw[512 * blk + trow:512 * blk + trow + 128,
                                   op // 4, 128 * (op % 4):128 * (op % 4) + 128],
                            in_=vst[:, 512 * half + 128 * k:512 * half + 128 * (k + 1)],
                        ).then_inc(s_sp, 16)
                        C["sp"] += 16
                    LM[f"sp_V_{oc}_{half}"] = C["sp"]
        LM["sp_contrib_done"] = C["sp"]



        for oc in range(0, 32):
            slot = oc % 2
            emit_wload(200 + oc, wqkvT[:, 128 * oc:128 * (oc + 1)], projm.get(slot))
            prev_evac = LM.get(f"ac_evac_q{oc - 2}") if oc >= 2 else LM[f"ac_evac_{94 + oc}"]
            emit_proj_mm(200 + oc, PS1 if slot == 0 else PS2, prev_evac, ln_rhs)
            LM[f"pe_proj_q{oc}"] = LM[f"pe_proj_{200 + oc}"]

            @block.scalar
            def _(s, oc=oc, slot=slot):
                s.wait_ge(s_pe, LM[f"pe_proj_q{oc}"])
                if oc >= 2:
                    s.wait_ge(s_sp, LM[f"sp_Q_{oc - 2}"])
                else:
                    s.wait_ge(s_pe, LM[f"pe_vT_{94 + oc}"])
                inc(s.activation(evb[slot][:, :], (PS1 if slot == 0 else PS2)[:, :],
                                 AF.Identity, scale=QKV_SCALE, bias=bqq_sb[:, oc:oc + 1]), "ac")
            LM[f"ac_evac_q{oc}"] = C["ac"]

            @block.sync
            def _(sy, oc=oc, slot=slot):
                sy.wait_ge(s_ac, LM[f"ac_evac_q{oc}"])
                sy.dma_start(out=qd.ap()[128 * oc:128 * (oc + 1), :],
                             in_=evb[slot][:, :]).then_inc(s_sp, 16)
                C["sp"] += 16
            LM[f"sp_Q_{oc}"] = C["sp"]
        LM["sp_Q_done"] = C["sp"]
        LM["pe_q_done"] = C["pe"]

        if not skip_ag:
            @block.gpsimd
            def _(g):
                g.wait_ge(s_sp, LM["sp_contrib_done"])
                inc(g.collective_compute(
                    "AllGather", ALU.bypass,
                    replica_groups=[list(range(NCORE))],
                    ins=[contrib.ap()[0:8192, :]],
                    outs=[gathered.ap()[0:65536, :]],
                ), "cc")
                inc(g.collective_compute(
                    "AllGather", ALU.bypass,
                    replica_groups=[list(range(NCORE))],
                    ins=[contrib.ap()[8192:16384, :]],
                    outs=[gathered.ap()[65536:131072, :]],
                ), "cc")
        else:
            @block.gpsimd
            def _(g):
                g.wait_ge(s_sp, LM["sp_contrib_done"])
                inc(g.nop(), "cc")
        LM["cc_ag"] = C["cc"]

        for n in ([] if skip_dq else range(32)):
            @block.sync
            def _(sy, n=n):
                if n >= 1:
                    sy.wait_ge(s_dv, LM[f"dv_uq_{n - 1}"])
                sy.dma_start(out=qwb[:, :],
                             in_=qw_in[128 * n:128 * (n + 1), :]).then_inc(s_x, 16)
                C["x"] += 16
            LM[f"x_qw_{n}"] = C["x"]

            @block.vector
            def _(v, n=n):
                v.wait_ge(s_x, LM[f"x_qw_{n}"])
                if n >= 1:
                    v.wait_ge(s_ac, LM[f"ac_dq_{n - 1}"])
                v.tensor_single_scalar(uqb[:, 0:1024], qwb[:, 0:1024], 63, ALU.bitwise_and)
                v.tensor_single_scalar(uqb[:, 1024:2048], qwb[:, 0:1024], 6, ALU.logical_shift_right)
                v.tensor_single_scalar(uqb[:, 2048:3072], qwb[:, 1024:2048], 63, ALU.bitwise_and)
                inc(v.tensor_single_scalar(uqb[:, 3072:4096], qwb[:, 2048:3072], 63,
                                           ALU.bitwise_and), "dv")
            LM[f"dv_uq_{n}"] = C["dv"]

            @block.scalar
            def _(s, n=n):
                s.wait_ge(s_dv, LM[f"dv_uq_{n}"])
                if n >= 1:
                    s.wait_ge(s_sp, LM[f"sp_dq_{n - 1}"])
                inc(s.activation(wdq_sb[:, :], uqb[:, :], AF.Identity,
                                 scale=wsc_sb[:, n:n + 1], bias=nzs_sb[:, n:n + 1]), "ac")
            LM[f"ac_dq_{n}"] = C["ac"]

            @block.sync
            def _(sy, n=n):
                sy.wait_ge(s_ac, LM[f"ac_dq_{n}"])
                sy.dma_start(out=wdq.ap()[128 * n:128 * (n + 1), :],
                             in_=wdq_sb[:, :]).then_inc(s_sp, 16)
                C["sp"] += 16
            LM[f"sp_dq_{n}"] = C["sp"]
        LM["sp_dq_done"] = C["sp"]
        if skip_dq:
            LM[f"ac_dq_done"] = C["ac"]

        if stop_after_q:
            if dbg:
                @block.sync
                def _(sy):
                    sy.wait_ge(s_sp, LM["sp_Q_done"])
                    if "d_qd" in dbg:
                        sy.dma_start(out=dbg["d_qd"], in_=qd.ap()).then_inc(s_sp, 16)
                        C["sp"] += 16
                    if "d_gath" in dbg:
                        sy.wait_ge(s_cc, LM["cc_ag"])
                        sy.dma_start(out=dbg["d_gath"], in_=gathered.ap()).then_inc(s_sp, 16)
                        C["sp"] += 16
                    if "d_wdq" in dbg:
                        sy.wait_ge(s_sp, LM["sp_dq_done"])
                        sy.dma_start(out=dbg["d_wdq"], in_=wdq.ap()).then_inc(s_sp, 16)
                        C["sp"] += 16

            @block.sync
            def _(sy):
                sy.wait_ge(s_sp, C["sp"])
                sy.wait_ge(s_cc, LM["cc_ag"])
            return nc

        CPORD = [0, 8, 1, 9, 2, 10, 3, 11, 4, 12, 5, 13, 6, 14, 7, 15]
        CPK = {t: k for k, t in enumerate(CPORD)}

        SCH = {}
        pes, acs, dvs = C["pe"], C["ac"], C["dv"]
        pe_attn_h = {}
        dv_attn_h = {}
        ring_lm = [0, 0]
        KV_TOT = [0, 0]
        QH_TOT = [0, 0]
        OS_TOT = [0, 0]
        OS_HIST = []
        gctr = 0
        prev_pe_mm_end = LM["pe_q_done"]
        prev_ac_end = C["ac"]
        prev_dv_norm = 0
        HORD = list(range(H - 1, -1, -1))
        pe_attn_pos = {}
        dv_attn_pos = {}
        for hi, h in enumerate(HORD):
            QH_TOT[hi % 2] += 32
            qh_wait = QH_TOT[hi % 2]
            bias_ok = BIAS_OK[h]
            for role in range(2):
                base, Tp = COLS[(h, role)]
                ngrp = Tp // 4
                npair = Tp // 2
                ndve = 4 if bias_ok else Tp
                d = {"base": base, "Tp": Tp, "ngrp": ngrp, "npair": npair,
                     "bias_ok": bias_ok, "ndve": ndve,
                     "qh_wait": qh_wait, "prev_pe_mm_end": prev_pe_mm_end,
                     "prev_ac_end": prev_ac_end, "prev_dv_norm": prev_dv_norm}
                dv_pt = {i: dvs + i + 1 for i in range(ndve)}
                kv_wait = {}
                ring_w = {}
                ring_rs = {}
                os_wait = {}
                for j in range(ngrp):
                    st = j % 2
                    KV_TOT[st] += 32
                    kv_wait[j] = KV_TOT[st]
                    if j >= 1 and not bias_ok:
                        rs = gctr % 2
                        ring_rs[j] = rs
                        ring_w[j] = ring_lm[rs]
                        OS_TOT[rs] += 16
                        os_wait[j] = OS_TOT[rs]
                        ring_lm[rs] = dv_pt[min(4 * j + 3, Tp - 1)]
                        gctr += 1
                pe_S = {}
                pe_lav = {}
                for i in range(Tp):
                    pes += 1
                    pe_S[i] = pes
                    if i >= 2:
                        pes += 1
                        pe_lav[i - 2] = pes
                for iv in (Tp - 2, Tp - 1):
                    pes += 1
                    pe_lav[iv] = pes
                d.update(kv_wait=kv_wait, ring_w=ring_w, ring_rs=ring_rs,
                         os_wait=os_wait,
                         pe_S=pe_S, pe_lav=pe_lav, pe_mm_end=pes,
                         ac_exp={i: acs + i + 1 for i in range(Tp)},
                         ac_end=acs + Tp, dv_pt=dv_pt,
                         dv_recip=dvs + ndve + 1, dv_recb=dvs + ndve + 2,
                         dv_norm=dvs + ndve + 3,
                         pe_bcast=pes + 1)
                if not bias_ok:
                    d["dve_rs"] = {i: ring_rs[i // 4] for i in range(4, Tp)}
                acs += Tp
                dvs += ndve + 3
                pes += 1
                prev_pe_mm_end = d["pe_mm_end"]
                prev_ac_end = d["ac_end"]
                prev_dv_norm = d["dv_norm"]
                SCH[(h, role)] = d
            pe_attn_pos[hi] = pes
            dv_attn_pos[hi] = dvs

        def emit_copy(eng, pid, k, t):
            off = ((pid + (16 - t)) % 16) * 4194304
            eng.dma_start(out=rbuf.ap()[8192 * t:8192 * (t + 1), :],
                          in_=AP(gathered, off, [[512, 8192], [1, 512]])
                          ).then_inc(s_cp[t], 16)

        first_need = {}
        pos = 0
        for hi, h in enumerate(HORD):
            for role in range(2):
                d0 = SCH[(h, role)]
                for j in range(d0["ngrp"]):
                    t0_ = (j + 8 * role) & 15
                    first_need.setdefault(t0_, pos)
                pos += 1
        place_at = {}
        for k in range(6, 16):
            t0_ = CPORD[k]
            place_at.setdefault(max(first_need.get(t0_, 0) - 2, 0), []).append(k)

        @block.sync
        def _(sy):
            sy.wait_ge(s_cc, LM["cc_ag"])
            sy.wait_ge(s_sp, LM["sp_Q_done"])
            pid_s = sy.partition_id()
            SYNC_CP = [1, 9, 2, 10]
            for idx, t_ in enumerate(SYNC_CP):
                if idx >= 2:
                    sy.wait_ge(s_cp[SYNC_CP[idx - 2]], 16)
                emit_copy(sy, pid_s, idx, t_)
            for hi, h in enumerate(HORD):
                if hi >= 2:
                    sy.wait_ge(s_pe, pe_attn_pos[hi - 2])
                    sy.wait_ge(s_dv, dv_attn_pos[hi - 2])
                sl_ = hi % 2
                sy.dma_start(out=qb[sl_][:, :],
                             in_=qd.ap()[128 * h:128 * (h + 1), :]).then_inc(s_qh2[sl_], 16)
                sy.dma_start(out=dsb[sl_][:, :],
                             in_=dstrip_in[128 * h:128 * (h + 1), :]).then_inc(s_qh2[sl_], 16)
                for role in range(2):
                    d = SCH[(h, role)]
                    for j in range(d["ngrp"]):
                        gslot = j % 2
                        st = j % 2
                        if j >= 2:
                            sy.wait_ge(s_pe, d["pe_lav"][4 * (j - 2) + 3])
                        else:
                            sy.wait_ge(s_pe, d["prev_pe_mm_end"])
                        t_rel = (j + 8 * role) & 15
                        if t_rel % 8 == 0:
                            krow = (0 if t_rel == 0 else 8192) + 128 * h
                            sy.dma_start(out=kbb[gslot][:, :],
                                         in_=contrib.ap()[krow:krow + 128, :]
                                         ).then_inc(s_kv2[st], 16)
                        else:
                            sy.wait_ge(s_cp[t_rel], 16)
                            krow = 8192 * t_rel + 128 * h
                            sy.dma_start(out=kbb[gslot][:, :],
                                         in_=rbuf.ap()[krow:krow + 128, :]
                                         ).then_inc(s_kv2[st], 16)
                        if j >= 1 and not d["bias_ok"]:
                            rs = d["ring_rs"][j]
                            sy.wait_ge(s_dv, d["ring_w"][j])
                            off = OS_OFF[h] + 512 * (j - 1)
                            sy.dma_start(out=osb[rs][:, :],
                                         in_=ostrip_in[:, off:off + 896]
                                         ).then_inc(s_os3[rs], 16)

        for hi, h in enumerate(HORD):
            for role in range(2):
                d = SCH[(h, role)]
                kjit = place_at.get(2 * hi + role, [])
                if kjit:
                    @block.scalar
                    def _(s, kjit=kjit):
                        pid_a = s.partition_id()
                        for k in kjit:
                            emit_copy(s, pid_a, k, CPORD[k])
                base, Tp = d["base"], d["Tp"]
                pe_S, pe_lav = d["pe_S"], d["pe_lav"]
                ac_exp, dv_pt = d["ac_exp"], d["dv_pt"]

                @block.tensor
                def _(t, h=h, hi=hi, role=role, d=d, Tp=Tp, pe_S=pe_S, pe_lav=pe_lav,
                      ac_exp=ac_exp, dv_pt=dv_pt):
                    t.wait_ge(s_qh2[hi % 2], d["qh_wait"])
                    t.wait_ge(s_dv, d["prev_dv_norm"])

                    def lav(iv):
                        if iv in dv_pt:
                            t.wait_ge(s_dv, dv_pt[iv])
                        else:
                            t.wait_ge(s_ac, ac_exp[iv])
                        jv = iv // 4
                        mv = 3 - (iv % 4)
                        col = 512 * (iv % 8)
                        t.matmul(PS2[0:1, 512:1024], onesb[:, :],
                                 ptb[:, col:col + 512],
                                 start=(iv == 0), stop=(iv == Tp - 1))
                        inc(t.matmul(PS2[:, 0:512], vbb[jv % 2][:, 128 * mv:128 * (mv + 1)],
                                     ptb[:, col:col + 512],
                                     start=(iv == 0), stop=(iv == Tp - 1)), "pe")

                    for i in range(Tp):
                        j = i // 4
                        m = 3 - (i % 4)
                        pair = i // 2
                        sp_ = PS1 if pair % 2 == 0 else PS3
                        if i % 4 == 0:
                            t.wait_ge(s_kv2[j % 2], d["kv_wait"][j])
                        if i % 2 == 0:
                            if pair >= 2:
                                t.wait_ge(s_ac, ac_exp[2 * pair - 3])
                            else:
                                t.wait_ge(s_ac, d["prev_ac_end"])
                        inc(t.matmul(sp_[:, 512 * (i % 2):512 * (i % 2 + 1)],
                                     kbb[j % 2][:, 128 * m:128 * (m + 1)],
                                     qb[hi % 2][:, 512 * role:512 * (role + 1)],
                                     start=True, stop=True), "pe")
                        if i >= 2:
                            lav(i - 2)
                    lav(Tp - 2)
                    lav(Tp - 1)
                    assert C["pe"] == d["pe_mm_end"], (C["pe"], d["pe_mm_end"])

                @block.scalar
                def _(s, h=h, role=role, d=d, base=base, pe_S=pe_S, pe_lav=pe_lav):
                    def vload(j):
                        gslot = j % 2
                        if j >= 2:
                            s.wait_ge(s_pe, pe_lav[4 * (j - 2) + 3])
                        else:
                            s.wait_ge(s_pe, d["prev_pe_mm_end"])
                        t_rel = (j + 8 * role) & 15
                        if t_rel % 8 == 0:
                            vbase = (2097152 if t_rel == 0 else 6291456) + 128 * h
                            s.dma_start(out=vbb[gslot].ap().rearrange("p (m d) -> p m d", d=128),
                                        in_=AP(contrib, vbase, [[4096, 128], [524288, 4], [1, 128]])
                                        ).then_inc(s_kv2[j % 2], 16)
                        else:
                            s.wait_ge(s_cp[t_rel], 16)
                            vbase = 8192 * t_rel * 512 + 2097152 + 128 * h
                            s.dma_start(out=vbb[gslot].ap().rearrange("p (m d) -> p m d", d=128),
                                        in_=AP(rbuf, vbase, [[4096, 128], [524288, 4], [1, 128]])
                                        ).then_inc(s_kv2[j % 2], 16)
                    vload(0)
                    if d["ngrp"] >= 2:
                        vload(1)
                    for i in range(d["Tp"]):
                        pair = i // 2
                        if i % 4 == 0 and i >= 4 and i // 4 + 1 < d["ngrp"]:
                            vload(i // 4 + 1)
                        s.wait_ge(s_pe, pe_S[i])
                        if i >= 8:
                            s.wait_ge(s_pe, pe_lav[i - 8])
                        else:
                            s.wait_ge(s_pe, d["prev_pe_mm_end"])
                        sp_ = PS1 if pair % 2 == 0 else PS3
                        col = 512 * (i % 8)
                        pcol = sp_[:, 512 * (i % 2):512 * (i % 2 + 1)]
                        if d["bias_ok"] and i >= 4:
                            inc(s.activation(ptb[:, col:col + 512], pcol, AF.Exp,
                                             bias=ktab[:, base + i:base + i + 1]), "ac")
                        else:
                            inc(s.activation(ptb[:, col:col + 512], pcol, AF.Exp), "ac")

                @block.vector
                def _(v, hi=hi, d=d, base=base, ac_exp=ac_exp):
                    for i in range(d["ndve"]):
                        v.wait_ge(s_ac, ac_exp[i])
                        col = 512 * (i % 8)
                        if d["bias_ok"]:
                            strip = dsb[hi % 2][:, 512 * i:512 * i + 512]
                            inc(v.tensor_tensor(ptb[:, col:col + 512],
                                                ptb[:, col:col + 512],
                                                strip, ALU.mult), "dv")
                            continue
                        if i < 4:
                            strip = dsb[hi % 2][:, 128 * i:128 * i + 512]
                        else:
                            j = i // 4
                            rs = d["dve_rs"][i]
                            v.wait_ge(s_os3[rs], d["os_wait"][j])
                            loc = 128 * (i - 4) - 512 * (j - 1)
                            strip = osb[rs][:, loc:loc + 512]
                        inc(v.scalar_tensor_tensor(ptb[:, col:col + 512],
                                                   ptb[:, col:col + 512],
                                                   ktab[:, base + i:base + i + 1],
                                                   strip, ALU.mult, ALU.mult), "dv")

                @block.vector
                def _(v, d=d, Tp=Tp, pe_lav=pe_lav):
                    v.wait_ge(s_pe, pe_lav[Tp - 1])
                    inc(v.reciprocal(recip[:, :], PS2[0:1, 512:1024]), "dv")

                @block.tensor
                def _(t, d=d):
                    t.wait_ge(s_dv, d["dv_recip"])
                    inc(t.matmul(PS2[:, 512:1024], onesf[:, :], recip[:, :],
                                 start=True, stop=True), "pe")

                @block.scalar
                def _(s, h=h, role=role, d=d):
                    if "d_l" in dbg:
                        s.wait_ge(s_pe, d["pe_bcast"])
                        s.dma_start(out=dbg["d_l"][2 * h + role:2 * h + role + 1, :],
                                    in_=recip[:, :]).then_inc(s_sp, 16)
                        C["sp"] += 16

                @block.vector
                def _(v, h=h, hi=hi, role=role, d=d):
                    v.wait_ge(s_pe, d["pe_bcast"])
                    inc(v.tensor_copy(recb[:, :], PS2[:, 512:1024]), "dv")
                    v.wait_ge(s_dv, d["dv_recb"])
                    if hi == 0 and role == 0:
                        v.wait_ge(s_pe, LM["pe_q_done"])
                    inc(v.scalar_tensor_tensor(ao_arena(h, 512 * role, 512),
                                               PS2[:, 0:512], 1.0, recb[:, :],
                                               ALU.mult, ALU.mult), "dv")

                assert C["pe"] == d["pe_bcast"], (C["pe"], d["pe_bcast"])
                assert C["ac"] == d["ac_end"] and C["dv"] == d["dv_norm"]
            LM[f"pe_attn_p{hi}"] = C["pe"]
            LM[f"dv_attn_p{hi}"] = C["dv"]
        LM["attn_done_dv"] = C["dv"]

        if stop_after_attn:
            @block.sync
            def _(sy):
                sy.wait_ge(s_dv, LM["attn_done_dv"])
                if "d_rb" in dbg:
                    sy.dma_start(out=dbg["d_rb"], in_=rbuf.ap()).then_inc(s_sp, 16)
                    C["sp"] += 16
                if "d_ao" in dbg:
                    sy.dma_start(out=dbg["d_ao"], in_=arena[:, :]).then_inc(s_sp, 16)
                    C["sp"] += 16
                sy.wait_ge(s_sp, C["sp"])
            return nc

        LM["ac_attn_end"] = C["ac"]
        for oc in range(32):
            slot = oc % 2
            emit_wload(300 + oc, woutT[:, 128 * oc:128 * (oc + 1)], projm.get(slot),
                       cc_wait=(oc < 2))

            @block.sync
            def _(sy, oc=oc, slot=slot):
                if oc >= 2:
                    sy.wait_ge(s_sp, LM[f"sp_xp_{oc - 2}"])
                sy.dma_start(out=xo_sb[slot].ap().rearrange("p (tt o) -> p tt o", o=128),
                             in_=x_in[:, 128 * oc:128 * (oc + 1)]
                             .rearrange("(tt p) o -> p tt o", p=128)).then_inc(s_x, 16)
                C["x"] += 16
            LM[f"x_xo_{oc}"] = C["x"]

            @block.tensor
            def _(t, oc=oc, slot=slot):
                t.wait_ge(s_w, LM[f"w_chunk_{300 + oc}"])
                t.wait_ge(s_dv, LM["attn_done_dv"])
                if oc >= 2:
                    t.wait_ge(s_ac, LM[f"ac_op_{oc - 2}"])
                else:
                    t.wait_ge(s_ac, LM["ac_attn_end"])
                psum = PS1 if slot == 0 else PS2
                last = None
                for dc in range(32):
                    for half in range(2):
                        last = t.matmul(psum[:, 512 * half:512 * (half + 1)],
                                        wbuf[slot][:, 128 * dc:128 * (dc + 1)],
                                        arena[:, 1024 * dc + 512 * half:1024 * dc + 512 * (half + 1)],
                                        start=(dc == 0), stop=(dc == 31))
                inc(last, "pe")
            projm[slot] = C["pe"]
            LM[f"pe_op_{oc}"] = C["pe"]

            @block.scalar
            def _(s, oc=oc, slot=slot):
                s.wait_ge(s_pe, LM[f"pe_op_{oc}"])
                if oc >= 2:
                    s.wait_ge(s_pe, LM[f"pe_opT_{oc - 2}"])
                inc(s.activation(evb[slot][:, :], (PS1 if slot == 0 else PS2)[:, :],
                                 AF.Identity, bias=bo_sb[:, oc:oc + 1]), "ac")
            LM[f"ac_op_{oc}"] = C["ac"]

            @block.tensor
            def _(t, oc=oc, slot=slot):
                t.wait_ge(s_ac, LM[f"ac_op_{oc}"])
                for half in range(2):
                    t.wait_ge(s_dv, LM.get(f"dv_xp_{oc - 1}_{half}", LM["attn_done_dv"]))
                    last = None
                    for k in range(4):
                        tt = 4 * half + k
                        last = t.transpose(PSB[:, 1024 * half + 128 * k:1024 * half + 128 * (k + 1)],
                                           evb[slot][:, 128 * tt:128 * (tt + 1)], idn[:, :])
                    inc(last, "pe")
                    LM[f"pe_opT_{oc}_{half}"] = C["pe"]
                LM[f"pe_opT_{oc}"] = C["pe"]

            @block.vector
            def _(v, oc=oc, slot=slot):
                v.wait_ge(s_x, LM[f"x_xo_{oc}"])
                if oc >= 2:
                    v.wait_ge(s_sp, LM[f"sp_xp_{oc - 2}"])
                for half in range(2):
                    v.wait_ge(s_pe, LM[f"pe_opT_{oc}_{half}"])
                    inc(v.tensor_tensor(xps[slot][:, 512 * half:512 * (half + 1)],
                                        PSB[:, 1024 * half:1024 * half + 512],
                                        xo_sb[slot][:, 512 * half:512 * (half + 1)],
                                        ALU.add), "dv")
                    LM[f"dv_xp_{oc}_{half}"] = C["dv"]
                LM[f"dv_xp_{oc}"] = C["dv"]

            @block.sync
            def _(sy, oc=oc, slot=slot):
                sy.wait_ge(s_dv, LM[f"dv_xp_{oc}"])
                sy.dma_start(out=xp_d.ap()[:, 128 * oc:128 * (oc + 1)]
                             .rearrange("(tt p) o -> p tt o", p=128),
                             in_=xps[slot].ap().rearrange("p (tt o) -> p tt o", o=128)).then_inc(s_sp, 16)
                C["sp"] += 16
            LM[f"sp_xp_{oc}"] = C["sp"]
        LM["sp_xp_done"] = C["sp"]
        LM["pe_op_done"] = LM["pe_opT_31"]

        emit_ln_phase("l2", xp_d.ap())

        for ngi in range(16):
            gslot = ngi % 2
            pe0f = C["pe"]
            pe_ffn = {kp: pe0f + kp + 1 for kp in range(32)}

            @block.sync
            def _(sy, ngi=ngi, gslot=gslot):
                for ncc in range(2):
                    n = 2 * ngi + ncc
                    dst = xo_sb[gslot] if ncc == 0 else xps[gslot]
                    if ngi >= 2:
                        sy.wait_ge(s_sp, LM[f"sp_y_{2 * (ngi - 2) + ncc}"])
                    else:
                        sy.wait_ge(s_sp, LM["sp_xp_done"])
                    sy.dma_start(out=dst.ap().rearrange("p (tt o) -> p tt o", o=128),
                                 in_=xp_d.ap()[:, 128 * n:128 * (n + 1)]
                                 .rearrange("(tt p) o -> p tt o", p=128)).then_inc(s_x, 16)
                    C["x"] += 16
            LM[f"x_ffn_{ngi}"] = C["x"]

            @block.scalar
            def _(s, ngi=ngi):
                if ngi == 0:
                    s.wait_ge(s_sp, LM["sp_dq_done"])
                for kp in range(32):
                    wslot = kp % 2
                    if kp >= 2:
                        s.wait_ge(s_pe, pe_ffn[kp - 2])
                    elif ngi >= 1:
                        s.wait_ge(s_pe, LM[f"pe_ffn_{ngi - 1}_{30 + kp}"])
                    s.dma_start_transpose(wf_sb[wslot][:, :],
                                          wdq.ap()[256 * ngi:256 * (ngi + 1),
                                                   128 * kp:128 * (kp + 1)]).then_inc(s_kv2[wslot], 16)
                    KV_TOT[wslot] += 16
                    LM[f"kv_wf_{ngi}_{kp}"] = KV_TOT[wslot]

            @block.tensor
            def _(t, ngi=ngi):
                t.wait_ge(s_dv, LM["dv_l2T_done"])
                t.wait_ge(s_sp, LM["sp_dq_done"])
                if ngi >= 1:
                    t.wait_ge(s_ac, LM[f"ac_ffn_{ngi - 1}"])
                else:
                    t.wait_ge(s_ac, LM["ac_op_31"])
                for kp in range(32):
                    wslot = kp % 2
                    t.wait_ge(s_kv2[wslot], LM[f"kv_wf_{ngi}_{kp}"])
                    last = None
                    for ncc in range(2):
                        psum = PS1 if ncc == 0 else PS2
                        for half in range(2):
                            last = t.matmul(psum[:, 512 * half:512 * (half + 1)],
                                            wf_sb[wslot][:, 128 * ncc:128 * (ncc + 1)],
                                            arena_v[:, 4 * half:4 * half + 4, 128 * kp:128 * (kp + 1)],
                                            start=(kp == 0), stop=(kp == 31))
                    inc(last, "pe")
                    LM[f"pe_ffn_{ngi}_{kp}"] = C["pe"]
            assert C["pe"] == pe0f + 32

            for ncc in range(2):
                n = 2 * ngi + ncc
                eslot = n % 2

                @block.scalar
                def _(s, ngi=ngi, ncc=ncc, n=n, eslot=eslot):
                    s.wait_ge(s_pe, LM[f"pe_ffn_{ngi}_31"])
                    if n >= 2:
                        s.wait_ge(s_pe, LM[f"pe_yT_{n - 2}"])
                    inc(s.activation(evb[eslot][:, :], (PS1 if ncc == 0 else PS2)[:, :],
                                     AF.Identity, bias=fb_sb[:, n:n + 1]), "ac")
                LM[f"ac_ffn_n{n}"] = C["ac"]
                if ncc == 1:
                    LM[f"ac_ffn_{ngi}"] = C["ac"]

                @block.tensor
                def _(t, n=n, eslot=eslot):
                    t.wait_ge(s_ac, LM[f"ac_ffn_n{n}"])
                    for half in range(2):
                        t.wait_ge(s_dv, LM.get(f"dv_y_{n - 1}_{half}", LM["dv_l2T_done"]))
                        last = None
                        for k in range(4):
                            tt = 4 * half + k
                            last = t.transpose(PSB[:, 1024 * half + 128 * k:1024 * half + 128 * (k + 1)],
                                               evb[eslot][:, 128 * tt:128 * (tt + 1)], idn[:, :])
                        inc(last, "pe")
                        LM[f"pe_yT_{n}_{half}"] = C["pe"]
                    LM[f"pe_yT_{n}"] = C["pe"]

                @block.vector
                def _(v, ngi=ngi, ncc=ncc, n=n, gslot=gslot):
                    v.wait_ge(s_x, LM[f"x_ffn_{ngi}"])
                    buf = xo_sb[gslot] if ncc == 0 else xps[gslot]
                    for half in range(2):
                        v.wait_ge(s_pe, LM[f"pe_yT_{n}_{half}"])
                        inc(v.tensor_tensor(buf[:, 512 * half:512 * (half + 1)],
                                            PSB[:, 1024 * half:1024 * half + 512],
                                            buf[:, 512 * half:512 * (half + 1)],
                                            ALU.add), "dv")
                        LM[f"dv_y_{n}_{half}"] = C["dv"]
                    LM[f"dv_y_n{n}"] = C["dv"]

                @block.sync
                def _(sy, n=n, ncc=ncc, gslot=gslot):
                    sy.wait_ge(s_dv, LM[f"dv_y_n{n}"])
                    src = xo_sb[gslot] if ncc == 0 else xps[gslot]
                    sy.dma_start(out=y_out[:, 128 * n:128 * (n + 1)]
                                 .rearrange("(tt p) o -> p tt o", p=128),
                                 in_=src.ap().rearrange("p (tt o) -> p tt o", o=128)).then_inc(s_sp, 16)
                    C["sp"] += 16
                LM[f"sp_y_{n}"] = C["sp"]
        LM["sp_y_done"] = C["sp"]

        if dbg:
            @block.sync
            def _(sy):
                sy.wait_ge(s_sp, LM["sp_y_done"])
                if "d_arena" in dbg:
                    sy.dma_start(out=dbg["d_arena"], in_=arena[:, :]).then_inc(s_sp, 16)
                    C["sp"] += 16
                if "d_xp" in dbg:
                    sy.dma_start(out=dbg["d_xp"], in_=xp_d.ap()).then_inc(s_sp, 16)
                    C["sp"] += 16
                if "d_qd" in dbg:
                    sy.dma_start(out=dbg["d_qd"], in_=qd.ap()).then_inc(s_sp, 16)
                    C["sp"] += 16
                if "d_gath" in dbg:
                    sy.dma_start(out=dbg["d_gath"], in_=gathered.ap()).then_inc(s_sp, 16)
                    C["sp"] += 16
                if "d_wdq" in dbg:
                    sy.dma_start(out=dbg["d_wdq"], in_=wdq.ap()).then_inc(s_sp, 16)
                    C["sp"] += 16

        @block.sync
        def _(sy):
            sy.wait_ge(s_sp, C["sp"])
    return nc


def make_in_maps():
    dstrip, ostrip, ident = host_tables()
    return dstrip, ostrip, ident


def kernel(L0_attn_x, L0_attn_w_qkv, L0_attn_b_qkv, L0_attn_w_out, L0_attn_b_out,
           L0_ffn_qweight, L0_ffn_weight_scale, L0_ffn_weight_zero_point, L0_ffn_bias):
    x = np.ascontiguousarray(np.asarray(L0_attn_x, dtype=np.float32).reshape(S, D))
    wqkvT_a = np.ascontiguousarray(np.asarray(L0_attn_w_qkv, dtype=np.float32).T)
    bqkv = np.ascontiguousarray(np.asarray(L0_attn_b_qkv, dtype=np.float32))
    woutT_a = np.ascontiguousarray(np.asarray(L0_attn_w_out, dtype=np.float32).T)
    bout = np.ascontiguousarray(np.asarray(L0_attn_b_out, dtype=np.float32))
    qw = np.asarray(L0_ffn_qweight, dtype=np.int32).reshape(D, 1024, 3)
    qw_r = np.ascontiguousarray(qw.transpose(0, 2, 1).reshape(D, 3072))
    wsc = np.ascontiguousarray(np.asarray(L0_ffn_weight_scale, dtype=np.float32).reshape(D))
    wzp = np.ascontiguousarray(np.asarray(L0_ffn_weight_zero_point, dtype=np.float32).reshape(D))
    fb = np.ascontiguousarray(np.asarray(L0_ffn_bias, dtype=np.float32))
    dstrip, ostrip, ident = host_tables()

    nc = build()
    in_maps = []
    for c in range(NCORE):
        xc = np.ascontiguousarray(
            np.concatenate([x[512 * c:512 * (c + 1)], x[512 * (c + 8):512 * (c + 9)]], axis=0))
        in_maps.append({
            "x_c": xc, "wqkvT": wqkvT_a, "bqkv": bqkv, "woutT": woutT_a, "bout": bout,
            "qw_r": qw_r, "wsc": wsc, "wzp": wzp, "fb": fb,
            "dstrip": dstrip, "ostrip": ostrip, "ident": ident,
            "ktab": kill_table(c),
        })
    res = run_bass_kernel_spmd(nc, in_maps, core_ids=list(range(NCORE)))
    out = np.empty((S, D), dtype=np.float32)
    for c in range(NCORE):
        yc = res.results[c]["y"]
        out[512 * c:512 * (c + 1)] = yc[0:512]
        out[512 * (c + 8):512 * (c + 9)] = yc[512:1024]
    return out.reshape(1, S, D)
